# version 03
# baseline (speedup 1.0000x reference)
import numpy as np
import ml_dtypes
import concourse.bass as bass
import concourse.mybir as mybir
from concourse.bass_utils import run_bass_kernel_spmd

F32 = mybir.dt.float32
BF16 = mybir.dt.bfloat16
U8 = mybir.dt.uint8
AX = mybir.AxisListType
ALU = mybir.AluOpType
ACTF = mybir.ActivationFunctionType

P = 128
NTT = 8
T = 1024
D = 2048
KC = D // P
MAGIC = 3 * 2.0**22
NH = 32
NCH = 32
DEBUG = None
STOP = None


class Sem:
    def __init__(self, nc, name):
        self.h = nc.alloc_semaphore(name)
        self.n = 0

    def reserve(self, k):
        b = self.n
        self.n += k
        return b


class Overlay:
    def __init__(self, nc, nbytes):
        self.nc, self.nbytes = nc, nbytes
        self.arena = nc.alloc_sbuf_tensor("ARENA", [P, nbytes], U8)
        self.base = nc.lookup_mloc(self.arena).addr
        self.count = 0

    def place(self, name, shape, dtype, off):
        per_part = int(np.prod(shape[1:])) * mybir.dt.size(dtype)
        assert off % 32 == 0 and off + per_part <= self.nbytes, (name, off, per_part)
        self.count += 1
        return self.nc.alloc_sbuf_tensor_at(f"{name}_{self.count}", list(shape), dtype,
                                            offset=self.base + off)


def build():
    nc = bass.Bass("TRN2", target_bir_lowering=False)

    def din(name, shape, dtype=F32):
        return nc.dram_tensor(name, shape, dtype, kind="ExternalInput").ap()

    X = din("X", [T, D]); XH = din("XH", [3, D])
    Wip = din("Wip", [D, 8448]); Wop = din("Wop", [D, D])
    CONVW = din("CONVW", [4096, 4]); CONVB = din("CONVB", [4096])
    ALOG = din("ALOG", [64]); DTB = din("DTB", [64]); DPAR = din("DPAR", [NH])
    NSC = din("NSC", [D]); WNORM = din("WNORM", [D])
    QGV = din("QGV", [2048, 2048]); QWO = din("QWO", [1024, 2048])
    QW1 = din("QW1", [2048, 768]); QW3 = din("QW3", [2048, 768]); QW2 = din("QW2", [768, 2048])
    WQ = din("WQ", [D, D]); WK = din("WK", [D, 512]); WV = din("WV", [D, 512])
    WG = din("WG", [D, D]); WMB = din("WMB", [D, D])
    BQ = din("BQ", [D]); BK = din("BK", [512]); BV = din("BV", [512])
    BG = din("BG", [D]); BMB = din("BMB", [D])
    MEM = din("MEM", [16, 768, 128]); MNORM = din("MNORM", [16, 768])
    IDT = din("IDT", [P, P], BF16); TRIL = din("TRIL", [64, 64])

    OUT = nc.dram_tensor("OUT", [T, D], F32, kind="ExternalOutput").ap()
    DBG = nc.dram_tensor("DBG", [4160, 4160], F32, kind="ExternalOutput").ap() if DEBUG else None

    pk_x = nc.dram_tensor("pk_x", [T, D], F32).ap()
    pk_z = nc.dram_tensor("pk_z", [T, 4096], BF16).ap()
    pk_xcv = nc.dram_tensor("pk_xcv", [4096, T], BF16).ap()
    pk_xc = nc.dram_tensor("pk_xc", [D, 2048], BF16).ap()
    pk_y = nc.dram_tensor("pk_y", [2048, D], BF16).ap()
    sh_z = nc.dram_tensor("sh_z", [2048, 4096], BF16, addr_space="Shared").ap()
    sh_xc = nc.dram_tensor("sh_xc", [4096, 2048], BF16, addr_space="Shared").ap()
    sh_qk = nc.dram_tensor("sh_qk", [P, 2048], F32, addr_space="Shared").ap()
    sh_la = nc.dram_tensor("sh_la", [64, 2048], F32, addr_space="Shared").ap()
    sh_rms = nc.dram_tensor("sh_rms", [2, 2048], F32, addr_space="Shared").ap()
    sh_op = nc.dram_tensor("sh_op", [2, T, D], F32, addr_space="Shared").ap()
    bar_io = nc.dram_tensor("bar_io", [2, 8], F32).ap()
    sc_in = nc.dram_tensor("sc_in", [1, 8], F32).ap()
    sc_out = nc.dram_tensor("sc_out", [1, 8], F32).ap()
    QSL = 2048 * 2048 + 1024 * 2048 + 2 * 2048 * 768 + 768 * 2048
    qag_in = nc.dram_tensor("qag_in", [1, QSL], BF16)
    qag_out = nc.dram_tensor("qag_out", [8, QSL], BF16)
    DM = 16 * 768 * 128 + 16 * 768
    dm_in = nc.dram_tensor("dm_in", [1, DM], F32)
    dm_out = nc.dram_tensor("dm_out", [1, DM], F32)

    q_off, o = {}, 0
    for nm, nel in [("gv", 2048 * 2048), ("wo", 1024 * 2048), ("w1", 2048 * 768),
                    ("w3", 2048 * 768), ("w2", 768 * 2048)]:
        q_off[nm] = o; o += nel

    def qap_gv(hc):
        rank, c0 = hc // 16, (hc % 16) * P
        return bass.AP(qag_out, rank * QSL + q_off["gv"] + c0,
                       [[2048, P], [P * 2048, KC], [1, P]])

    def qap_w13(nm, hc):
        rank, j = hc % 8, hc // 8
        return bass.AP(qag_out, rank * QSL + q_off[nm] + j * P,
                       [[768, P], [P * 768, KC], [1, P]])

    def qap_wo(dmc, r):
        return bass.AP(qag_out, r * QSL + q_off["wo"] + dmc * P,
                       [[2048, P], [P * 2048, 8], [1, P]])

    def qap_w2(kc, n0, ncols):
        rank, loc = kc // 6, kc % 6
        return bass.AP(qag_out, rank * QSL + q_off["w2"] + loc * P * 2048 + n0,
                       [[2048, P], [1, ncols]])

    OV = Overlay(nc, 212832)
    ones_row = OV.place("ones", [1, 512], F32, 0)
    idt = OV.place("idt", [P, P], BF16, 2048)
    trilm = OV.place("trilm", [64, 64], F32, 2304)
    convw = OV.place("convw", [P, 128], F32, 2560)
    convb = OV.place("convb", [P, 32], F32, 3072)
    cA = OV.place("cA", [64, 1], F32, 3200)
    cDTB = OV.place("cDTB", [64, 1], F32, 3232)
    cDP = OV.place("cDP", [P, NH], F32, 3264)
    nsc_sb = OV.place("nsc", [P, 16], F32, 3424)
    qsc = OV.place("qsc", [P, 8], F32, 3488)
    colA = OV.place("colA", [P, 16], F32, 3552)
    colB = OV.place("colB", [P, 16], F32, 3616)
    colC = OV.place("colC", [P, 16], F32, 3680)
    colD = OV.place("colD", [P, 16], F32, 3744)
    mcol = OV.place("mcol", [P, 12], F32, 3808)
    qcol = OV.place("qcol", [P, 12], F32, 3872)
    rcol = OV.place("rcol", [P, 12], F32, 3936)
    bcol = OV.place("bcol", [P, 12], F32, 4000)
    cst = OV.place("cst", [P, 8], F32, 4064)
    nc.const_aps.register(F32, 1e-5, cst[:, 0:1])
    nc.const_aps.register(F32, 1e-6, cst[:, 1:2])
    wnorm_bc = OV.place("wnorm_bc", [P, D], F32, 4096)
    scr1 = OV.place("scr1", [P, 2048], F32, 12288)
    scr2 = OV.place("scr2", [P, 2048], F32, 20480)
    junk = scr2
    WB0 = 28672
    wb = [OV.place(f"wb{i}", [P, 8192], BF16, WB0 + i * 16384) for i in range(2)]
    ws = OV.place("ws", [P, 8192], F32, WB0 + 32768)
    nT = OV.place("nT", [P, KC * T], BF16, WB0 + 65536)
    BIG = WB0 + 65536 + 32768
    BIGEND = 212832
    assert BIG == 135136 + 0 or True
    BIGSZ = BIGEND - BIG

    PS = [nc.alloc_psum_tensor(f"PS{i}", [P, 512], F32) for i in range(6)]
    PT = [nc.alloc_psum_tensor(f"PT{i}", [P, 1024], BF16) for i in range(2)]

    s_ld = Sem(nc, "s_ld"); s_w = Sem(nc, "s_w"); s_cast = Sem(nc, "s_cast")
    s_z0 = Sem(nc, "s_z0"); s_z1 = Sem(nc, "s_z1")
    s_x0 = Sem(nc, "s_x0"); s_x1 = Sem(nc, "s_x1")
    s_gd = Sem(nc, "s_gd")
    s_ops = [Sem(nc, f"s_op{i}") for i in range(4)]
    s_mm = Sem(nc, "s_mm"); s_ev = Sem(nc, "s_ev"); s_tp = Sem(nc, "s_tp")
    s_st = Sem(nc, "s_st"); s_cc = Sem(nc, "s_cc"); s_a = Sem(nc, "s_a")
    s_b = Sem(nc, "s_b"); s_g = Sem(nc, "s_g"); s_pe = Sem(nc, "s_pe")

    def barrier(blk):
        b = s_cc.reserve(1)

        def g(e):
            e.collective_compute("AllReduce", ALU.add, replica_groups=[list(range(8))],
                                 ins=[bar_io[0:1, :]], outs=[bar_io[1:2, :]]).then_inc(s_cc.h, 1)
            e.wait_ge(s_cc.h, b + 1)
        blk.gpsimd(g)

    def transpose_batch(blk, items):
        n = len(items)
        b_pe = s_pe.reserve(n)
        b_tp = s_tp.reserve(n)

        def peb(te):
            for i, (src, dst) in enumerate(items):
                if i >= 2:
                    te.wait_ge(s_tp.h, b_tp + i - 1)
                pp, m = src.shape
                te.transpose(PT[i % 2][0:m, 0:pp], src, idt[0:pp, 0:pp]).then_inc(s_pe.h, 1)

        def evb(v):
            for i, (src, dst) in enumerate(items):
                v.wait_ge(s_pe.h, b_pe + i + 1)
                pp, m = src.shape
                v.tensor_copy(dst, PT[i % 2][0:m, 0:pp]).then_inc(s_tp.h, 1)
        blk.tensor(peb)
        blk.vector(evb)

    def emit_ln(blk, tiles, width, eps, halo=None):
        nt = len(tiles) + (1 if halo is not None else 0)
        b_a = s_a.reserve(3)
        b_b = s_b.reserve(2)

        def v1(v):
            last = None
            for i, src in enumerate(tiles):
                last = v.tensor_reduce(out=mcol[:, i:i + 1], in_=src, axis=AX.X, op=ALU.add)
            if halo is not None:
                last = v.tensor_reduce(out=mcol[:3, nt - 1:nt], in_=halo, axis=AX.X, op=ALU.add)
            last.then_inc(s_a.h, 1)
        blk.vector(v1)

        def sc1(sc):
            last = None
            for i, src in enumerate(tiles):
                if i > 0:
                    sc.drain()
                last = sc.activation(out=junk[:, 0:width], in_=src, func=ACTF.Square, accum_out=qcol[:, i:i + 1])
            if halo is not None:
                sc.drain()
                last = sc.activation(out=junk[:3, 0:width], in_=halo, func=ACTF.Square, accum_out=qcol[:3, nt - 1:nt])
            last.then_inc(s_b.h, 1)
        blk.scalar(sc1)

        regions = [(slice(0, P), slice(0, len(tiles)))]
        if halo is not None:
            regions.append((slice(0, 3), slice(nt - 1, nt)))

        def v2(v):
            v.wait_ge(s_a.h, b_a + 1)
            v.wait_ge(s_b.h, b_b + 1)
            last = None
            for pr, cr in regions:
                v.tensor_scalar(out=mcol[pr, cr], in0=mcol[pr, cr], scalar1=1.0 / width, scalar2=None, op0=ALU.mult)
                v.tensor_scalar(out=qcol[pr, cr], in0=qcol[pr, cr], scalar1=1.0 / width, scalar2=None, op0=ALU.mult)
                v.drain()
                v.tensor_tensor(out=rcol[pr, cr], in0=mcol[pr, cr], in1=mcol[pr, cr], op=ALU.mult)
                v.drain()
                last = v.tensor_sub(qcol[pr, cr], qcol[pr, cr], rcol[pr, cr])
            last.then_inc(s_a.h, 1)
        blk.vector(v2)

        def sc2(sc):
            sc.wait_ge(s_a.h, b_a + 2)
            last = None
            for i, (pr, cr) in enumerate(regions):
                if i:
                    sc.drain()
                last = sc.activation(out=qcol[pr, cr], in_=qcol[pr, cr], func=ACTF.Sqrt, bias=eps)
            last.then_inc(s_b.h, 1)
        blk.scalar(sc2)

        def v3(v):
            v.wait_ge(s_b.h, b_b + 2)
            last = None
            for pr, cr in regions:
                v.reciprocal(rcol[pr, cr], qcol[pr, cr])
                v.drain()
                v.tensor_tensor(out=bcol[pr, cr], in0=mcol[pr, cr], in1=rcol[pr, cr], op=ALU.mult)
                v.drain()
                last = v.tensor_scalar(out=bcol[pr, cr], in0=bcol[pr, cr], scalar1=-1.0, scalar2=None, op0=ALU.mult)
            last.then_inc(s_a.h, 1)
        blk.vector(v3)
        return nt

    def dbg_dump(tag, items):
        if DEBUG != tag:
            return False
        with nc.Block() as blk:
            def _(e):
                b = s_gd.reserve(16 * len(items))
                for i, (ap, r0, ncols) in enumerate(items):
                    e.dma_start(out=bass.AP(DBG.tensor, r0 * 4160, [[4160, ap.shape[0]], [1, ncols]]),
                                in_=ap).then_inc(s_gd.h, 16)
                e.wait_ge(s_gd.h, b + 16 * len(items))
            blk.gpsimd(_)
        return True

    with nc.Block() as blk:
        def _(e):
            b = s_ld.reserve(9 * 16)
            e.dma_start(out=idt[:, :], in_=IDT[:, :]).then_inc(s_ld.h, 16)
            e.dma_start(out=trilm[:, :], in_=TRIL[:, :]).then_inc(s_ld.h, 16)
            e.dma_start(out=bass.AP(convw, 0, [[128, P], [4, 32], [1, 4]]), in_=bass.AP(CONVW.tensor, 0, [[4, P], [512, 32], [1, 4]])).then_inc(s_ld.h, 16)
            e.dma_start(out=convb[:, :], in_=bass.AP(CONVB.tensor, 0, [[1, P], [128, 32]]), allow_slow_non_contiguous=True).then_inc(s_ld.h, 16)
            e.dma_start(out=cA[:, 0:1], in_=bass.AP(ALOG.tensor, 0, [[1, 64], [1, 1]])).then_inc(s_ld.h, 16)
            e.dma_start(out=cDTB[:, 0:1], in_=bass.AP(DTB.tensor, 0, [[1, 64], [1, 1]])).then_inc(s_ld.h, 16)
            e.dma_start(out=cDP[0:1, :], in_=bass.AP(DPAR.tensor, 0, [[NH, 1], [1, NH]])).then_inc(s_ld.h, 16)
            e.dma_start(out=nsc_sb[:, :], in_=bass.AP(NSC.tensor, 0, [[1, P], [128, 16]]), allow_slow_non_contiguous=True).then_inc(s_ld.h, 16)
            e.dma_start(out=scr1[0:1, 0:D], in_=bass.AP(WNORM.tensor, 0, [[D, 1], [1, D]])).then_inc(s_ld.h, 16)
            e.wait_ge(s_ld.h, b + 144)
        blk.sync(_)

        def _(v):
            v.memset(ones_row[:, :], 1.0)
            v.memset(cst[:, 0:1], 1e-5)
            v.memset(cst[:, 1:2], 1e-6)
        blk.vector(_)



        def _(sc):
            sc.wait_ge(s_ld.h, s_ld.n)
            sc.activation(out=cA[:, 0:1], in_=cA[:, 0:1], func=ACTF.Exp)
        blk.scalar(_)

    with nc.Block() as blk:
        def _init(e):
            b2 = s_gd.reserve(16)
            e.dma_start(out=bar_io[0:1, :], in_=ones_row[0:1, 0:8]).then_inc(s_gd.h, 16)
            e.wait_ge(s_gd.h, b2 + 16)
        blk.gpsimd(_init)
        b_pe = s_pe.reserve(5)

        def _(te):
            te.matmul(PS[0][:, 0:NH], ones_row[0:1, 0:P], cDP[0:1, :], start=True, stop=True).then_inc(s_pe.h, 1)
            for j in range(4):
                te.matmul(PS[1 + j][:, 0:512], ones_row[0:1, 0:P], scr1[0:1, j * 512:(j + 1) * 512],
                          start=True, stop=True).then_inc(s_pe.h, 1)
        blk.tensor(_)

        def _(v):
            v.wait_ge(s_pe.h, b_pe + 1)
            v.tensor_copy(cDP[:, :], PS[0][:, 0:NH])
            for j in range(4):
                v.wait_ge(s_pe.h, b_pe + 2 + j)
                v.tensor_copy(wnorm_bc[:, j * 512:(j + 1) * 512], PS[1 + j][:, 0:512])
        blk.vector(_)

    qmask = OV.place("qmask", [P, 4096], F32, BIG)
    qbf = [OV.place(f"qbf{i}", [P, 4096], BF16, BIG + 16384 + i * 8192) for i in range(2)]
    wsh = [ws[:, 0:4096], ws[:, 4096:8192]]
    qsl = {"gv": (QGV, 2048, 2048), "wo": (QWO, 1024, 2048), "w1": (QW1, 2048, 768),
           "w3": (QW3, 2048, 768), "w2": (QW2, 768, 2048)}
    qcnt = {"gv": 16384 * 2048, "wo": 8192 * 2048, "w1": 5461 * 2048,
            "w3": 5461 * 2048, "w2": 2048 * 5461}
    qeps = {"gv": 1e-5, "wo": 1e-5, "w1": 0.0, "w3": 0.0, "w2": 0.0}
    qorder = ["gv", "wo", "w1", "w3", "w2"]
    qtl = []
    for nm in qorder:
        _, rows, cols = qsl[nm]
        for r in range(0, rows, P):
            for c in range(0, cols, 4096):
                qtl.append((qorder.index(nm), r, c, min(4096, cols - c)))
    NQT = len(qtl)

    with nc.Block() as blk:
        s_zp = [s_z0, s_z1]
        b_zp = [s_z0.reserve(((NQT + 1) // 2) * 16), s_z1.reserve((NQT // 2) * 16)]
        b_a = s_a.reserve(NQT)

        def dmab(e):
            for i, (wi, r, c, cw) in enumerate(qtl):
                if i >= 2:
                    e.wait_ge(s_a.h, b_a + i - 1)
                W = qsl[qorder[wi]][0]
                e.dma_start(out=wsh[i % 2][:, 0:cw], in_=W[r:r + P, c:c + cw]).then_inc(s_zp[i % 2].h, 16)
        blk.sync(dmab)

        def vb(v):
            for i, (wi, r, c, cw) in enumerate(qtl):
                v.wait_ge(s_zp[i % 2].h, b_zp[i % 2] + (i // 2 + 1) * 16)
                v.drain()
                v.tensor_reduce(out=qcol[:, 0:1], in_=wsh[i % 2][:, 0:cw], axis=AX.X,
                                op=ALU.add, apply_absolute_value=True)
                v.drain()
                if (r, c) == (0, 0):
                    v.tensor_copy(colA[:, wi:wi + 1], qcol[:, 0:1]).then_inc(s_a.h, 1)
                else:
                    v.tensor_add(colA[:, wi:wi + 1], colA[:, wi:wi + 1], qcol[:, 0:1]).then_inc(s_a.h, 1)
        blk.vector(vb)

    with nc.Block() as blk:
        b_pe = s_pe.reserve(2)
        b_a = s_a.reserve(3)
        b_gd = s_gd.reserve(32)
        b_cc = s_cc.reserve(1)

        def v0(v):
            v.memset(cst[:, 2:3], 1.0).then_inc(s_a.h, 1)
        blk.vector(v0)

        def pe1(te):
            te.wait_ge(s_a.h, b_a + 1)
            te.matmul(PS[0][0:1, 0:5], cst[:, 2:3], colA[:, 0:5], start=True, stop=True).then_inc(s_pe.h, 1)
        blk.tensor(pe1)

        def v1(v):
            v.wait_ge(s_pe.h, b_pe + 1)
            v.tensor_copy(colB[0:1, 0:5], PS[0][0:1, 0:5]).then_inc(s_a.h, 1)
        blk.vector(v1)

        def g1(e):
            e.wait_ge(s_a.h, b_a + 2)
            e.dma_start(out=sc_in[0:1, 0:5], in_=colB[0:1, 0:5]).then_inc(s_gd.h, 16)
            e.wait_ge(s_gd.h, b_gd + 16)
            e.collective_compute("AllReduce", ALU.add, replica_groups=[list(range(8))],
                                 ins=[sc_in[0:1, 0:5]], outs=[sc_out[0:1, 0:5]]).then_inc(s_cc.h, 1)
            e.wait_ge(s_cc.h, b_cc + 1)
            e.dma_start(out=colB[0:1, 0:5], in_=sc_out[0:1, 0:5]).then_inc(s_gd.h, 32 - 16)
            e.wait_ge(s_gd.h, b_gd + 32)
        blk.gpsimd(g1)

        def v2(v):
            v.wait_ge(s_gd.h, b_gd + 32)
            for wi, nm in enumerate(qorder):
                v.drain()
                v.tensor_scalar(out=colB[0:1, wi:wi + 1], in0=colB[0:1, wi:wi + 1],
                                scalar1=1.0 / qcnt[nm], scalar2=qeps[nm], op0=ALU.mult, op1=ALU.max)
            v.drain()
            v.reciprocal(colB[0:1, 5:10], colB[0:1, 0:5]).then_inc(s_a.h, 1)
        blk.vector(v2)

        def pe2(te):
            te.wait_ge(s_a.h, b_a + 3)
            te.matmul(PS[1][:, 0:10], ones_row[0:1, 0:P], colB[0:1, 0:10],
                      start=True, stop=True).then_inc(s_pe.h, 1)
        blk.tensor(pe2)

        def v3(v):
            v.wait_ge(s_pe.h, b_pe + 2)
            v.tensor_copy(qsc[:, 0:5], PS[1][:, 0:5])
            v.drain()
            v.tensor_copy(colC[:, 0:5], PS[1][:, 5:10])
        blk.vector(v3)

    with nc.Block() as blk:
        s_zp = [s_z0, s_z1]
        b_zp = [s_z0.reserve(((NQT + 1) // 2) * 16), s_z1.reserve((NQT // 2) * 16)]
        b_a = s_a.reserve(NQT)
        b_tp = s_tp.reserve(NQT)
        b_sp = [s_ops[0].reserve(((NQT + 1) // 2) * 16), s_ops[1].reserve((NQT // 2) * 16)]

        def dmab(e):
            for i, (wi, r, c, cw) in enumerate(qtl):
                if i >= 2:
                    e.wait_ge(s_a.h, b_a + i - 1)
                W = qsl[qorder[wi]][0]
                e.dma_start(out=wsh[i % 2][:, 0:cw], in_=W[r:r + P, c:c + cw]).then_inc(s_zp[i % 2].h, 16)
        blk.sync(dmab)

        def vb(v):
            for i, (wi, r, c, cw) in enumerate(qtl):
                v.wait_ge(s_zp[i % 2].h, b_zp[i % 2] + (i // 2 + 1) * 16)
                if i >= 1:
                    v.drain()
                v.tensor_scalar(out=qmask[:, 0:cw], in0=wsh[i % 2][:, 0:cw],
                                scalar1=colC[:, wi:wi + 1], scalar2=MAGIC,
                                op0=ALU.mult, op1=ALU.add).then_inc(s_a.h, 1)
                v.drain()
                v.tensor_scalar(out=qmask[:, 0:cw], in0=qmask[:, 0:cw], scalar1=MAGIC,
                                scalar2=1.0, op0=ALU.subtract, op1=ALU.min)
                v.drain()
                if i >= 2:
                    v.wait_ge(s_ops[i % 2].h, b_sp[i % 2] + (i // 2) * 16)
                v.tensor_scalar(out=qbf[i % 2][:, 0:cw], in0=qmask[:, 0:cw], scalar1=-1.0,
                                scalar2=None, op0=ALU.max).then_inc(s_tp.h, 1)
        blk.vector(vb)

        def stb(e):
            for i, (wi, r, c, cw) in enumerate(qtl):
                e.wait_ge(s_tp.h, b_tp + i + 1)
                _, rows, cols = qsl[qorder[wi]]
                e.dma_start(out=bass.AP(qag_in, q_off[qorder[wi]] + r * cols + c, [[cols, P], [1, cw]]),
                            in_=qbf[i % 2][:, 0:cw]).then_inc(s_ops[i % 2].h, 16)
            e.wait_ge(s_ops[0].h, b_sp[0] + ((NQT + 1) // 2) * 16)
            e.wait_ge(s_ops[1].h, b_sp[1] + (NQT // 2) * 16)
        blk.gpsimd(stb)

    with nc.Block() as blk:
        b_cc = s_cc.reserve(1)

        def g1(e):
            e.collective_compute("AllGather", ALU.bypass, replica_groups=[list(range(8))],
                                 ins=[qag_in.ap()], outs=[qag_out.ap()]).then_inc(s_cc.h, 1)
            e.wait_ge(s_cc.h, b_cc + 1)
        blk.gpsimd(g1)

    if STOP == "QP":
        return nc

    x_sb = OV.place("x_sb", [P, NTT * D], F32, BIG)
    xh_sb = OV.place("xh_sb", [3, D], F32, BIG + 65536)
    nh_sb = OV.place("nh_sb", [3, D], BF16, BIG + 73728)
    nhT = OV.place("nhT", [P, KC * 3], BF16, BIG + 77824)

    with nc.Block() as blk:
        def _(e):
            b = s_ld.reserve(9 * 16)
            for t in range(NTT):
                e.dma_start(out=x_sb[:, t * D:(t + 1) * D], in_=X[t * P:(t + 1) * P, :]).then_inc(s_ld.h, 16)
            e.dma_start(out=xh_sb[:, :], in_=XH[:, :]).then_inc(s_ld.h, 16)
            e.wait_ge(s_ld.h, b + 144)
        blk.sync(_)

    with nc.Block() as blk:
        emit_ln(blk, [x_sb[:, t * D:(t + 1) * D] for t in range(NTT)], D, 1e-5, halo=xh_sb[:, :])

    with nc.Block() as blk:
        n0_st = OV.place("n0_st", [P, D], BF16, 12288)
        b_b = s_b.reserve(NTT + 1)
        b_pe = s_pe.reserve(NTT * KC + KC)
        b_tp = s_tp.reserve(NTT * KC + KC)

        def scb(sc):
            for t in range(NTT):
                if t > 0:
                    sc.wait_ge(s_tp.h, b_tp + t * KC)
                sc.activation(out=n0_st[:, :], in_=x_sb[:, t * D:(t + 1) * D], func=ACTF.Identity,
                              scale=rcol[:, t:t + 1], bias=bcol[:, t:t + 1]).then_inc(s_b.h, 1)
            sc.wait_ge(s_tp.h, b_tp + NTT * KC)
            sc.activation(out=nh_sb[:, :], in_=xh_sb[:, :], func=ACTF.Identity,
                          scale=rcol[:3, 8:9], bias=bcol[:3, 8:9]).then_inc(s_b.h, 1)
        blk.scalar(scb)

        def peb(te):
            i = 0
            for t in range(NTT):
                te.wait_ge(s_b.h, b_b + t + 1)
                for kc in range(KC):
                    if i >= 2:
                        te.wait_ge(s_tp.h, b_tp + i - 1)
                    te.transpose(PT[i % 2][:, 0:P], n0_st[:, kc * P:(kc + 1) * P], idt[:, :]).then_inc(s_pe.h, 1)
                    i += 1
            te.wait_ge(s_b.h, b_b + NTT + 1)
            for kc in range(KC):
                te.wait_ge(s_tp.h, b_tp + i - 1)
                te.transpose(PT[i % 2][0:P, 0:3], nh_sb[0:3, kc * P:(kc + 1) * P], idt[0:3, 0:3]).then_inc(s_pe.h, 1)
                i += 1
        blk.tensor(peb)

        def evb(v):
            i = 0
            for t in range(NTT):
                for kc in range(KC):
                    v.wait_ge(s_pe.h, b_pe + i + 1)
                    v.tensor_copy(nT[:, kc * T + t * P:kc * T + (t + 1) * P], PT[i % 2][:, 0:P]).then_inc(s_tp.h, 1)
                    i += 1
            for kc in range(KC):
                v.wait_ge(s_pe.h, b_pe + i + 1)
                v.tensor_copy(nhT[:, kc * 3:(kc + 1) * 3], PT[i % 2][0:P, 0:3]).then_inc(s_tp.h, 1)
                i += 1
        blk.vector(evb)

    with nc.Block() as blk:
        def _(e):
            b = s_st.reserve(NTT * 16)
            for t in range(NTT):
                e.dma_start(out=pk_x[t * P:(t + 1) * P, :], in_=x_sb[:, t * D:(t + 1) * D]).then_inc(s_st.h, 16)
            e.wait_ge(s_st.h, b + NTT * 16)
        blk.sync(_)

    if dbg_dump("ln0", [(nT[:, 0:4096], 0, 4096), (rcol[:, 0:8], 300, 8)]):
        return nc

    XW = 1027
    x_fm = OV.place("x_fm", [P, 32 * XW], BF16, BIG)
    z_st = [OV.place(f"z_st{i}", [P, 512], BF16, BIG + 65728 + i * 1024) for i in range(2)]
    bcraw = OV.place("bcraw", [P, T], F32, BIG + 67776)
    dtraw = OV.place("dtraw", [64, T], F32, BIG + 71872)
    qk32 = OV.place("qk32", [P, T], F32, BIG + 67776)
    la_my = OV.place("la_my", [64, T], F32, BIG + 71872)

    groups = [("z", g * 512, 512) for g in range(8)] + \
             [("x", 4096 + g * 512, 512) for g in range(8)] + [("bc", 8192, 256)]
    tile_seq = []
    for gi, (kind, c0, cw) in enumerate(groups):
        if kind == "z":
            tile_seq += [(gi, "z", (c0, t)) for t in range(NTT)]
        elif kind == "x":
            tile_seq += [(gi, "x", (c0, oc, tg)) for oc in range(4) for tg in range(2)]
            tile_seq += [(gi, "xh", (c0, oc)) for oc in range(4)]
        else:
            tile_seq += [(gi, "bc", (tg,)) for tg in range(2)]
            tile_seq += [(gi, "dt", (tg,)) for tg in range(2)]
    NTILE = len(tile_seq)
    first_tile = {}
    for ti, (gi, k, m) in enumerate(tile_seq):
        first_tile.setdefault(gi, ti)

    with nc.Block() as blk:
        NGR = len(groups)
        b_w = s_w.reserve(NGR * 16)
        b_cast = s_cast.reserve(NGR)
        b_mm = s_mm.reserve(NTILE)
        b_ev = s_ev.reserve(NTILE)

        def dmab(e):
            zi = 0
            for gi, (kind, c0, cw) in enumerate(groups):
                if gi >= 1:
                    e.wait_ge(s_cast.h, b_cast + gi)
                e.dma_start(out=ws[:, 0:KC * cw],
                            in_=bass.AP(Wip.tensor, c0, [[8448, P], [P * 8448, KC], [1, cw]])).then_inc(s_w.h, 16)
                if gi >= 1 and groups[gi - 1][0] == "z":
                    zc0 = groups[gi - 1][1]
                    for t in range(NTT):
                        ti = (gi - 1) * NTT + t
                        e.wait_ge(s_ev.h, b_ev + ti + 1)
                        e.dma_start(out=bass.AP(pk_z.tensor, t * P * 4096 + zc0, [[4096, P], [1, 512]]),
                                    in_=z_st[zi % 2][:, 0:512]).then_inc((s_z0 if zi % 2 == 0 else s_z1).h, 16)
                        zi += 1
            e.wait_ge(s_z0.h, b_pz0 + 16 * 32)
            e.wait_ge(s_z1.h, b_pz1 + 16 * 32)
        b_pz0 = s_z0.reserve(16 * 32)
        b_pz1 = s_z1.reserve(16 * 32)
        blk.sync(dmab)

        def castb(sc):
            for gi, (kind, c0, cw) in enumerate(groups):
                sc.wait_ge(s_w.h, b_w + (gi + 1) * 16)
                if gi >= 2:
                    lim = first_tile.get(gi - 1, NTILE)
                    sc.wait_ge(s_mm.h, b_mm + lim)
                sc.copy(wb[gi % 2][:, 0:KC * cw], ws[:, 0:KC * cw]).then_inc(s_cast.h, 1)
        blk.scalar(castb)

        def peb(te):
            for ti, (gi, knd, meta) in enumerate(tile_seq):
                kind, c0, cw = groups[gi]
                if ti == first_tile[gi]:
                    te.wait_ge(s_cast.h, b_cast + gi + 1)
                bank = ti % 2
                if ti >= 2:
                    te.wait_ge(s_ev.h, b_ev + ti - 1)
                wbg = wb[gi % 2]
                last = None
                if knd == "z":
                    t = meta[1]
                    for kc in range(KC):
                        last = te.matmul(PS[bank][:, 0:512], nT[:, kc * T + t * P:kc * T + (t + 1) * P],
                                         wbg[:, kc * 512:(kc + 1) * 512], start=(kc == 0), stop=(kc == KC - 1))
                elif knd == "x":
                    _, oc, tg = meta
                    for kc in range(KC):
                        last = te.matmul(PS[bank][:, 0:512], wbg[:, kc * 512 + oc * P:kc * 512 + (oc + 1) * P],
                                         nT[:, kc * T + tg * 512:kc * T + (tg + 1) * 512],
                                         start=(kc == 0), stop=(kc == KC - 1))
                elif knd == "xh":
                    _, oc = meta
                    for kc in range(KC):
                        last = te.matmul(PS[bank][:, 0:3], wbg[:, kc * 512 + oc * P:kc * 512 + (oc + 1) * P],
                                         nhT[:, kc * 3:(kc + 1) * 3], start=(kc == 0), stop=(kc == KC - 1))
                elif knd == "bc":
                    tg = meta[0]
                    for kc in range(KC):
                        last = te.matmul(PS[bank][:, 0:512], wbg[:, kc * 256:kc * 256 + P],
                                         nT[:, kc * T + tg * 512:kc * T + (tg + 1) * 512],
                                         start=(kc == 0), stop=(kc == KC - 1))
                else:
                    tg = meta[0]
                    for kc in range(KC):
                        last = te.matmul(PS[bank][:, 0:512], wbg[:, kc * 256 + P:kc * 256 + 2 * P],
                                         nT[:, kc * T + tg * 512:kc * T + (tg + 1) * 512],
                                         start=(kc == 0), stop=(kc == KC - 1))
                last.then_inc(s_mm.h, 1)
        blk.tensor(peb)

        def evb(v):
            for ti, (gi, knd, meta) in enumerate(tile_seq):
                bank = ti % 2
                v.wait_ge(s_mm.h, b_mm + ti + 1)
                if knd == "z":
                    t = meta[1]
                    zi2 = z_idx[ti]
                    if zi2 >= 2:
                        v.wait_ge((s_z0 if zi2 % 2 == 0 else s_z1).h,
                                  (b_pz0 if zi2 % 2 == 0 else b_pz1) + 16 * (zi2 // 2))
                    v.tensor_copy(z_st[zi2 % 2][:, 0:512], PS[bank][:, 0:512]).then_inc(s_ev.h, 1)
                elif knd == "x":
                    _, oc, tg = meta
                    ci = (meta[0] - 4096) // P + oc
                    v.tensor_copy(x_fm[:, ci * XW + 3 + tg * 512:ci * XW + 3 + (tg + 1) * 512],
                                  PS[bank][:, 0:512]).then_inc(s_ev.h, 1)
                elif knd == "xh":
                    ci = (meta[0] - 4096) // P + meta[1]
                    v.tensor_copy(x_fm[:, ci * XW:ci * XW + 3], PS[bank][:, 0:3]).then_inc(s_ev.h, 1)
                elif knd == "bc":
                    tg = meta[0]
                    v.tensor_copy(bcraw[:, tg * 512:(tg + 1) * 512], PS[bank][:, 0:512]).then_inc(s_ev.h, 1)
                else:
                    tg = meta[0]
                    v.tensor_copy(dtraw[:, tg * 512:(tg + 1) * 512], PS[bank][0:64, 0:512]).then_inc(s_ev.h, 1)
        z_idx, z_count = {}, {}
        zc = 0
        for ti, (gi, knd, meta) in enumerate(tile_seq):
            if knd == "z":
                z_idx[ti] = zc
                z_count[ti] = zc
                zc += 1
        blk.vector(evb)


    with nc.Block() as blk:
        b_a = s_a.reserve(2)
        b_b = s_b.reserve(1)

        def v1(v):
            v.tensor_single_scalar(out=scr2[:, 0:T], in_=bcraw[:, 0:T], scalar=0.0, op=ALU.min)
            v.tensor_relu(scr1[:, 0:T], bcraw[:, 0:T]).then_inc(s_a.h, 1)
        blk.vector(v1)

        def sc1(sc):
            sc.wait_ge(s_a.h, b_a + 1)
            sc.activation(out=scr2[:, 0:T], in_=scr2[:, 0:T], func=ACTF.Exp)
            sc.drain()
            sc.activation(out=scr2[0:64, T:2 * T], in_=dtraw[:, 0:T], func=ACTF.Abs, bias=cDTB[:, 0:1])
            sc.drain()
            sc.activation(out=scr2[0:64, T:2 * T], in_=scr2[0:64, T:2 * T], func=ACTF.Exp, scale=-1.0)
            sc.drain()
            sc.activation(out=scr2[0:64, T:2 * T], in_=scr2[0:64, T:2 * T], func=ACTF.Ln, bias=1.0)
            sc.activation(out=scr1[0:64, T:2 * T], in_=dtraw[:, 0:T], func=ACTF.Relu,
                          bias=cDTB[:, 0:1]).then_inc(s_b.h, 1)
        blk.scalar(sc1)

        def v2(v):
            v.wait_ge(s_b.h, b_b + 1)
            v.tensor_add(qk32[:, 0:T], scr1[:, 0:T], scr2[:, 0:T])
            v.drain()
            v.tensor_add(la_my[:, 0:T], scr1[0:64, T:2 * T], scr2[0:64, T:2 * T])
            v.drain()
            v.tensor_scalar(out=la_my[:, 0:T], in0=la_my[:, 0:T], scalar1=cA[:, 0:1], scalar2=-1.0, op0=ALU.mult, op1=ALU.mult).then_inc(s_a.h, 1)
        blk.vector(v2)

    xcv = [OV.place(f"xcv{i}", [P, T], BF16, BIG + 75968 + i * 2048) for i in range(2)]
    cacc2 = [scr1, scr2]
    with nc.Block() as blk:
        b_a = s_a.reserve(32)
        b_b = s_b.reserve(32)

        def v1(v):
            for ci in range(32):
                xin = x_fm[:, ci * XW:(ci + 1) * XW]
                acc = cacc2[ci % 2]
                if ci >= 2:
                    v.wait_ge(s_b.h, b_b + ci - 1)
                v.tensor_scalar(out=acc[:, 0:T], in0=xin[:, 0:T], scalar1=convw[:, ci * 4:ci * 4 + 1],
                                scalar2=None, op0=ALU.mult)
                last = None
                for k in range(1, 4):
                    v.drain()
                    last = v.scalar_tensor_tensor(out=acc[:, 0:T], in0=xin[:, k:k + T],
                                                  scalar=convw[:, ci * 4 + k:ci * 4 + k + 1],
                                                  in1=acc[:, 0:T], op0=ALU.mult, op1=ALU.add)
                last.then_inc(s_a.h, 1)
        blk.vector(v1)

        b_px0 = s_x0.reserve(16 * 16)
        b_px1 = s_x1.reserve(16 * 16)

        def sc1(sc):
            for ci in range(32):
                sc.wait_ge(s_a.h, b_a + ci + 1)
                if ci >= 2:
                    sc.wait_ge((s_x0 if ci % 2 == 0 else s_x1).h,
                               (b_px0 if ci % 2 == 0 else b_px1) + 16 * (ci // 2))
                sc.activation(out=xcv[ci % 2][:, :], in_=cacc2[ci % 2][:, 0:T], func=ACTF.Silu,
                              bias=convb[:, ci:ci + 1]).then_inc(s_b.h, 1)
        blk.scalar(sc1)

        def st1(e):
            for ci in range(32):
                e.wait_ge(s_b.h, b_b + ci + 1)
                e.dma_start(out=bass.AP(pk_xcv.tensor, ci * P * T, [[T, P], [1, T]]),
                            in_=xcv[ci % 2][:, :]).then_inc((s_x0 if ci % 2 == 0 else s_x1).h, 16)
            e.wait_ge(s_x0.h, b_px0 + 16 * 16)
            e.wait_ge(s_x1.h, b_px1 + 16 * 16)
        blk.sync(st1)

    if dbg_dump("inproj", [(qk32[:, 0:T], 0, T), (la_my[:, 0:T], 128, T),
                           (x_fm[:, 0:2 * XW], 192, 2 * XW),
                           (xcv[1][:, :], 320, T)]):
        return nc

    with nc.Block() as blk:
        def pub(e):
            b = s_st.reserve(4 * 16)
            for arm in e.switch_core_id(n=128):
                h2 = arm.logical % 2
                e.dma_start(out=sh_z[h2 * T:(h2 + 1) * T, :], in_=pk_z[:, :]).then_inc(s_st.h, 16)
                e.dma_start(out=bass.AP(sh_xc.tensor, h2 * T, [[2048, 4096], [1, T]]),
                            in_=pk_xcv[:, :]).then_inc(s_st.h, 16)
                e.dma_start(out=bass.AP(sh_qk.tensor, h2 * T, [[2048, P], [1, T]]),
                            in_=qk32[:, 0:T]).then_inc(s_st.h, 16)
                e.dma_start(out=bass.AP(sh_la.tensor, h2 * T, [[2048, 64], [1, T]]),
                            in_=la_my[:, 0:T]).then_inc(s_st.h, 16)
            e.wait_ge(s_st.h, b + 64)
        blk.sync(pub)
    with nc.Block() as blk:
        barrier(blk)

    def mk(name, shape, dtype, off):
        return OV.place(name, shape, dtype, off)

    cur = [BIG]

    def nxt(name, shape, dtype):
        per = int(np.prod(shape[1:])) * mybir.dt.size(dtype)
        per = (per + 31) // 32 * 32
        t_ = OV.place(name, shape, dtype, cur[0])
        cur[0] += per
        return t_

    k_f = nxt("k_f", [64, 2048], F32)
    q_f = nxt("q_f", [64, 2048], F32)
    k_b = nxt("k_b", [64, 2048], BF16)
    q_b = nxt("q_b", [64, 2048], BF16)
    la32 = nxt("la32", [32, 2048], F32)
    cs = nxt("cs", [32, 2048], F32)
    rst = nxt("rst", [32, 2048], F32)
    QKm = None
    csT2 = nxt("csT2", [64, 32 * 32], F32)
    wkT2 = nxt("wkT2", [64, 32 * 32], BF16)
    k_tm = nxt("k_tm", [64, 32 * 64], BF16)
    d_bc = nxt("d_bc", [64, 32 * 32], F32)
    dex = nxt("dex", [32, 32], F32)
    idt32 = nxt("idt32", [P, P], F32)
    csb = [nxt(f"csb{i}", [64, 512], F32) for i in range(2)]
    Dsc = [nxt(f"Dsc{i}", [64, 512], F32) for i in range(2)]
    Esc = [nxt(f"Esc{i}", [64, 512], F32) for i in range(2)]
    xpcs = [nxt(f"xpcs{i}", [64, 512], F32) for i in range(2)]
    assert cur[0] <= BIGEND, cur[0]
    QKm = OV.place("QKm", [64, 2048], F32, nc.lookup_mloc(la32).addr - OV.base)
    y_stg = [OV.place(f"y_stg{i}", [64, 2048], BF16, nc.lookup_mloc(rst).addr - OV.base + i * 4096) for i in range(2)]
    attT = OV.place("attT", [64, 16 * 512], BF16, WB0 + 32768)
    qsc8 = OV.place("qsc8", [64, 16 * 512], BF16, WB0 + 32768 + 16384)
    state = OV.place("state", [64, 16 * 64], F32, WB0)
    state_b = OV.place("state_b", [64, 16 * 64], BF16, WB0 + 4096 + 8192)
    states_st = OV.place("states_st", [64, 2 * 16 * 64], F32, WB0 + 4096)
    vslab = [OV.place(f"vslab{i}", [P, 16 * 128], BF16, WB0 + 14336 + i * 4096) for i in range(2)]
    NT0 = WB0 + 65536
    v_tm = [OV.place(f"v_tm{i}", [64, 2 * 2048], BF16, NT0 + i * 8192) for i in range(2)]
    ks = [OV.place(f"ks{i}", [64, 2 * 1024], BF16, NT0 + 16384 + i * 4096) for i in range(2)]

    s_v0 = Sem(nc, "s_v0"); s_v1 = Sem(nc, "s_v1")
    s_y0 = Sem(nc, "s_y0"); s_y1 = Sem(nc, "s_y1")

    IDT32 = din("IDT32", [P, P])

    with nc.Block() as blk:
        def rep(e):
            b = s_ld.reserve(5 * 16)
            for arm in e.switch_core_id(n=128):
                h2 = arm.logical % 2
                e.dma_start(out=pk_xc[:, :], in_=sh_xc[h2 * D:(h2 + 1) * D, 0:2048]).then_inc(s_ld.h, 16)
                e.dma_start(out=la32[:, :], in_=sh_la[h2 * 32:(h2 + 1) * 32, :]).then_inc(s_ld.h, 16)
            e.dma_start(out=k_f[:, :], in_=sh_qk[0:64, :]).then_inc(s_ld.h, 16)
            e.dma_start(out=q_f[:, :], in_=sh_qk[64:128, :]).then_inc(s_ld.h, 16)
            e.dma_start(out=idt32[:, :], in_=IDT32[:, :]).then_inc(s_ld.h, 16)
            e.wait_ge(s_ld.h, b + 80)
        blk.sync(rep)

        def cst_b(v):
            b_a2 = s_a.reserve(2)
            v.wait_ge(s_ld.h, s_ld.n)
            v.tensor_copy(k_b[:, :], k_f[:, :])
            v.tensor_copy(q_b[:, :], q_f[:, :]).then_inc(s_a.h, 1)
            v.memset(rst[:, :], 1.0)
            v.drain()
            v.memset(bass.AP(rst, 0, [[2048, 32], [64, 32]]), 0.0)
            v.drain()
            v.tensor_tensor_scan(out=cs[:, :], data0=rst[:, :], data1=la32[:, :],
                                 initial=0.0, op0=ALU.mult, op1=ALU.add).then_inc(s_a.h, 1)
        blk.vector(cst_b)

    with nc.Block() as blk:
        b_a = s_a.reserve(1)
        b_b = s_b.reserve(2)

        def v1(v):
            v.tensor_tensor(out=rst[:, :], in0=bass.AP(cs, 63, [[2048, 32], [64, 32], [0, 64]]),
                            in1=cs[:, :], op=ALU.subtract).then_inc(s_a.h, 1)
        blk.vector(v1)

        def sc1(sc):
            sc.wait_ge(s_a.h, b_a + 1)
            sc.activation(out=rst[:, :], in_=rst[:, :], func=ACTF.Exp).then_inc(s_b.h, 1)
            sc.activation(out=dex[:, :], in_=bass.AP(cs, 63, [[2048, 32], [64, 32]]),
                          func=ACTF.Exp).then_inc(s_b.h, 1)
        blk.scalar(sc1)

        ntp = 32 + 32 + 32
        b_pe = s_pe.reserve(ntp)
        b_tp = s_tp.reserve(ntp)

        def peb(te):
            i = 0
            te.wait_ge(s_b.h, b_b + 1)
            for c in range(32):
                if i >= 2:
                    te.wait_ge(s_tp.h, b_tp + i - 1)
                te.transpose(PS[4 + i % 2][0:64, 0:32], cs[:, c * 64:(c + 1) * 64], idt32[0:32, 0:32])
                te.nop().then_inc(s_pe.h, 1) if False else None
                i += 1
                te.then_inc if False else None
        def peb2(te):
            i = 0
            te.wait_ge(s_b.h, b_b + 2)
            for c in range(32):
                if i >= 2:
                    te.wait_ge(s_tp.h, b_tp + i - 1)
                te.transpose(PS[4 + i % 2][0:64, 0:32], cs[:, c * 64:(c + 1) * 64],
                             idt32[0:32, 0:32]).then_inc(s_pe.h, 1)
                i += 1
            for c in range(32):
                if i >= 2:
                    te.wait_ge(s_tp.h, b_tp + i - 1)
                te.transpose(PS[4 + i % 2][0:64, 0:32], rst[:, c * 64:(c + 1) * 64],
                             idt32[0:32, 0:32]).then_inc(s_pe.h, 1)
                i += 1
            for c in range(32):
                if i >= 2:
                    te.wait_ge(s_tp.h, b_tp + i - 1)
                te.transpose(PS[4 + i % 2][0:64, 0:64], k_f[0:64, c * 64:(c + 1) * 64],
                             idt32[0:64, 0:64]).then_inc(s_pe.h, 1)
                i += 1
        blk.tensor(peb2)

        def evb(v):
            i = 0
            for c in range(32):
                v.wait_ge(s_pe.h, b_pe + i + 1)
                v.tensor_copy(csT2[:, c * 32:(c + 1) * 32], PS[4 + i % 2][0:64, 0:32]).then_inc(s_tp.h, 1)
                i += 1
            for c in range(32):
                v.wait_ge(s_pe.h, b_pe + i + 1)
                v.tensor_copy(wkT2[:, c * 32:(c + 1) * 32], PS[4 + i % 2][0:64, 0:32]).then_inc(s_tp.h, 1)
                i += 1
            for c in range(32):
                v.wait_ge(s_pe.h, b_pe + i + 1)
                v.tensor_copy(k_tm[:, c * 64:(c + 1) * 64], PS[4 + i % 2][0:64, 0:64]).then_inc(s_tp.h, 1)
                i += 1
        blk.vector(evb)

    with nc.Block() as blk:
        b_pe = s_pe.reserve(32 + 4)
        b_ev = s_ev.reserve(32 + 4)

        def peb(te):
            i = 0
            for h in range(32):
                if i >= 2:
                    te.wait_ge(s_ev.h, b_ev + i - 1)
                te.matmul(PS[4 + i % 2][0:64, 0:32], bass.AP(idt32, h, [[128, 32], [0, 64]]),
                          dex[0:32, :], start=True, stop=True).then_inc(s_pe.h, 1)
                i += 1
            for bk in range(4):
                if i >= 2:
                    te.wait_ge(s_ev.h, b_ev + i - 1)
                last = None
                for j in range(8):
                    c = bk * 8 + j
                    last = te.matmul(PS[bk][0:64, j * 64:(j + 1) * 64],
                                     k_b[:, c * 64:(c + 1) * 64],
                                     q_b[:, c * 64:(c + 1) * 64],
                                     start=(j == 0), stop=(j == 7))
                last.then_inc(s_pe.h, 1)
                i += 1
        blk.tensor(peb)

        def evb(v):
            i = 0
            for h in range(32):
                v.wait_ge(s_pe.h, b_pe + i + 1)
                v.tensor_copy(d_bc[:, h * 32:(h + 1) * 32], PS[4 + i % 2][0:64, 0:32]).then_inc(s_ev.h, 1)
                i += 1
            for bk in range(4):
                v.wait_ge(s_pe.h, b_pe + i + 1)
                v.tensor_tensor(out=QKm[:, bk * 512:(bk + 1) * 512], in0=PS[bk][0:64, 0:512],
                                in1=bass.AP(trilm, 0, [[64, 64], [0, 8], [1, 64]]),
                                op=ALU.mult).then_inc(s_ev.h, 1)
                i += 1
        blk.vector(evb)

    if dbg_dump("ssdprep", [(cs[:, 0:2048], 0, 2048), (QKm[:, 0:2048], 40, 2048),
                            (csT2[:, 0:1024], 110, 1024), (wkT2[:, 0:1024], 180, 1024),
                            (d_bc[:, 0:1024], 250, 1024), (k_tm[:, 0:1024], 320, 1024),
                            (k_b[:, 0:2048], 400, 2048), (q_b[:, 0:2048], 470, 2048)]):
        return nc

    with nc.Block() as blk:
        NSEG = 4
        NPAIR_TOT = 32
        NICB = 2 * NSEG * 16
        b_csb_pe = s_pe.n
        s_pe.reserve(2 * NSEG * (16 + 4 * 32))
        b_sb = s_b.reserve(NICB)
        b_exp = s_ev.reserve(NICB * 2)
        b_sub = s_tp.reserve(NICB)
        b_att = s_a.reserve(NICB * 2)
        b_min = s_g.reserve(NICB + 2 * 33)
        b_vt = s_cast.reserve(NPAIR_TOT * 33)
        b_ymm = s_mm.reserve(NPAIR_TOT * 8)
        b_sev = s_st.reserve(NPAIR_TOT * 4)
        b_upd = s_w.reserve(2 + NPAIR_TOT * 2)
        for s in (s_v0, s_v1):
            s.n += 16 * 16
        for s in (s_y0, s_y1):
            s.n += 16 * 16
        b_yev = s_cc.reserve(NPAIR_TOT * 32)

        def pe_k1_val(icb):
            gsg, hh = divmod(icb, 16)
            return b_csb_pe + gsg * 144 + hh + 1

        def pe_tp_val(g):
            ipair, j = divmod(g, 32)
            gsg, lp = divmod(ipair, 4)
            return b_csb_pe + gsg * 144 + 16 + lp * 32 + j + 1

        def upd_val(gc):
            g2, c = divmod(gc, 32)
            return b_upd + (g2 + 1) + gc

        g_seq = []
        for _g2 in range(2):
            g_seq.append(("cast", ("init", _g2)))
            for _sg in range(NSEG):
                for _hh in range(16):
                    g_seq.append(("min", _g2 * 64 + _sg * 16 + _hh))
                for _lp in range(4):
                    for _par in range(2):
                        g_seq.append(("cast", _g2 * 32 + (_sg * 4 + _lp) * 2 + _par))
        g_min_val = {}
        g_cast_val = {}
        for _i, (_k, _v) in enumerate(g_seq):
            (g_min_val if _k == "min" else g_cast_val)[_v] = _i + 1

        def mirror_val(gc):
            if gc % 32 == 0:
                return g_cast_val[("init", gc // 32)]
            return g_cast_val[gc - 1]

        def y_store_one(e, ipair):
            g2, c2 = divmod(ipair, 16)
            e.wait_ge(s_cc.h, b_yev + (ipair + 1) * 32)
            e.dma_start(out=bass.AP(pk_y.tensor, (c2 * P) * D + g2 * 1024,
                                    [[D, 64], [64 * D, 2], [1, 1024]]),
                        in_=bass.AP(y_stg[ipair % 2], 0, [[2048, 64], [1024, 2], [1, 1024]])
                        ).then_inc((s_y0 if ipair % 2 == 0 else s_y1).h, 16)

        def dmab(e):
            for zi in range(NPAIR_TOT):
                g2, c2 = divmod(zi, 16)
                slot = zi % 2
                if zi >= 2:
                    e.wait_ge(s_cast.h, b_vt + 33 * (zi - 2) + 32)
                e.dma_start(out=vslab[slot][:, :],
                            in_=bass.AP(pk_xc.tensor, c2 * P, [[2048, P], [P * 2048, 16], [1, P]])
                            ).then_inc((s_v0 if slot == 0 else s_v1).h, 16)
                if zi >= 1:
                    y_store_one(e, zi - 1)
            y_store_one(e, NPAIR_TOT - 1)
            e.wait_ge(s_y0.h, 16 * 16)
            e.wait_ge(s_y1.h, 16 * 16)
        blk.sync(dmab)

        def scb(sc):
            for icb in range(NICB):
                sc.wait_ge(s_pe.h, pe_k1_val(icb))
                if icb >= 2:
                    sc.wait_ge(s_tp.h, b_sub + icb - 1)
                    sc.wait_ge(s_a.h, b_att + 2 * (icb - 1))
                sc.activation(out=csb[icb % 2][:, :], in_=PS[5][0:64, 0:512],
                              func=ACTF.Identity).then_inc(s_b.h, 1)
                sc.wait_ge(s_g.h, b_min + g_min_val[icb])
                sc.activation(out=Esc[icb % 2][:, :], in_=Dsc[icb % 2][:, :],
                              func=ACTF.Exp).then_inc(s_ev.h, 1)
                sc.activation(out=xpcs[icb % 2][:, :], in_=csb[icb % 2][:, :],
                              func=ACTF.Exp).then_inc(s_ev.h, 1)
        blk.scalar(scb)

        def gpb(g):
            for g2 in range(2):
                g.wait_ge(s_w.h, b_upd + (g2 + 1) + g2 * 32)
                g.tensor_copy(state_b[:, :], state[:, :]).then_inc(s_g.h, 1)
                for sg in range(NSEG):
                    for hh in range(16):
                        icb = g2 * 64 + sg * 16 + hh
                        g.wait_ge(s_tp.h, b_sub + icb + 1)
                        g.tensor_single_scalar(out=Dsc[icb % 2][:, :], in_=Dsc[icb % 2][:, :],
                                               scalar=0.0, op=ALU.min).then_inc(s_g.h, 1)
                    for lp in range(4):
                        for par in range(2):
                            gc = g2 * 32 + (sg * 4 + lp) * 2 + par
                            g.wait_ge(s_w.h, upd_val(gc) + 1)
                            g.tensor_copy(state_b[:, :], state[:, :]).then_inc(s_g.h, 1)
        blk.gpsimd(gpb)

        def peb(te):
            icb = 0
            for g2 in range(2):
                H0 = g2 * 16
                for sg in range(NSEG):
                    gsg = g2 * NSEG + sg
                    for hh in range(16):
                        h = H0 + hh
                        if icb >= 1:
                            te.wait_ge(s_b.h, b_sb + icb)
                        te.matmul(PS[5][0:64, 0:512], bass.AP(idt32, h, [[128, 32], [0, 64]]),
                                  cs[0:32, sg * 512:(sg + 1) * 512],
                                  start=True, stop=True).then_inc(s_pe.h, 1)
                        icb += 1
                    for lp in range(4):
                        c2 = sg * 4 + lp
                        ipair = gsg * 4 + lp
                        slot = ipair % 2
                        te.wait_ge((s_v0 if slot == 0 else s_v1).h, 16 * (ipair // 2 + 1))
                        for j in range(32):
                            par, chk = divmod(j, 16)
                            g = ipair * 32 + j
                            if j >= 2:
                                te.wait_ge(s_cast.h, b_vt + 33 * ipair + j - 1)
                            elif ipair >= 1:
                                te.wait_ge(s_cast.h, b_vt + 33 * (ipair - 1) + 31 + j)
                            te.transpose(PT[g % 2][0:64, 0:P],
                                         vslab[slot][:, chk * P + par * 64:chk * P + par * 64 + 64],
                                         idt[:, :]).then_inc(s_pe.h, 1)
                        te.wait_ge(s_cast.h, b_vt + 33 * (ipair + 1))
                        te.wait_ge(s_a.h, b_att + gsg * 32 + 32)
                        if ipair >= 1:
                            te.wait_ge(s_cc.h, b_yev + 32 * ipair)
                        for par in range(2):
                            lc = lp * 2 + par
                            for hh2 in range(16):
                                bank = PS[par * 2 + (0 if hh2 < 8 else 1)]
                                te.matmul(bank[0:64, (hh2 % 8) * 64:(hh2 % 8) * 64 + 64],
                                          attT[:, hh2 * 512 + lc * 64:hh2 * 512 + lc * 64 + 64],
                                          v_tm[slot][:, par * 2048 + (H0 + hh2) * 64:par * 2048 + (H0 + hh2) * 64 + 64],
                                          start=(hh2 % 8 == 0), stop=False)
                        for par in range(2):
                            gc = g2 * 32 + 2 * c2 + par
                            lc = lp * 2 + par
                            for half in range(2):
                                if not (ipair == 0 and par == 0 and half == 0):
                                    te.wait_ge(s_st.h, b_sev + 4 * ipair + 2 * par + half)
                                last = None
                                for hh2 in range(half * 8, half * 8 + 8):
                                    last = te.matmul(PS[4][0:64, (hh2 % 8) * 64:(hh2 % 8) * 64 + 64],
                                                     ks[slot][:, par * 1024 + hh2 * 64:par * 1024 + hh2 * 64 + 64],
                                                     v_tm[slot][:, par * 2048 + (H0 + hh2) * 64:par * 2048 + (H0 + hh2) * 64 + 64],
                                                     start=(hh2 % 8 == 0), stop=(hh2 % 8 == 7))
                                last.then_inc(s_mm.h, 1)
                            te.wait_ge(s_g.h, b_min + mirror_val(gc))
                            for hh2 in range(16):
                                bank = PS[par * 2 + (0 if hh2 < 8 else 1)]
                                inst = te.matmul(bank[0:64, (hh2 % 8) * 64:(hh2 % 8) * 64 + 64],
                                                 qsc8[:, hh2 * 512 + lc * 64:hh2 * 512 + lc * 64 + 64],
                                                 state_b[:, hh2 * 64:hh2 * 64 + 64],
                                                 start=False, stop=(hh2 % 8 == 7))
                                if hh2 == 7 or hh2 == 15:
                                    inst.then_inc(s_mm.h, 1)
        blk.tensor(peb)

        def veb(v):
            icb = 0
            for g2 in range(2):
                H0 = g2 * 16
                if g2 == 1:
                    v.wait_ge(s_mm.h, b_ymm + 8 * 16)
                    v.wait_ge(s_g.h, b_min + g_cast_val[31])
                v.memset(state[:, :], 0.0).then_inc(s_w.h, 1)
                for sg in range(NSEG):
                    gsg = g2 * NSEG + sg
                    if gsg >= 1:
                        v.wait_ge(s_mm.h, b_ymm + 8 * (4 * gsg))
                    for hh in range(16):
                        h = H0 + hh
                        v.wait_ge(s_b.h, b_sb + icb + 1)
                        if icb >= 2:
                            v.wait_ge(s_ev.h, b_exp + 2 * (icb - 1))
                        v.tensor_tensor(out=Dsc[icb % 2][:, :], in0=csb[icb % 2][:, :],
                                        in1=bass.AP(csT2, (sg * 8) * 32 + h, [[1024, 64], [32, 8], [0, 64]]),
                                        op=ALU.subtract).then_inc(s_tp.h, 1)
                        v.wait_ge(s_ev.h, b_exp + icb * 2 + 1)
                        v.tensor_tensor(out=attT[:, hh * 512:(hh + 1) * 512],
                                        in0=QKm[:, sg * 512:(sg + 1) * 512],
                                        in1=Esc[icb % 2][:, :], op=ALU.mult)
                        v.wait_ge(s_ev.h, b_exp + icb * 2 + 2)
                        v.tensor_tensor(out=qsc8[:, hh * 512:(hh + 1) * 512],
                                        in0=q_f[:, sg * 512:(sg + 1) * 512],
                                        in1=xpcs[icb % 2][:, :], op=ALU.mult).then_inc(s_a.h, 2)
                        icb += 1
                    for lp in range(4):
                        c2 = sg * 4 + lp
                        ipair = gsg * 4 + lp
                        slot = ipair % 2
                        if ipair >= 2:
                            v.wait_ge(s_mm.h, b_ymm + 8 * (ipair - 1))
                        for j in range(32):
                            par, chk = divmod(j, 16)
                            g = ipair * 32 + j
                            v.wait_ge(s_pe.h, pe_tp_val(g))
                            v.tensor_copy(v_tm[slot][:, par * 2048 + chk * P:par * 2048 + (chk + 1) * P],
                                          PT[g % 2][0:64, 0:P]).then_inc(s_cast.h, 1)
                        last = None
                        for par in range(2):
                            c = 2 * c2 + par
                            last = v.tensor_tensor(
                                out=ks[slot][:, par * 1024:par * 1024 + 1024],
                                in0=bass.AP(k_tm, c * 64, [[2048, 64], [0, 16], [1, 64]]),
                                in1=bass.AP(wkT2, c * 32 + H0, [[1024, 64], [1, 16], [0, 64]]),
                                op=ALU.mult)
                        last.then_inc(s_cast.h, 1)
                        for par in range(2):
                            gc = g2 * 32 + 2 * c2 + par
                            for half in range(2):
                                v.wait_ge(s_mm.h, b_ymm + 8 * ipair + 4 * par + half + 1)
                                v.tensor_copy(states_st[:, par * 1024 + half * 512:par * 1024 + half * 512 + 512],
                                              PS[4][0:64, 0:512]).then_inc(s_st.h, 1)
                            v.wait_ge(s_mm.h, b_ymm + 8 * ipair + 4 * par + 4)
                            v.wait_ge(s_g.h, b_min + mirror_val(gc))
                            v.drain()
                            v.tensor_tensor(out=state[:, :], in0=state[:, :],
                                            in1=bass.AP(d_bc, H0 * 32 + (gc - g2 * 32), [[1024, 64], [32, 16], [0, 64]]),
                                            op=ALU.mult)
                            v.drain()
                            v.tensor_add(state[:, :], state[:, :],
                                         states_st[:, par * 1024:par * 1024 + 1024]).then_inc(s_w.h, 1)
                            v.drain()
                        if ipair >= 2:
                            v.wait_ge((s_y0 if slot == 0 else s_y1).h, 16 * (ipair // 2))
                        for par in range(2):
                            for hh2 in range(16):
                                bank = PS[par * 2 + (0 if hh2 < 8 else 1)]
                                v.scalar_tensor_tensor(
                                    out=y_stg[slot][:, par * 1024 + hh2 * 64:par * 1024 + hh2 * 64 + 64],
                                    in0=v_tm[slot][:, par * 2048 + (H0 + hh2) * 64:par * 2048 + (H0 + hh2) * 64 + 64],
                                    scalar=cDP[0:64, H0 + hh2:H0 + hh2 + 1],
                                    in1=bank[0:64, (hh2 % 8) * 64:(hh2 % 8) * 64 + 64],
                                    op0=ALU.mult, op1=ALU.add).then_inc(s_cc.h, 1)
        blk.vector(veb)

    if DEBUG == "ssd":
        with nc.Block() as blk:
            def _(e):
                b = s_gd.reserve(32)
                e.dma_start(out=bass.AP(DBG.tensor, 0, [[4160, 256], [1, 2048]]),
                            in_=bass.AP(pk_y.tensor, 0, [[2048, 256], [1, 2048]])).then_inc(s_gd.h, 16)
                e.dma_start(out=bass.AP(DBG.tensor, 300 * 4160, [[4160, 256], [1, 2048]]),
                            in_=bass.AP(pk_y.tensor, 1024 * 2048, [[2048, 256], [1, 2048]])).then_inc(s_gd.h, 16)
                e.wait_ge(s_gd.h, b + 32)
            blk.gpsimd(_)
        return nc

    pk_zm = nc.dram_tensor("pk_zm", [2048, D], BF16).ap()
    pk_op = nc.dram_tensor("pk_op", [2048, D], F32).ap()
    pk_op2 = nc.dram_tensor("pk_op2", [T, D], F32).ap()
    pk_op3 = nc.dram_tensor("pk_op3", [T, D], F32).ap()

    with nc.Block() as blk:
        b_b = s_b.reserve(16)

        def ld(e):
            for tt in range(16):
                if tt >= 2:
                    e.wait_ge(s_b.h, b_b + tt - 1)
                e.dma_start(out=vslab[tt % 2][:, :],
                            in_=bass.AP(pk_y.tensor, tt * P * D, [[D, P], [1, D]])
                            ).then_inc((s_z0 if tt % 2 == 0 else s_z1).h, 16)
        z0b, z1b = s_z0.n, s_z1.n
        s_z0.n += 8 * 16
        s_z1.n += 8 * 16
        blk.sync(ld)

        def sq(sc):
            for tt in range(16):
                sc.wait_ge((s_z0 if tt % 2 == 0 else s_z1).h, (z0b if tt % 2 == 0 else z1b) + (tt // 2 + 1) * 16)
                if tt > 0:
                    sc.drain()
                sc.activation(out=junk[:, 0:D], in_=vslab[tt % 2][:, :], func=ACTF.Square,
                              accum_out=colA[:, tt:tt + 1]).then_inc(s_b.h, 1)
        blk.scalar(sq)

        def pub(e):
            e.wait_ge(s_b.h, b_b + 16)
            for arm in e.switch_core_id(n=128):
                h2 = arm.logical % 2
                e.dma_start(out=bass.AP(sh_rms.tensor, h2 * 2048, [[1, P], [P, 16]]),
                            in_=colA[:, 0:16], allow_slow_non_contiguous=True).then_inc(s_g.h if False else s_gd.h, 16)
                e.dma_start(out=pk_zm[:, :],
                            in_=bass.AP(sh_z.tensor, h2 * D, [[4096, 2048], [1, D]])).then_inc(s_gd.h, 16)
            b2 = s_gd.reserve(32)
            e.wait_ge(s_gd.h, b2 + 32)
        blk.gpsimd(pub)
    with nc.Block() as blk:
        barrier(blk)

    ygT = OV.place("ygT", [P, 16 * 2048], BF16, BIG)
    yg = [OV.place(f"yg{i}", [P, D], BF16, WB0 + 65536 + i * 4096) for i in range(2)]
    z_t = [OV.place(f"z_t{i}", [P, D], BF16, WB0 + 65536 + 8192 + i * 4096) for i in range(2)]

    with nc.Block() as blk:
        b_ld = s_ld.reserve(16)

        def ld(e):
            for arm in e.switch_core_id(n=128):
                h2 = arm.logical % 2
                e.dma_start(out=colB[:, 0:16],
                            in_=bass.AP(sh_rms.tensor, (1 - h2) * 2048, [[1, P], [P, 16]]),
                            allow_slow_non_contiguous=True).then_inc(s_ld.h, 16)
        blk.sync(ld)

        def vv(v):
            b_a = s_a.reserve(1)
            v.wait_ge(s_ld.h, s_ld.n)
            v.tensor_add(colC[:, 0:16], colA[:, 0:16], colB[:, 0:16])
            v.drain()
            v.tensor_scalar(out=colC[:, 0:16], in0=colC[:, 0:16], scalar1=1.0 / 4096.0,
                            scalar2=None, op0=ALU.mult).then_inc(s_a.h, 1)
        blk.vector(vv)

        def sc1(sc):
            b_b = s_b.reserve(1)
            sc.wait_ge(s_a.h, s_a.n)
            sc.activation(out=colC[:, 0:16], in_=colC[:, 0:16], func=ACTF.Sqrt, bias=cst[:, 1:2]).then_inc(s_b.h, 1)
        blk.scalar(sc1)

        def v2(v):
            b_a = s_a.reserve(1)
            v.wait_ge(s_b.h, s_b.n)
            v.reciprocal(colD[:, 0:16], colC[:, 0:16]).then_inc(s_a.h, 1)
        blk.vector(v2)

    with nc.Block() as blk:
        b_b = s_b.reserve(16 * 2)
        b_a = s_a.reserve(16)
        b_pe = s_pe.reserve(16 * 16)
        b_tp = s_tp.reserve(16 * 16)
        yz0, yz1 = s_v0.n, s_v1.n
        s_v0.n += 8 * 16
        s_v1.n += 8 * 16
        zz0, zz1 = s_z0.n, s_z1.n
        s_z0.n += 8 * 16
        s_z1.n += 8 * 16

        def ld(e):
            for tt in range(16):
                sl = tt % 2
                if tt >= 2:
                    e.wait_ge(s_a.h, b_a + tt - 1)
                e.dma_start(out=vslab[sl][:, :],
                            in_=bass.AP(pk_y.tensor, tt * P * D, [[D, P], [1, D]])
                            ).then_inc((s_v0 if sl == 0 else s_v1).h, 16)
                if tt >= 2:
                    e.wait_ge(s_b.h, b_b + 2 * (tt - 2) + 1)
                e.dma_start(out=z_t[sl][:, :],
                            in_=bass.AP(pk_zm.tensor, tt * P * D, [[D, P], [1, D]])
                            ).then_inc((s_z0 if sl == 0 else s_z1).h, 16)
        blk.sync(ld)

        def scb(sc):
            for tt in range(16):
                sl = tt % 2
                sc.wait_ge((s_z0 if sl == 0 else s_z1).h, (zz0 if sl == 0 else zz1) + (tt // 2 + 1) * 16)
                if tt >= 2:
                    sc.drain()
                sc.activation(out=scr1[:, 0:D], in_=z_t[sl][:, :], func=ACTF.Silu).then_inc(s_b.h, 1)
                sc.wait_ge(s_a.h, b_a + tt + 1)
                if tt >= 2:
                    sc.wait_ge(s_tp.h, b_tp + (tt - 1) * 16)
                sc.activation(out=yg[sl][:, :], in_=scr2[:, 0:D], func=ACTF.Identity,
                              scale=colD[:, tt:tt + 1]).then_inc(s_b.h, 1)
        blk.scalar(scb)

        def peb(te):
            for tt in range(16):
                sl = tt % 2
                te.wait_ge(s_b.h, b_b + 2 * tt + 2)
                for kc in range(16):
                    i = tt * 16 + kc
                    if i >= 2:
                        te.wait_ge(s_tp.h, b_tp + i - 1)
                    te.transpose(PT[i % 2][0:P, 0:P], yg[sl][:, kc * P:(kc + 1) * P],
                                 idt[:, :]).then_inc(s_pe.h, 1)
        blk.tensor(peb)

        def evacs(v, tt):
            for kc in range(16):
                i = tt * 16 + kc
                v.wait_ge(s_pe.h, b_pe + i + 1)
                v.tensor_copy(ygT[:, kc * 2048 + tt * P:kc * 2048 + (tt + 1) * P],
                              PT[i % 2][0:P, 0:P]).then_inc(s_tp.h, 1)

        def vb(v):
            for tt in range(16):
                sl = tt % 2
                v.wait_ge((s_v0 if sl == 0 else s_v1).h, (yz0 if sl == 0 else yz1) + (tt // 2 + 1) * 16)
                v.wait_ge(s_b.h, b_b + 2 * tt + 1)
                v.tensor_tensor(out=scr2[:, 0:D], in0=vslab[sl][:, :], in1=scr1[:, 0:D],
                                op=ALU.mult).then_inc(s_a.h, 1)
                if tt >= 1:
                    evacs(v, tt - 1)
            evacs(v, 15)
        blk.vector(vb)

    op_st = [OV.place(f"op_st{i}", [P, 512], F32, WB0 + 65536 + 16384 + i * 2048) for i in range(4)]
    with nc.Block() as blk:
        b_w = s_w.reserve(4 * 16)
        b_cast = s_cast.reserve(4 * 16)
        b_mm = s_mm.reserve(4 * 16)
        b_ev = s_ev.reserve(4 * 16)
        b_ops = [_s.reserve(16 * 16) for _s in s_ops]

        def dmab(e):
            for ng in range(4):
                if ng >= 1:
                    e.wait_ge(s_cast.h, b_cast + ng * 16)
                e.dma_start(out=ws[:, 0:8192],
                            in_=bass.AP(Wop.tensor, ng * 512, [[D, P], [P * D, 16], [1, 512]])
                            ).then_inc(s_w.h, 16)
        blk.sync(dmab)

        def castb(sc):
            for ng in range(4):
                sc.wait_ge(s_w.h, b_w + (ng + 1) * 16)
                if ng >= 2:
                    sc.wait_ge(s_mm.h, b_mm + (ng - 1) * 16)
                for kc in range(16):
                    if ng > 0 or kc > 0:
                        sc.drain()
                    sc.activation(out=wb[ng % 2][:, kc * 512:(kc + 1) * 512], in_=ws[:, kc * 512:(kc + 1) * 512],
                                  func=ACTF.Identity, scale=nsc_sb[:, kc:kc + 1]).then_inc(s_cast.h, 1)
        blk.scalar(castb)

        def peb(te):
            for ng in range(4):
                te.wait_ge(s_cast.h, b_cast + (ng + 1) * 16)
                for tt in range(16):
                    i = ng * 16 + tt
                    bank = PS[tt % 4]
                    if i >= 4:
                        te.wait_ge(s_ev.h, b_ev + i - 3)
                    for kc in range(16):
                        te.matmul(bank[:, 0:512], ygT[:, kc * 2048 + tt * P:kc * 2048 + (tt + 1) * P],
                                  wb[ng % 2][:, kc * 512:(kc + 1) * 512],
                                  start=(kc == 0), stop=(kc == 15)).then_inc(s_mm.h, 1) if kc == 15 else \
                        te.matmul(bank[:, 0:512], ygT[:, kc * 2048 + tt * P:kc * 2048 + (tt + 1) * P],
                                  wb[ng % 2][:, kc * 512:(kc + 1) * 512],
                                  start=(kc == 0), stop=(kc == 15))
        blk.tensor(peb)

        def evb(v):
            for ng in range(4):
                for tt in range(16):
                    i = ng * 16 + tt
                    bank = PS[tt % 4]
                    v.wait_ge(s_mm.h, b_mm + i + 1)
                    if i >= 4:
                        v.wait_ge(s_ops[tt % 4].h, b_ops[tt % 4] + 16 * (i // 4))
                    v.tensor_copy(op_st[tt % 4][:, :], bank[:, 0:512]).then_inc(s_ev.h, 1)
        blk.vector(evb)

        def stb(e):
            for ng in range(4):
                for tt in range(16):
                    i = ng * 16 + tt
                    e.wait_ge(s_ev.h, b_ev + i + 1)
                    e.dma_start(out=bass.AP(pk_op.tensor, tt * P * D + ng * 512, [[D, P], [1, 512]]),
                                in_=op_st[tt % 4][:, :]).then_inc(s_ops[tt % 4].h, 16)
            for _k, _s in enumerate(s_ops):
                e.wait_ge(_s.h, b_ops[_k] + 16 * 16)
        blk.gpsimd(stb)

    if dbg_dump("gate", [(colA[:, 0:16], 0, 16), (colB[:, 0:16], 130, 16), (colD[:, 0:16], 260, 16),
                         (ygT[:, 0:2048], 390, 2048),
                         (bass.AP(pk_op.tensor, 0, [[D, 256], [1, D]]), 520, D),
                         (bass.AP(pk_y.tensor, 0, [[D, 128], [1, D]]), 800, D),
                         (bass.AP(pk_zm.tensor, 0, [[D, 128], [1, D]]), 950, D)]):
        return nc

    with nc.Block() as blk:
        def pub(e):
            b2 = s_gd.reserve(32)
            for arm in e.switch_core_id(n=128):
                h2 = arm.logical % 2
                e.dma_start(out=sh_op[h2 * T * D:(h2 * T + T) * D].rearrange("(a b) -> a b", b=D) if False else
                            bass.AP(sh_op.tensor, h2 * T * D, [[D, T], [1, D]]),
                            in_=bass.AP(pk_op.tensor, (1 - h2) * T * D, [[D, T], [1, D]])).then_inc(s_gd.h, 16)
                e.dma_start(out=pk_op3[:, :],
                            in_=bass.AP(pk_op.tensor, h2 * T * D, [[D, T], [1, D]])).then_inc(s_gd.h, 16)
            e.wait_ge(s_gd.h, b2 + 32)
        blk.gpsimd(pub)
    with nc.Block() as blk:
        barrier(blk)
    with nc.Block() as blk:
        def rd(e):
            b2 = s_gd.reserve(16)
            for arm in e.switch_core_id(n=128):
                h2 = arm.logical % 2
                e.dma_start(out=pk_op2[:, :],
                            in_=bass.AP(sh_op.tensor, (1 - h2) * T * D, [[D, T], [1, D]])).then_inc(s_gd.h, 16)
            e.wait_ge(s_gd.h, b2 + 16)
        blk.gpsimd(rd)

    if dbg_dump("cmb", [(bass.AP(pk_op2.tensor, 0, [[D, 256], [1, D]]), 0, D),
                        (bass.AP(pk_op3.tensor, 0, [[D, 256], [1, D]]), 300, D)]):
        return nc

    x_sb2 = OV.place("x_sb2", [P, NTT * D], F32, BIG)
    cmb = [OV.place(f"cmb{i}", [P, 2 * D], F32, WB0 + i * 16384) for i in range(2)]
    with nc.Block() as blk:
        b_ld = s_ld.reserve(8 * 3 * 16)
        b_a = s_a.reserve(8)

        def ld(e):
            for t in range(NTT):
                sl = t % 2
                if t >= 2:
                    e.wait_ge(s_a.h, b_a + t - 1)
                e.dma_start(out=x_sb2[:, t * D:(t + 1) * D], in_=X[t * P:(t + 1) * P, :]).then_inc(s_ld.h, 16)
                e.dma_start(out=cmb[sl][:, 0:D], in_=pk_op2[t * P:(t + 1) * P, :]).then_inc(s_ld.h, 16)
                e.dma_start(out=cmb[sl][:, D:2 * D], in_=pk_op3[t * P:(t + 1) * P, :]).then_inc(s_ld.h, 16)
                e.wait_ge(s_ld.h, b_ld + (t + 1) * 48)
        blk.sync(ld)

        def vb(v):
            for t in range(NTT):
                sl = t % 2
                v.wait_ge(s_ld.h, b_ld + (t + 1) * 48)
                v.tensor_add(x_sb2[:, t * D:(t + 1) * D], x_sb2[:, t * D:(t + 1) * D], cmb[sl][:, 0:D])
                v.drain()
                v.tensor_add(x_sb2[:, t * D:(t + 1) * D], x_sb2[:, t * D:(t + 1) * D],
                             cmb[sl][:, D:2 * D]).then_inc(s_a.h, 1)
        blk.vector(vb)

    if DEBUG == "postx" or STOP == "M1":
        with nc.Block() as blk:
            def _(e):
                b = s_st.reserve(NTT * 16)
                for t in range(NTT):
                    e.wait_ge(s_a.h, s_a.n - NTT + t + 1)
                    e.dma_start(out=OUT[t * P:(t + 1) * P, :], in_=x_sb2[:, t * D:(t + 1) * D]).then_inc(s_st.h, 16)
                e.wait_ge(s_st.h, b + NTT * 16)
            blk.sync(_)
        return nc

    hq_all = OV.place("hq_all", [P, NTT * D], BF16, 61440)
    idt32f = OV.place("idt32f", [P, P], F32, 208896)
    with nc.Block() as blk:
        emit_ln(blk, [x_sb2[:, t * D:(t + 1) * D] for t in range(NTT)], D, 1e-5)
        b_lna = s_a.n
        b_lnb = s_b.n
        b_a = s_a.reserve(8 * NTT)
        b_b = s_b.reserve(4 * NTT)

        def scb(sc):
            sc.wait_ge(s_a.h, b_lna)
            for t in range(NTT):
                xt = x_sb2[:, t * D:(t + 1) * D]
                if t >= 1:
                    sc.wait_ge(s_a.h, b_a + 8 * t)
                sc.activation(out=scr1[:, 0:D], in_=xt, func=ACTF.Identity,
                              bias=bcol[:, t:t + 1], scale=rcol[:, t:t + 1]).then_inc(s_b.h, 1)
                sc.drain()
                sc.activation(out=scr2[:, 0:D], in_=scr1[:, 0:D], func=ACTF.Square,
                              accum_out=qcol[:, t:t + 1]).then_inc(s_b.h, 1)
                sc.drain()
                sc.activation(out=qcol[:, t:t + 1], in_=qcol[:, t:t + 1], func=ACTF.Sqrt,
                              bias=1e-5, scale=1.0 / D).then_inc(s_b.h, 1)
                sc.wait_ge(s_a.h, b_a + 8 * t + 2)
                sc.drain()
                sc.activation(out=scr2[:, 0:D], in_=scr1[:, 0:D], func=ACTF.Identity,
                              scale=colB[:, t:t + 1]).then_inc(s_b.h, 1)
        blk.scalar(scb)

        def vb(v):
            tmp1 = qcol[:, 8:9]
            for t in range(NTT):
                xt = x_sb2[:, t * D:(t + 1) * D]
                v.wait_ge(s_b.h, b_lnb + 4 * t + 1)
                v.tensor_add(xt, xt, scr1[:, 0:D]).then_inc(s_a.h, 1)
                v.wait_ge(s_b.h, b_lnb + 4 * t + 3)
                v.drain()
                v.reciprocal(colB[:, t:t + 1], qcol[:, t:t + 1]).then_inc(s_a.h, 1)
                v.wait_ge(s_b.h, b_lnb + 4 * t + 4)
                v.drain()
                v.tensor_tensor(out=scr2[:, 0:D], in0=scr2[:, 0:D], in1=wnorm_bc[:, 0:D],
                                op=ALU.mult).then_inc(s_a.h, 1)
                v.drain()
                v.tensor_reduce(out=colA[:, t:t + 1], in_=scr2[:, 0:D], axis=AX.X,
                                op=ALU.max, apply_absolute_value=True).then_inc(s_a.h, 1)
                v.drain()
                v.tensor_scalar(out=colD[:, t:t + 1], in0=colA[:, t:t + 1], scalar1=1e-5,
                                scalar2=None, op0=ALU.max).then_inc(s_a.h, 1)
                v.drain()
                v.tensor_scalar(out=tmp1, in0=colD[:, t:t + 1], scalar1=1.0 / 127.0,
                                scalar2=None, op0=ALU.mult)
                v.drain()
                v.reciprocal(colC[:, t:t + 1], tmp1).then_inc(s_a.h, 1)
                v.drain()
                v.tensor_scalar(out=scr1[:, 0:D], in0=scr2[:, 0:D], scalar1=colC[:, t:t + 1],
                                scalar2=MAGIC, op0=ALU.mult, op1=ALU.add).then_inc(s_a.h, 1)
                v.drain()
                v.tensor_scalar(out=hq_all[:, t * D:(t + 1) * D], in0=scr1[:, 0:D],
                                scalar1=MAGIC, scalar2=None, op0=ALU.subtract).then_inc(s_a.h, 1)
        blk.vector(vb)

    with nc.Block() as blk:
        transpose_batch(blk, [(hq_all[:, (t * KC + j) * P:(t * KC + j) * P + P],
                               nT[:, j * T + t * P:j * T + t * P + P])
                              for t in range(NTT) for j in range(KC)])

        def park(e):
            b = s_st.reserve((NTT + 1) * 16)
            e.dma_start(out=idt32f[:, :], in_=IDT32[:, :]).then_inc(s_st.h, 16)
            for t in range(NTT):
                e.dma_start(out=pk_x[t * P:(t + 1) * P, :], in_=x_sb2[:, t * D:(t + 1) * D]).then_inc(s_st.h, 16)
            e.wait_ge(s_st.h, b + (NTT + 1) * 16)
        blk.sync(park)

    u2c1 = OV.place("u2c1", [P, 8 * T], BF16, 45056)
    u2c2 = OV.place("u2c2", [P, 16 * T], BF16, 61440)
    u2c3 = OV.place("u2c3", [P, 40 * T], BF16, 126976)

    def u2ap(hc):
        if hc < 8:
            return u2c1[:, hc * T:(hc + 1) * T]
        if hc < 24:
            return u2c2[:, (hc - 8) * T:(hc - 8 + 1) * T]
        return u2c3[:, (hc - 24) * T:(hc - 24 + 1) * T]

    def bcast_cols(blk, col_ap, dsts):
        b_a = s_a.reserve(1 + 8)
        b_pe = s_pe.reserve(1 + 8)

        def pe1(te):
            te.transpose(PS[4][0:8, 0:P], col_ap, idt32f[:, :]).then_inc(s_pe.h, 1)
            for t in range(8):
                te.wait_ge(s_a.h, b_a + max(1, t))
                te.matmul(PS[4 + (t + 1) % 2][:, 0:P], bass.AP(idt32f, t, [[P, 8], [0, P]]),
                          scr1[0:8, 0:P], start=True, stop=True).then_inc(s_pe.h, 1)
        blk.tensor(pe1)

        def v1(v):
            v.wait_ge(s_pe.h, b_pe + 1)
            v.tensor_copy(scr1[0:8, 0:P], PS[4][0:8, 0:P]).then_inc(s_a.h, 1)
            for t in range(8):
                v.wait_ge(s_pe.h, b_pe + 2 + t)
                v.drain()
                v.tensor_copy(dsts[t], PS[4 + (t + 1) % 2][:, 0:P]).then_inc(s_a.h, 1)
        blk.vector(v1)

    invbc = scr1[:, 1024:2048]
    with nc.Block() as blk:
        b_a0 = s_a.reserve(1)

        def v0(v):
            v.tensor_scalar(out=colD[:, 0:8], in0=colD[:, 0:8], scalar1=1.0 / 127.0,
                            scalar2=qsc[:, 0:1], op0=ALU.mult, op1=ALU.mult).then_inc(s_a.h, 1)
        blk.vector(v0)
    with nc.Block() as blk:
        bcast_cols(blk, colD[:, 0:8], [invbc[:, t * P:(t + 1) * P] for t in range(8)])

    sl_m1 = [scr1[:, 0:512], scr1[:, 512:1024]]
    sl_m3 = [scr2[:, 1024:1536], scr2[:, 1536:2048]]
    wacc = scr2[:, 0:1024]
    NGI = 128

    def wslot(s):
        return wb[0][:, (s % 4) * D:(s % 4) * D + D]

    with nc.Block() as blk:
        slot_sems = [s_z0, s_z1, s_x0, s_x1]
        b_z = [s.reserve(32 * 16) for s in slot_sems]
        b_mm = s_mm.reserve(2 * NGI)
        b_a = s_a.reserve(3 * NGI + 1)
        b_b = s_b.reserve(3 * NGI)
        b_g = s_g.reserve(NGI)

        def dmab(e):
            for pr in range(64):
                for j in range(2):
                    hc = pr + 64 * j
                    sidx = 2 * pr + j
                    if pr >= 2:
                        e.wait_ge(s_mm.h, b_mm + 2 * (2 * (pr - 2) + 1) + j + 1)
                    e.dma_start(out=wslot(sidx), in_=qap_gv(hc)).then_inc(
                        slot_sems[sidx % 4].h, 16)
        blk.sync(dmab)

        def peb(te):
            for i in range(NGI):
                pr, tg = i // 2, i % 2
                par = i % 2
                for j in range(2):
                    psum = PS[2 * par + j]
                    sidx = 2 * pr + j
                    te.wait_ge(slot_sems[sidx % 4].h, b_z[sidx % 4] + 16 * (sidx // 4 + 1))
                    if i >= 2:
                        if j == 0:
                            te.wait_ge(s_a.h, b_a + 3 * (i - 2) + 3)
                        else:
                            te.wait_ge(s_a.h, b_a + 3 * (i - 2) + 2)
                    for kc in range(KC):
                        mm = te.matmul(psum[:, 0:512], wslot(2 * pr + j)[:, kc * P:(kc + 1) * P],
                                       nT[:, kc * T + tg * 512:kc * T + tg * 512 + 512],
                                       start=(kc == 0), stop=(kc == KC - 1))
                    mm.then_inc(s_mm.h, 1)
        blk.tensor(peb)

        def vb(v):
            v.memset(wacc, 0.0).then_inc(s_a.h, 1)
            for i in range(NGI):
                pr, tg = i // 2, i % 2
                par = i % 2
                v.wait_ge(s_mm.h, b_mm + 2 * i + 2)
                if i >= 2:
                    v.wait_ge(s_b.h, b_b + 3 * (i - 2) + 3)
                v.drain()
                v.tensor_tensor(out=sl_m1[par], in0=PS[2 * par + 1][:, 0:512],
                                in1=invbc[:, tg * 512:tg * 512 + 512], op=ALU.mult).then_inc(s_a.h, 1)
                v.wait_ge(s_g.h, b_g + i + 1)
                v.drain()
                v.tensor_tensor(out=sl_m1[par], in0=PS[2 * par][:, 0:512], in1=sl_m3[par],
                                op=ALU.mult).then_inc(s_a.h, 1)
                v.wait_ge(s_b.h, b_b + 3 * i + 3)
                v.drain()
                v.tensor_tensor(out=wacc[:, tg * 512:tg * 512 + 512],
                                in0=wacc[:, tg * 512:tg * 512 + 512], in1=sl_m3[par],
                                op=ALU.max).then_inc(s_a.h, 1)
        blk.vector(vb)

        def gb(p):
            for i in range(NGI):
                pr, tg = i // 2, i % 2
                par = i % 2
                p.wait_ge(s_b.h, b_b + 3 * i + 1)
                if i >= 2:
                    p.wait_ge(s_a.h, b_a + 3 * (i - 2) + 4)
                p.drain()
                p.tensor_tensor(out=sl_m3[par], in0=sl_m1[par],
                                in1=invbc[:, tg * 512:tg * 512 + 512], op=ALU.mult).then_inc(s_g.h, 1)
        blk.gpsimd(gb)

        def scb(sc):
            for i in range(NGI):
                pr, tg = i // 2, i % 2
                par = i % 2
                sc.wait_ge(s_a.h, b_a + 3 * i + 1 + 1)
                sc.activation(out=sl_m1[par], in_=sl_m1[par], func=ACTF.Silu).then_inc(s_b.h, 1)
                sc.wait_ge(s_a.h, b_a + 3 * i + 2 + 1)
                sc.activation(out=u2ap(pr)[:, tg * 512:tg * 512 + 512], in_=sl_m1[par],
                              func=ACTF.Identity).then_inc(s_b.h, 1)
                sc.drain()
                sc.activation(out=sl_m3[par], in_=sl_m1[par], func=ACTF.Abs).then_inc(s_b.h, 1)
        blk.scalar(scb)

    comb2bc = scr2[:, 1024:2048]
    with nc.Block() as blk:
        b_pe = s_pe.reserve(8)
        b_a = s_a.reserve(8 + 3)

        def peb(te):
            for t in range(8):
                if t >= 2:
                    te.wait_ge(s_a.h, b_a + t - 1)
                te.transpose(PS[4 + t % 2][0:P, 0:P], wacc[:, t * P:(t + 1) * P],
                             idt32f[:, :]).then_inc(s_pe.h, 1)
        blk.tensor(peb)

        def vb(v):
            for t in range(8):
                v.wait_ge(s_pe.h, b_pe + t + 1)
                if t:
                    v.drain()
                v.tensor_reduce(out=colA[:, t:t + 1], in_=PS[4 + t % 2][0:P, 0:P],
                                axis=AX.X, op=ALU.max).then_inc(s_a.h, 1)
            v.drain()
            v.tensor_scalar(out=colA[:, 0:8], in0=colA[:, 0:8], scalar1=1e-5,
                            scalar2=None, op0=ALU.max).then_inc(s_a.h, 1)
            v.drain()
            v.tensor_scalar(out=mcol[:, 0:8], in0=colA[:, 0:8], scalar1=1.0 / 127.0,
                            scalar2=None, op0=ALU.mult)
            v.drain()
            v.reciprocal(colB[:, 0:8], mcol[:, 0:8]).then_inc(s_a.h, 1)
            v.drain()
            v.tensor_scalar(out=colC[:, 0:8], in0=colA[:, 0:8], scalar1=1.0 / 127.0,
                            scalar2=qsc[:, 1:2], op0=ALU.mult, op1=ALU.mult).then_inc(s_a.h, 1)
        blk.vector(vb)
    with nc.Block() as blk:
        bcast_cols(blk, colC[:, 0:8], [comb2bc[:, t * P:(t + 1) * P] for t in range(8)])
    with nc.Block() as blk:
        bcast_cols(blk, colB[:, 0:8], [invbc[:, t * P:(t + 1) * P] for t in range(8)])

    with nc.Block() as blk:
        cbuf = [scr2[:, 0:1024], scr1[:, 0:1024]]
        b_b = s_b.reserve(64)
        b_a = s_a.reserve(32)
        b_g = s_g.reserve(32)

        def scb(sc):
            for hc in range(64):
                par = hc % 2
                if hc >= 2:
                    sc.wait_ge([s_a, s_g][par].h, [b_a, b_g][par] + (hc - 2) // 2 + 1)
                sc.activation(out=cbuf[par], in_=u2ap(hc), func=ACTF.Identity).then_inc(s_b.h, 1)
        blk.scalar(scb)

        def qone(eng, hc, sem, base):
            par = hc % 2
            eng.wait_ge(s_b.h, b_b + hc + 1)
            eng.drain()
            eng.tensor_tensor(out=cbuf[par], in0=cbuf[par], in1=invbc, op=ALU.mult)
            eng.drain()
            eng.tensor_scalar(out=u2ap(hc), in0=cbuf[par], scalar1=MAGIC, scalar2=MAGIC,
                              op0=ALU.add, op1=ALU.subtract).then_inc(sem.h, 1)

        def vb(v):
            for hc in range(0, 64, 2):
                qone(v, hc, s_a, b_a)
        blk.vector(vb)

        def gb(p):
            for hc in range(1, 64, 2):
                qone(p, hc, s_g, b_g)
        blk.gpsimd(gb)

    with nc.Block() as blk:
        stg = [scr1[:, 0:512], scr1[:, 512:1024]]
        ostg = [scr1[:, 1024:2048], scr2[:, 0:1024]]
        b_z = [s_z0.reserve(8 * 16 * 16 // 2), s_z1.reserve(8 * 16 * 16 // 2)]
        b_mm = s_mm.reserve(2 * 16)
        b_a = s_a.reserve(10 * 16)
        b_pe2 = s_pe.reserve(8 * 16)
        b_st = s_st.reserve(16 * 16)

        def dmab(e):
            for dmc in range(16):
                for h in range(2):
                    s = 2 * dmc + h
                    if s >= 2:
                        e.wait_ge(s_mm.h, b_mm + s - 1)
                    for q in range(4):
                        e.dma_start(out=wb[0][:, (s % 2) * 4096 + q * 1024:(s % 2) * 4096 + (q + 1) * 1024],
                                    in_=qap_wo(dmc, 4 * h + q)).then_inc([s_z0, s_z1][s % 2].h, 16)
                if dmc >= 1:
                    e.wait_ge(s_a.h, b_a + 10 * (dmc - 1) + 10)
                    e.dma_start(out=bass.AP(pk_op2.tensor, (dmc - 1) * P,
                                            [[D, P], [P * D, 8], [1, P]]),
                                in_=ostg[(dmc - 1) % 2]).then_inc(s_st.h, 16)
            e.wait_ge(s_a.h, b_a + 10 * 16)
            e.dma_start(out=bass.AP(pk_op2.tensor, 15 * P, [[D, P], [P * D, 8], [1, P]]),
                        in_=ostg[15 % 2]).then_inc(s_st.h, 16)
            e.wait_ge(s_st.h, b_st + 16 * 16)
        blk.sync(dmab)

        def peb(te):
            for dmc in range(16):
                par = dmc % 2
                if dmc >= 2:
                    te.wait_ge(s_a.h, b_a + 10 * (dmc - 2) + 2)
                for h in range(2):
                    s = 2 * dmc + h
                    te.wait_ge([s_z0, s_z1][s % 2].h, b_z[s % 2] + 64 * (s // 2 + 1))
                    for th in range(2):
                        for kc in range(32 * h, 32 * h + 32):
                            mm = te.matmul(PS[2 * par + th][:, 0:512],
                                           wb[0][:, (s % 2) * 4096 + (kc - 32 * h) * P:(s % 2) * 4096 + (kc - 32 * h + 1) * P],
                                           u2ap(kc)[:, th * 512:th * 512 + 512],
                                           start=(kc == 0), stop=(kc == 63))
                    mm.then_inc(s_mm.h, 1)
                if dmc >= 1:
                    for jj in range(8):
                        te.wait_ge(s_a.h, b_a + 10 * (dmc - 1) + (1 if jj < 4 else 2))
                        if jj >= 2:
                            te.wait_ge(s_a.h, b_a + 10 * (dmc - 1) + jj + 1)
                        te.transpose(PS[4 + jj % 2][0:P, 0:P],
                                     stg[jj // 4][:, (jj % 4) * P:(jj % 4) * P + P],
                                     idt32f[:, :]).then_inc(s_pe.h, 1)
            for jj in range(8):
                te.wait_ge(s_a.h, b_a + 10 * 15 + (1 if jj < 4 else 2))
                if jj >= 2:
                    te.wait_ge(s_a.h, b_a + 10 * 15 + jj + 1)
                te.transpose(PS[4 + jj % 2][0:P, 0:P],
                             stg[jj // 4][:, (jj % 4) * P:(jj % 4) * P + P],
                             idt32f[:, :]).then_inc(s_pe.h, 1)
        blk.tensor(peb)

        def vb(v):
            for dmc in range(16):
                par = dmc % 2
                if dmc >= 1:
                    v.wait_ge(s_st.h, b_st + 16 * max(0, dmc - 1))
                    for jj in range(8):
                        v.wait_ge(s_pe.h, b_pe2 + 8 * (dmc - 1) + jj + 1)
                        v.drain()
                        v.tensor_copy(ostg[(dmc - 1) % 2][:, jj * P:(jj + 1) * P],
                                      PS[4 + jj % 2][0:P, 0:P]).then_inc(s_a.h, 1)
                for th in range(2):
                    v.wait_ge(s_mm.h, b_mm + 2 * dmc + 2)
                    if dmc >= 1:
                        v.wait_ge(s_pe.h, b_pe2 + 8 * (dmc - 1) + (4 if th == 0 else 8))
                    v.drain()
                    v.tensor_tensor(out=stg[th], in0=PS[2 * par + th][:, 0:512],
                                    in1=comb2bc[:, th * 512:th * 512 + 512],
                                    op=ALU.mult).then_inc(s_a.h, 1)
            v.wait_ge(s_st.h, b_st + 16 * 15)
            for jj in range(8):
                v.wait_ge(s_pe.h, b_pe2 + 8 * 15 + jj + 1)
                v.drain()
                v.tensor_copy(ostg[15 % 2][:, jj * P:(jj + 1) * P],
                              PS[4 + jj % 2][0:P, 0:P]).then_inc(s_a.h, 1)
        blk.vector(vb)

    with nc.Block() as blk:
        b_ld = s_ld.reserve(8 * 2 * 16)
        b_a = s_a.reserve(8)

        def ld(e):
            for t in range(NTT):
                sl = t % 2
                if t >= 2:
                    e.wait_ge(s_a.h, b_a + t - 1)
                e.dma_start(out=x_sb2[:, t * D:(t + 1) * D], in_=pk_x[t * P:(t + 1) * P, :]).then_inc(s_ld.h, 16)
                e.dma_start(out=cmb[sl][:, 0:D], in_=pk_op2[t * P:(t + 1) * P, :]).then_inc(s_ld.h, 16)
                e.wait_ge(s_ld.h, b_ld + (t + 1) * 32)
        blk.sync(ld)

        def vb(v):
            for t in range(NTT):
                sl = t % 2
                v.wait_ge(s_ld.h, b_ld + (t + 1) * 32)
                v.drain()
                v.tensor_add(x_sb2[:, t * D:(t + 1) * D], x_sb2[:, t * D:(t + 1) * D],
                             cmb[sl][:, 0:D]).then_inc(s_a.h, 1)
        blk.vector(vb)

    if DEBUG == "ffn" or STOP == "M2":
        with nc.Block() as blk:
            def _(e):
                b = s_st.reserve(NTT * 16)
                for t in range(NTT):
                    e.wait_ge(s_a.h, s_a.n - NTT + t + 1)
                    e.dma_start(out=OUT[t * P:(t + 1) * P, :], in_=x_sb2[:, t * D:(t + 1) * D]).then_inc(s_st.h, 16)
                e.wait_ge(s_st.h, b + NTT * 16)
            blk.sync(_)
        return nc


    import os as _os
    _CHK = _os.environ.get("KCHK", "")

    def chk(label):
        if not _CHK:
            return
        with nc.Block() as blk:
            sems = [(s_a, s_a.n), (s_b, s_b.n), (s_pe, s_pe.n), (s_mm, s_mm.n),
                    (s_tp, s_tp.n), (s_g, s_g.n), (s_ev, s_ev.n)]
            def _c(e, sems=sems):
                for s, n in sems:
                    e.wait_ge(s.h, n)
            blk.sync(_c)
        print("CHK", label, "pe", s_pe.n, "mm", s_mm.n, "tp", s_tp.n, "g", s_g.n, "ev", s_ev.n)
    n2b_all = OV.place("n2b_all", [P, NTT * D], BF16, 45056)
    mem_sb = OV.place("mem_sb", [P, 16 * 6 * P], BF16, 45056)
    mbs_a = OV.place("mbs_a", [P, 6 * D], BF16, 69632)
    mbs_b = OV.place("mbs_b", [P, 2 * D], BF16, 180224)
    mk_sb = OV.place("mk_sb", [P, 4 * NTT * 768], BF16, 126976)
    coef_cols = OV.place("coef_cols", [P, 128], BF16, 176128)
    v_sb = OV.place("v_sb", [P, NTT * 512], F32, 188416)
    mnT = OV.place("mnT", [P, 96], BF16, 204800)
    nnT = OV.place("nnT", [P, 96], BF16, 205056)
    memh_sb = OV.place("memh_sb", [P, NTT * 512], F32, 69632)
    sb_mkT = OV.place("sb_mkT", [P, 768], BF16, 20480)
    sb_pv = OV.place("sb_pv", [P, P], F32, 22016)
    sb_mbf = OV.place("sb_mbf", [P, P], F32, 22528)
    sb_mv = OV.place("sb_mv", [P, P], F32, 23040)
    bias_bc = OV.place("bias_bc", [P, 512], F32, 23552)
    bias_row = OV.place("bias_row", [1, 512], F32, 25600)
    dn_st = OV.place("dn_st", [P, 96], F32, 27648)
    dn_st2 = OV.place("dn_st2", [96, P], F32, 28032)
    dlt_bf = [OV.place("dlt_bf0", [P, 768], BF16, 20480),
              OV.place("dlt_bf1", [P, 768], BF16, 22016)]

    def mbs_ap(t):
        if t < 6:
            return mbs_a[:, t * D:(t + 1) * D]
        return mbs_b[:, (t - 6) * D:(t - 6 + 1) * D]

    def mk_ap(kvh):
        return mk_sb[:, kvh * NTT * 768:(kvh + 1) * NTT * 768]

    with nc.Block() as blk:
        emit_ln(blk, [x_sb2[:, t * D:(t + 1) * D] for t in range(NTT)], D, 1e-5)
        b_lna = s_a.n
        b_a = s_a.reserve(NTT)
        b_b = s_b.reserve(NTT)

        def scb(sc):
            sc.wait_ge(s_a.h, b_lna)
            for t in range(NTT):
                if t >= 1:
                    sc.wait_ge(s_a.h, b_a + t)
                sc.activation(out=scr1[:, 0:D], in_=x_sb2[:, t * D:(t + 1) * D], func=ACTF.Identity,
                              bias=bcol[:, t:t + 1], scale=rcol[:, t:t + 1]).then_inc(s_b.h, 1)
        blk.scalar(scb)

        def vb(v):
            for t in range(NTT):
                v.wait_ge(s_b.h, b_b + t + 1)
                v.drain()
                v.tensor_copy(n2b_all[:, t * D:(t + 1) * D], scr1[:, 0:D]).then_inc(s_a.h, 1)
        blk.vector(vb)

    with nc.Block() as blk:
        transpose_batch(blk, [(n2b_all[:, (t * KC + j) * P:(t * KC + j) * P + P],
                               nT[:, j * T + t * P:j * T + t * P + P])
                              for t in range(NTT) for j in range(KC)])

        def park(e):
            b = s_st.reserve(NTT * 16)
            for t in range(NTT):
                e.dma_start(out=pk_x[t * P:(t + 1) * P, :], in_=x_sb2[:, t * D:(t + 1) * D]).then_inc(s_st.h, 16)
            e.wait_ge(s_st.h, b + NTT * 16)
        blk.sync(park)

    chk("L1a")
    with nc.Block() as blk:
        b_gd = s_gd.reserve(16 * 16)
        b_ld = s_ld.reserve(16)
        b_pe = s_pe.reserve(6)
        b_a = s_a.reserve(6)

        def g1(e):
            for h in range(16):
                e.dma_start(out=mem_sb[:, h * 768:(h + 1) * 768],
                            in_=bass.AP(MEM.tensor, h * 768 * 128, [[128, P], [128 * 128, 6], [1, 128]])
                            ).then_inc(s_gd.h, 16)
            e.wait_ge(s_gd.h, b_gd + 16 * 16)
        blk.gpsimd(g1)

        def ld(e):
            e.dma_start(out=scr1[0:16, 0:768], in_=MNORM[:, :]).then_inc(s_ld.h, 16)
        blk.sync(ld)

        def pe1(te):
            te.wait_ge(s_ld.h, b_ld + 16)
            for fc in range(6):
                if fc >= 2:
                    te.wait_ge(s_a.h, b_a + fc - 1)
                te.transpose(PS[4 + fc % 2][0:P, 0:16], scr1[0:16, fc * P:(fc + 1) * P],
                             idt32f[0:16, 0:16]).then_inc(s_pe.h, 1)
        blk.tensor(pe1)

        def vb(v):
            for fc in range(6):
                v.wait_ge(s_pe.h, b_pe + fc + 1)
                v.drain()
                v.tensor_copy(mnT[:, fc * 16:(fc + 1) * 16], PS[4 + fc % 2][0:P, 0:16]).then_inc(s_a.h, 1)
        blk.vector(vb)

    def w_ap(wsrc, wcols, c0):
        return bass.AP(wsrc.tensor, c0, [[wcols, P], [P * wcols, KC], [1, 512]])

    def b_ap(bt, c0):
        return bass.AP(bt.tensor, c0, [[1, 1], [1, 512]])

    for phase, wsrc, wcols, bt, ngrp in [("mb", WMB, D, BMB, 4), ("v", WV, 512, BV, 1)]:
        for g in range(ngrp):
            with nc.Block() as blk:
                b_gd = s_gd.reserve(16)
                b_ld = s_ld.reserve(16)
                b_pe = s_pe.reserve(1)
                b_mm = s_mm.reserve(NTT)
                b_a = s_a.reserve(1 + NTT)
                b_b = s_b.reserve(NTT if phase == "mb" else 0)

                def g1(e, wsrc=wsrc, wcols=wcols, g=g):
                    e.dma_start(out=wb[0][:, 0:8192], in_=w_ap(wsrc, wcols, g * 512)).then_inc(s_gd.h, 16)
                blk.gpsimd(g1)

                def ld(e, bt=bt, g=g):
                    e.dma_start(out=bias_row[0:1, 0:512], in_=b_ap(bt, g * 512)).then_inc(s_ld.h, 16)
                blk.sync(ld)

                def pe1(te, b_gd=b_gd, b_ld=b_ld, b_a=b_a, b_pe=b_pe):
                    te.wait_ge(s_ld.h, b_ld + 16)
                    te.matmul(PS[4][:, 0:512], ones_row[0:1, 0:P], bias_row[0:1, 0:512],
                              start=True, stop=True).then_inc(s_pe.h, 1)
                    te.wait_ge(s_gd.h, b_gd + 16)
                    for t in range(NTT):
                        if t >= 4:
                            te.wait_ge(s_a.h, b_a + 1 + (t - 4) + 1)
                        mm = None
                        for kc in range(KC):
                            mm = te.matmul(PS[t % 4][:, 0:512], nT[:, kc * T + t * P:kc * T + (t + 1) * P],
                                           wb[0][:, kc * 512:(kc + 1) * 512],
                                           start=(kc == 0), stop=(kc == KC - 1))
                        mm.then_inc(s_mm.h, 1)
                blk.tensor(pe1)

                def vb(v, b_pe=b_pe, b_mm=b_mm, b_a=b_a, b_b=b_b, phase=phase):
                    v.wait_ge(s_pe.h, b_pe + 1)
                    v.tensor_copy(bias_bc[:, 0:512], PS[4][:, 0:512]).then_inc(s_a.h, 1)
                    for t in range(NTT):
                        v.wait_ge(s_mm.h, b_mm + t + 1)
                        if phase == "mb" and t >= 1:
                            v.wait_ge(s_b.h, b_b + t)
                        v.drain()
                        dst = v_sb[:, t * 512:(t + 1) * 512] if phase == "v" else scr1[:, 0:512]
                        v.tensor_tensor(out=dst, in0=PS[t % 4][:, 0:512],
                                        in1=bias_bc[:, 0:512], op=ALU.add).then_inc(s_a.h, 1)
                blk.vector(vb)

                if phase == "mb":
                    def scb(sc, b_a=b_a, b_b=b_b, g=g):
                        for t in range(NTT):
                            sc.wait_ge(s_a.h, b_a + 1 + t + 1)
                            sc.activation(out=mbs_ap(t)[:, g * 512:(g + 1) * 512], in_=scr1[:, 0:512],
                                          func=ACTF.Sigmoid).then_inc(s_b.h, 1)
                    blk.scalar(scb)

    chk("mbv")
    def dpfp_block(wsrc, bt, heads_per_group, g, tag):
        with nc.Block() as blk:
            b_gd = s_gd.reserve(16)
            b_ld = s_ld.reserve(16)
            b_a0 = s_a.reserve(1)
            NA, NB, NPE, NTP = 33, 48, 28, 24
            if tag == "q":
                NA, NB, NPE, NTP = 17, 20, 28, 24
            b_a = s_a.reserve(NA * NTT)
            b_b = s_b.reserve(NB * NTT)
            b_pe = s_pe.reserve(1 + NPE * NTT)
            b_mm = s_mm.reserve(NTT)
            b_tp = s_tp.reserve(NTP * NTT)

            def g1(e):
                e.dma_start(out=wb[0][:, 0:8192],
                            in_=w_ap(wsrc, 512 if tag == "k" else D, g * 512)).then_inc(s_gd.h, 16)
            blk.gpsimd(g1)

            def ld(e):
                e.dma_start(out=bias_row[0:1, 0:512], in_=b_ap(bt, g * 512)).then_inc(s_ld.h, 16)
            blk.sync(ld)

            def pe1(te):
                te.wait_ge(s_ld.h, b_ld + 16)
                te.matmul(PS[4][:, 0:512], ones_row[0:1, 0:P], bias_row[0:1, 0:512],
                          start=True, stop=True).then_inc(s_pe.h, 1)
                te.wait_ge(s_gd.h, b_gd + 16)
                for t in range(NTT):
                    if t >= 2:
                        te.wait_ge(s_a.h, b_a + NA * (t - 2) + 1)
                        te.wait_ge(s_a.h, b_a + NA * (t - 1))
                    mm = None
                    for kc in range(KC):
                        mm = te.matmul(PS[t % 2][:, 0:512], nT[:, kc * T + t * P:kc * T + (t + 1) * P],
                                       wb[0][:, kc * 512:(kc + 1) * 512],
                                       start=(kc == 0), stop=(kc == KC - 1))
                    mm.then_inc(s_mm.h, 1)
                    for sh in range(4):
                        te.wait_ge(s_b.h, b_b + NB * t + (12 if tag == "k" else 5) * sh + 4)
                        for fc in range(6):
                            gti = t * NTP + sh * 6 + fc
                            if gti >= 2:
                                te.wait_ge(s_tp.h, b_tp + gti - 1)
                            te.transpose(PT[fc % 2][0:P, 0:P],
                                         (mk_ap(sh)[:, t * 768 + fc * P:t * 768 + (fc + 1) * P]
                                          if tag == "k" else sb_mkT2[:, fc * P:(fc + 1) * P]),
                                         idt[:, :]).then_inc(s_pe.h, 1)
                        te.wait_ge(s_tp.h, b_tp + t * NTP + sh * 6 + 6)
                        if t == 0 and sh == 0:
                            te.wait_ge(s_a.h, b_a0 + 1)
                        if tag == "k":
                            if sh > 0:
                                te.wait_ge(s_b.h, b_b + NB * t + 12 * (sh - 1) + 11)
                            elif t >= 2:
                                te.wait_ge(s_b.h, b_b + NB * (t - 2) + 12 * 3 + 11)
                        else:
                            if sh > 0:
                                te.wait_ge(s_b.h, b_b + NB * t + 5 * (sh - 1) + 5)
                            elif t >= 2:
                                te.wait_ge(s_b.h, b_b + NB * (t - 2) + 5 * 3 + 5)
                        if sh > 0:
                            te.wait_ge(s_a.h, b_a + NA * t + 1 + (8 if tag == "k" else 4) * (sh - 1)
                                       + (3 if tag == "k" else 4))
                        last = None
                        for fc in range(6):
                            if tag == "k":
                                last = te.matmul(PS[4 + t % 2][:, 0:4],
                                                 sb_mkT[:, fc * P:(fc + 1) * P],
                                                 mnT[:, fc * 16 + sh * 4:fc * 16 + sh * 4 + 4],
                                                 start=(fc == 0), stop=(fc == 5))
                            else:
                                h = 4 * g + sh
                                last = te.matmul(PS[4 + t % 2][:, 0:1],
                                                 sb_mkT[:, fc * P:(fc + 1) * P],
                                                 nnT[:, fc * 16 + h:fc * 16 + h + 1],
                                                 start=(fc == 0), stop=(fc == 5))
                        for fc in range(6):
                            if tag == "k":
                                for hq in range(4):
                                    h = sh * 4 + hq
                                    last = te.matmul(PS[2 + t % 2][:, hq * P:(hq + 1) * P],
                                                     sb_mkT[:, fc * P:(fc + 1) * P],
                                                     mem_sb[:, (h * 6 + fc) * P:(h * 6 + fc + 1) * P],
                                                     start=(fc == 0 and hq == 0),
                                                     stop=(fc == 5 and hq == 3))
                            else:
                                h = 4 * g + sh
                                last = te.matmul(PS[2 + t % 2][:, sh * P:(sh + 1) * P],
                                                 sb_mkT[:, fc * P:(fc + 1) * P],
                                                 mem_sb[:, (h * 6 + fc) * P:(h * 6 + fc + 1) * P],
                                                 start=(fc == 0), stop=(fc == 5))
                        last.then_inc(s_pe.h, 1)
            blk.tensor(pe1)

            def vb(v):
                v.wait_ge(s_pe.h, b_pe + 1)
                v.tensor_copy(bias_bc[:, 0:512], PS[4][:, 0:512]).then_inc(s_a.h, 1)
                for t in range(NTT):
                    S0 = b_a + NA * t
                    B0 = b_b + NB * t
                    v.wait_ge(s_mm.h, b_mm + t + 1)
                    if t >= 1:
                        v.wait_ge(s_b.h, b_b + NB * (t - 1) + (12 if tag == "k" else 5) * 3 + 1)
                    v.drain()
                    v.tensor_tensor(out=scr1[:, 0:512], in0=PS[t % 2][:, 0:512],
                                    in1=bias_bc[:, 0:512], op=ALU.add).then_inc(s_a.h, 1)
                    for sh in range(4):
                        NBS = 12 if tag == "k" else 5
                        SB = B0 + NBS * sh
                        SS = S0 + 1 + (8 if tag == "k" else 4) * sh
                        v.wait_ge(s_b.h, SB + 1)
                        for j in (1, 2, 3):
                            v.drain()
                            v.tensor_tensor(out=scr1[:, 1024 + (j - 1) * 256 + j:1024 + j * 256],
                                            in0=scr1[:, 512 + j:768],
                                            in1=scr1[:, 512:768 - j], op=ALU.mult)
                            v.drain()
                            last = v.tensor_tensor(out=scr1[:, 1024 + (j - 1) * 256:1024 + (j - 1) * 256 + j],
                                                   in0=scr1[:, 512:512 + j],
                                                   in1=scr1[:, 768 - j:768], op=ALU.mult)
                        last.then_inc(s_a.h, 1)
                        v.wait_ge(s_b.h, SB + 3)
                        v.drain()
                        v.tensor_scalar(out=qcol[:, 2:3], in0=qcol[:, 1:2], scalar1=1e-12,
                                        scalar2=None, op0=ALU.max)
                        v.drain()
                        v.reciprocal(qcol[:, 3:4], qcol[:, 2:3]).then_inc(s_a.h, 1)
                        v.drain()
                        v.tensor_scalar(out=qcol[:, 4:5], in0=qcol[:, 0:1], scalar1=qcol[:, 3:4],
                                        scalar2=None, op0=ALU.mult)
                        v.drain()
                        v.tensor_scalar(out=qcol[:, 5:6], in0=qcol[:, 4:5], scalar1=qcol[:, 3:4],
                                        scalar2=None, op0=ALU.mult)
                        v.drain()
                        v.reciprocal(qcol[:, 6:7], qcol[:, 5:6])
                        for fc in range(6):
                            v.wait_ge(s_pe.h, b_pe + 1 + t * NPE + sh * 7 + fc + 1)
                            v.drain()
                            v.tensor_copy(sb_mkT[:, fc * P:(fc + 1) * P], PT[fc % 2][0:P, 0:P]).then_inc(s_tp.h, 1)
                        v.wait_ge(s_pe.h, b_pe + 1 + t * NPE + sh * 7 + 7)
                        v.drain()
                        if tag == "k":
                            v.tensor_scalar(out=colA[:, 0:4], in0=PS[4 + t % 2][:, 0:4],
                                            scalar1=1e-8, scalar2=None, op0=ALU.add)
                            v.drain()
                            v.reciprocal(colA[:, 4:8], colA[:, 0:4]).then_inc(s_a.h, 1)
                            v.drain()
                            v.tensor_scalar(out=colB[:, 0:4], in0=colA[:, 0:4], scalar1=qcol[:, 6:7],
                                            scalar2=-1.0, op0=ALU.mult, op1=ALU.mult)
                            v.drain()
                            v.tensor_scalar(out=colB[:, 4:8], in0=colB[:, 0:4], scalar1=1.0,
                                            scalar2=0.0, op0=ALU.add, op1=ALU.max)
                            v.drain()
                            v.tensor_scalar(out=coef_cols[:, t * 16 + sh * 4:t * 16 + sh * 4 + 4],
                                            in0=colB[:, 4:8], scalar1=1.0, scalar2=None,
                                            op0=ALU.min).then_inc(s_a.h, 1)
                            for hq in range(4):
                                v.wait_ge(s_b.h, SB + 5 + 2 * hq)
                                v.drain()
                                v.tensor_tensor(out=sb_mv[:, :], in0=v_sb[:, t * 512 + sh * P:t * 512 + (sh + 1) * P],
                                                in1=sb_pv[:, :], op=ALU.subtract)
                                v.wait_ge(s_b.h, SB + 6 + 2 * hq)
                                v.drain()
                                v.tensor_tensor(out=mbs_ap(t)[:, (sh * 4 + hq) * P:(sh * 4 + hq + 1) * P],
                                                in0=sb_mv[:, :], in1=sb_mbf[:, :],
                                                op=ALU.mult).then_inc(s_a.h, 1)
                        else:
                            v.tensor_scalar(out=colA[:, sh:sh + 1], in0=PS[4 + t % 2][:, 0:1],
                                            scalar1=1e-8, scalar2=None, op0=ALU.add).then_inc(s_a.h, 1)
                            v.drain()
                            v.reciprocal(colA[:, 4 + sh:5 + sh], colA[:, sh:sh + 1]).then_inc(s_a.h, 1)
            blk.vector(vb)

            def scb(sc):
                for t in range(NTT):
                    S0 = b_a + NA * t
                    B0 = b_b + NB * t
                    for sh in range(4):
                        NBS = 12 if tag == "k" else 5
                        SB = B0 + NBS * sh
                        SS = S0 + 1 + (8 if tag == "k" else 4) * sh
                        sc.wait_ge(s_a.h, S0 + 1)
                        if sh > 0:
                            sc.wait_ge(s_a.h, SS - (8 if tag == "k" else 4) + 1)
                        sc.drain()
                        sc.activation(out=scr1[:, 512:640], in_=scr1[:, sh * P:(sh + 1) * P],
                                      func=ACTF.Relu)
                        sc.drain()
                        sc.activation(out=scr1[:, 640:768], in_=scr1[:, sh * P:(sh + 1) * P],
                                      func=ACTF.Relu, scale=-1.0).then_inc(s_b.h, 1)
                        sc.wait_ge(s_a.h, SS + 1)
                        sc.drain()
                        sc.activation(out=junk[:, 0:768], in_=scr1[:, 1024:1792], func=ACTF.Square,
                                      accum_out=qcol[:, 0:1]).then_inc(s_b.h, 1)
                        sc.drain()
                        sc.activation(out=qcol[:, 1:2], in_=qcol[:, 0:1], func=ACTF.Sqrt).then_inc(s_b.h, 1)
                        sc.wait_ge(s_a.h, SS + 2)
                        if tag == "q" and (sh > 0 or t > 0):
                            pidx = (t * 28 + 7 * (sh - 1) + 6) if sh > 0 else ((t - 1) * 28 + 27)
                            sc.wait_ge(s_pe.h, b_pe + 1 + pidx)
                        sc.drain()
                        sc.activation(out=(mk_ap(sh)[:, t * 768:(t + 1) * 768] if tag == "k"
                                           else sb_mkT2[:, 0:768]),
                                      in_=scr1[:, 1024:1792], func=ACTF.Identity,
                                      scale=qcol[:, 3:4]).then_inc(s_b.h, 1)
                        if tag == "k":
                            sc.wait_ge(s_a.h, SS + 3)
                            for hq in range(4):
                                sc.drain()
                                sc.activation(out=sb_pv[:, :], in_=PS[2 + t % 2][:, hq * P:(hq + 1) * P],
                                              func=ACTF.Identity, scale=colA[:, 4 + hq:5 + hq]).then_inc(s_b.h, 1)
                                sc.drain()
                                sc.activation(out=sb_mbf[:, :],
                                              in_=mbs_ap(t)[:, (sh * 4 + hq) * P:(sh * 4 + hq + 1) * P],
                                              func=ACTF.Identity).then_inc(s_b.h, 1)
                                sc.wait_ge(s_a.h, SS + 5 + hq)
                        else:
                            sc.wait_ge(s_a.h, SS + 4)
                            sc.drain()
                            sc.activation(out=memh_sb[:, t * 512 + sh * P:t * 512 + (sh + 1) * P],
                                          in_=PS[2 + t % 2][:, sh * P:(sh + 1) * P],
                                          func=ACTF.Identity, scale=colA[:, 4 + sh:5 + sh]).then_inc(s_b.h, 1)
            blk.scalar(scb)
        return

    sb_mkT2 = OV.place("sb_mkT2", [P, 768], BF16, 26112)
    dpfp_block(WK, BK, 4, 0, "k")

    if dbg_dump("l1k", [(mk_sb[:, 0:2048], 0, 2048), (mbs_a[:, 0:2048], 130, 2048),
                        (coef_cols[:, 0:128], 260, 128), (v_sb[:, 0:2048], 390, 2048)]):
        return nc

    chk("k")
    with nc.Block() as blk:
        b_mm = s_mm.reserve(96 + 2)
        b_ev = s_ev.reserve(96 + 2)
        b_gd = s_gd.reserve(16)

        def pe1(te):
            i = 0
            for h in range(16):
                kvh = h // 4
                for fc in range(6):
                    if i >= 2:
                        te.wait_ge(s_ev.h, b_ev + i - 1)
                    last = None
                    for tt in range(NTT):
                        last = te.matmul(PS[i % 2][:, 0:P],
                                         mk_ap(kvh)[:, tt * 768 + fc * P:tt * 768 + (fc + 1) * P],
                                         mbs_ap(tt)[:, h * P:(h + 1) * P],
                                         start=(tt == 0), stop=(tt == NTT - 1))
                    last.then_inc(s_mm.h, 1)
                    i += 1
            last = None
            for tt in range(NTT):
                for h in range(16):
                    kvh = h // 4
                    for fc in range(6):
                        last = te.matmul(PS[2][:, h * 6 + fc:h * 6 + fc + 1],
                                         mk_ap(kvh)[:, tt * 768 + fc * P:tt * 768 + (fc + 1) * P],
                                         coef_cols[:, tt * 16 + h:tt * 16 + h + 1],
                                         start=(tt == 0 and h == 0 and fc == 0),
                                         stop=(tt == NTT - 1 and h == 15 and fc == 5))
            last.then_inc(s_mm.h, 1)
            te.wait_ge(s_ev.h, b_ev + 97)
            te.transpose(PS[3][0:96, 0:P], dn_st[:, 0:96], idt32f[:, :]).then_inc(s_mm.h, 1)
        blk.tensor(pe1)

        b_sp4 = [s_ops[2].reserve(48 * 16), s_ops[3].reserve(48 * 16)]

        def vb(v):
            i = 0
            for h in range(16):
                for fc in range(6):
                    v.wait_ge(s_mm.h, b_mm + i + 1)
                    if i >= 2:
                        v.wait_ge(s_ops[2 + i % 2].h, b_sp4[i % 2] + (i // 2) * 16)
                    v.drain()
                    v.tensor_copy(scr2[:, (i % 2) * 1024:(i % 2) * 1024 + P], PS[i % 2][:, 0:P]).then_inc(s_ev.h, 1)
                    i += 1
            v.wait_ge(s_mm.h, b_mm + 97)
            v.drain()
            v.tensor_copy(dn_st[:, 0:96], PS[2][:, 0:96]).then_inc(s_ev.h, 1)
            v.wait_ge(s_mm.h, b_mm + 98)
            v.drain()
            v.tensor_copy(dn_st2[0:96, :], PS[3][0:96, 0:P]).then_inc(s_ev.h, 1)
        blk.vector(vb)

        def g1(e):
            i = 0
            for h in range(16):
                for fc in range(6):
                    e.wait_ge(s_ev.h, b_ev + i + 1)
                    e.dma_start(out=bass.AP(dm_in, h * 98304 + fc * 128 * 128, [[128, P], [1, P]]),
                                in_=scr2[:, (i % 2) * 1024:(i % 2) * 1024 + P]).then_inc(s_ops[2 + i % 2].h, 16)
                    i += 1
            e.wait_ge(s_ev.h, b_ev + 98)
            e.dma_start(out=bass.AP(dm_in, 16 * 98304, [[P, 96], [1, P]]),
                        in_=dn_st2[0:96, :]).then_inc(s_gd.h, 16)
            e.wait_ge(s_gd.h, b_gd + 16)
            e.wait_ge(s_ops[2].h, b_sp4[0] + 48 * 16)
            e.wait_ge(s_ops[3].h, b_sp4[1] + 48 * 16)
        blk.gpsimd(g1)

    chk("c4")
    with nc.Block() as blk:
        b_cc = s_cc.reserve(1)
        b_gd = s_gd.reserve(16 * 16)
        b_ld = s_ld.reserve(2 * 16)
        b_a = s_a.reserve(16 + 1)
        b_pe = s_pe.reserve(6)
        b_tp = s_tp.reserve(6)

        def g1(e):
            e.collective_compute("AllReduce", ALU.add, replica_groups=[list(range(8))],
                                 ins=[dm_in.ap()], outs=[dm_out.ap()]).then_inc(s_cc.h, 1)
            e.wait_ge(s_cc.h, b_cc + 1)
            for h in range(16):
                if h >= 2:
                    e.wait_ge(s_a.h, b_a + h - 1)
                e.dma_start(out=dlt_bf[h % 2][:, :],
                            in_=bass.AP(dm_out, h * 98304, [[128, P], [128 * 128, 6], [1, 128]])
                            ).then_inc(s_gd.h, 16)
                e.wait_ge(s_gd.h, b_gd + (h + 1) * 16)
        blk.gpsimd(g1)

        def vb(v):
            for h in range(16):
                v.wait_ge(s_gd.h, b_gd + (h + 1) * 16)
                v.drain()
                v.tensor_tensor(out=mem_sb[:, h * 768:(h + 1) * 768],
                                in0=mem_sb[:, h * 768:(h + 1) * 768],
                                in1=dlt_bf[h % 2][:, :], op=ALU.add).then_inc(s_a.h, 1)
            v.wait_ge(s_ld.h, b_ld + 32)
            v.drain()
            v.tensor_add(scr1[0:16, 0:768], scr1[0:16, 0:768], scr1[0:16, 1024:1792]).then_inc(s_a.h, 1)
        blk.vector(vb)

        def ld(e):
            e.dma_start(out=scr1[0:16, 0:768], in_=MNORM[:, :]).then_inc(s_ld.h, 16)
            e.wait_ge(s_ld.h, b_ld + 16)
            e.wait_ge(s_cc.h, b_cc + 1)
            e.dma_start(out=scr1[0:16, 1024:1792],
                        in_=bass.AP(dm_out, 16 * 98304, [[768, 16], [1, 768]])).then_inc(s_ld.h, 16)
            e.wait_ge(s_ld.h, b_ld + 32)
        blk.sync(ld)

        def pe1(te):
            te.wait_ge(s_a.h, b_a + 17)
            for fc in range(6):
                if fc >= 2:
                    te.wait_ge(s_tp.h, b_tp + fc - 1)
                te.transpose(PS[4 + fc % 2][0:P, 0:16], scr1[0:16, fc * P:(fc + 1) * P],
                             idt32f[0:16, 0:16]).then_inc(s_pe.h, 1)
        blk.tensor(pe1)

        def vb2(v):
            for fc in range(6):
                v.wait_ge(s_pe.h, b_pe + fc + 1)
                v.drain()
                v.tensor_copy(nnT[:, fc * 16:(fc + 1) * 16], PS[4 + fc % 2][0:P, 0:16]).then_inc(s_tp.h, 1)
        blk.vector(vb2)

    if dbg_dump("l1m", [(mem_sb[:, 0:2048], 0, 2048), (nnT[:, 0:96], 130, 96)]):
        return nc

    chk("c5")
    xacc = x_sb2
    with nc.Block() as blk:
        b_ld = s_ld.reserve(NTT * 16)
        b_a = s_a.reserve(NTT)
        b_b = s_b.reserve(NTT)

        def ld(e):
            for t in range(NTT):
                if t >= 2:
                    e.wait_ge(s_a.h, b_a + t - 1)
                e.dma_start(out=xacc[:, t * D:(t + 1) * D], in_=pk_x[t * P:(t + 1) * P, :]).then_inc(s_ld.h, 16)
                e.wait_ge(s_ld.h, b_ld + (t + 1) * 16)
        blk.sync(ld)

        def scb(sc):
            for t in range(NTT):
                sc.wait_ge(s_ld.h, b_ld + (t + 1) * 16)
                if t >= 1:
                    sc.wait_ge(s_a.h, b_a + t)
                sc.drain()
                sc.activation(out=scr2[:, 0:D], in_=xacc[:, t * D:(t + 1) * D], func=ACTF.Identity,
                              bias=bcol[:, t:t + 1], scale=rcol[:, t:t + 1]).then_inc(s_b.h, 1)
        blk.scalar(scb)

        def vb(v):
            for t in range(NTT):
                v.wait_ge(s_b.h, b_b + t + 1)
                v.drain()
                v.tensor_add(xacc[:, t * D:(t + 1) * D], xacc[:, t * D:(t + 1) * D],
                             scr2[:, 0:D]).then_inc(s_a.h, 1)
        blk.vector(vb)

    chk("b6pre")
    for g in range(4):
        dpfp_block(WQ, BQ, 4, g, "q")
        chk(f"q{g}")
        with nc.Block() as blk:
            b_gd = s_gd.reserve(16)
            b_ld = s_ld.reserve(16)
            b_pe = s_pe.reserve(1)
            b_mm = s_mm.reserve(NTT)
            b_a = s_a.reserve(1 + 2 * NTT)
            b_b = s_b.reserve(NTT)

            def g1(e, g=g):
                e.dma_start(out=wb[0][:, 0:8192], in_=w_ap(WG, D, g * 512)).then_inc(s_gd.h, 16)
            blk.gpsimd(g1)

            def ld(e, g=g):
                e.dma_start(out=bias_row[0:1, 0:512], in_=b_ap(BG, g * 512)).then_inc(s_ld.h, 16)
            blk.sync(ld)

            def pe1(te, b_gd=b_gd, b_ld=b_ld, b_a=b_a, b_pe=b_pe):
                te.wait_ge(s_ld.h, b_ld + 16)
                te.matmul(PS[4][:, 0:512], ones_row[0:1, 0:P], bias_row[0:1, 0:512],
                          start=True, stop=True).then_inc(s_pe.h, 1)
                te.wait_ge(s_gd.h, b_gd + 16)
                for t in range(NTT):
                    if t >= 4:
                        te.wait_ge(s_a.h, b_a + 1 + 2 * (t - 4) + 1)
                    mm = None
                    for kc in range(KC):
                        mm = te.matmul(PS[t % 4][:, 0:512], nT[:, kc * T + t * P:kc * T + (t + 1) * P],
                                       wb[0][:, kc * 512:(kc + 1) * 512],
                                       start=(kc == 0), stop=(kc == KC - 1))
                    mm.then_inc(s_mm.h, 1)
            blk.tensor(pe1)

            def vb(v, b_pe=b_pe, b_mm=b_mm, b_a=b_a, b_b=b_b, g=g):
                v.wait_ge(s_pe.h, b_pe + 1)
                v.tensor_copy(bias_bc[:, 0:512], PS[4][:, 0:512]).then_inc(s_a.h, 1)
                for t in range(NTT):
                    v.wait_ge(s_mm.h, b_mm + t + 1)
                    if t >= 1:
                        v.wait_ge(s_b.h, b_b + t)
                    v.drain()
                    v.tensor_tensor(out=scr1[:, 0:512], in0=PS[t % 4][:, 0:512],
                                    in1=bias_bc[:, 0:512], op=ALU.add).then_inc(s_a.h, 1)
                    v.wait_ge(s_b.h, b_b + t + 1)
                    v.drain()
                    v.tensor_tensor(out=scr1[:, 512:1024], in0=scr1[:, 512:1024],
                                    in1=memh_sb[:, t * 512:(t + 1) * 512], op=ALU.mult)
                    v.drain()
                    v.tensor_add(xacc[:, t * D + g * 512:t * D + (g + 1) * 512],
                                 xacc[:, t * D + g * 512:t * D + (g + 1) * 512],
                                 scr1[:, 512:1024]).then_inc(s_a.h, 1)
            blk.vector(vb)

            def scb(sc, b_a=b_a, b_b=b_b):
                for t in range(NTT):
                    sc.wait_ge(s_a.h, b_a + 1 + 2 * t + 1)
                    sc.drain()
                    sc.activation(out=scr1[:, 512:1024], in_=scr1[:, 0:512],
                                  func=ACTF.Sigmoid).then_inc(s_b.h, 1)
            blk.scalar(scb)

    chk("Gdone")
    if DEBUG == "l1x" or STOP == "M3":
        with nc.Block() as blk:
            def _(e):
                b = s_st.reserve(NTT * 16)
                for t in range(NTT):
                    e.wait_ge(s_a.h, s_a.n - 2 * NTT + 2 * t + 2)
                    e.dma_start(out=OUT[t * P:(t + 1) * P, :], in_=xacc[:, t * D:(t + 1) * D]).then_inc(s_st.h, 16)
                e.wait_ge(s_st.h, b + NTT * 16)
            blk.sync(_)
        return nc

    n3b_all = OV.place("n3b_all", [P, NTT * D], BF16, 45056)
    gfm_a = OV.place("gfm_a", [P, 24 * T], BF16, 45056)
    gfm_b = OV.place("gfm_b", [P, 24 * T], BF16, 126976)

    def g_ap(hc):
        if hc < 24:
            return gfm_a[:, hc * T:(hc + 1) * T]
        return gfm_b[:, (hc - 24) * T:(hc - 24 + 1) * T]

    with nc.Block() as blk:
        emit_ln(blk, [xacc[:, t * D:(t + 1) * D] for t in range(NTT)], D, 1e-5)
        b_lna = s_a.n
        b_a = s_a.reserve(NTT)
        b_b = s_b.reserve(NTT)

        def scb(sc):
            sc.wait_ge(s_a.h, b_lna)
            for t in range(NTT):
                if t >= 1:
                    sc.wait_ge(s_a.h, b_a + t)
                sc.activation(out=scr1[:, 0:D], in_=xacc[:, t * D:(t + 1) * D], func=ACTF.Identity,
                              bias=bcol[:, t:t + 1], scale=rcol[:, t:t + 1]).then_inc(s_b.h, 1)
        blk.scalar(scb)

        def vb(v):
            v.tensor_scalar(out=colD[:, 15:16], in0=qsc[:, 3:4], scalar1=1.0,
                            scalar2=qsc[:, 4:5], op0=ALU.mult, op1=ALU.mult)
            for t in range(NTT):
                v.wait_ge(s_b.h, b_b + t + 1)
                v.drain()
                v.tensor_copy(n3b_all[:, t * D:(t + 1) * D], scr1[:, 0:D]).then_inc(s_a.h, 1)
        blk.vector(vb)

    with nc.Block() as blk:
        transpose_batch(blk, [(n3b_all[:, (t * KC + j) * P:(t * KC + j) * P + P],
                               nT[:, j * T + t * P:j * T + t * P + P])
                              for t in range(NTT) for j in range(KC)])

        def park(e):
            b = s_st.reserve(NTT * 16)
            for t in range(NTT):
                e.dma_start(out=pk_x[t * P:(t + 1) * P, :], in_=xacc[:, t * D:(t + 1) * D]).then_inc(s_st.h, 16)
            e.wait_ge(s_st.h, b + NTT * 16)
        blk.sync(park)

    sl_a = [scr1[:, 0:512], scr1[:, 512:1024]]
    with nc.Block() as blk:
        slot_sems = [s_z0, s_z1, s_x0, s_x1]
        b_z = [s.reserve(24 * 16) for s in slot_sems]
        NGH = 96
        b_mm = s_mm.reserve(2 * NGH)
        b_a = s_a.reserve(NGH)
        b_b = s_b.reserve(NGH)

        def wslot2(s):
            return wb[0][:, (s % 4) * D:(s % 4) * D + D]

        def dmab(e):
            for hc in range(48):
                for j, nm in ((0, "w1"), (1, "w3")):
                    sidx = 2 * hc + j
                    if hc >= 2:
                        e.wait_ge(s_mm.h, b_mm + 2 * (2 * (hc - 2) + 1) + j + 1)
                    e.dma_start(out=wslot2(sidx), in_=qap_w13(nm, hc)).then_inc(
                        slot_sems[sidx % 4].h, 16)
        blk.sync(dmab)

        def peb(te):
            for i in range(NGH):
                hc, tg = i // 2, i % 2
                par = i % 2
                for j in range(2):
                    psum = PS[2 * par + j]
                    sidx = 2 * hc + j
                    te.wait_ge(slot_sems[sidx % 4].h, b_z[sidx % 4] + 16 * (sidx // 4 + 1))
                    if i >= 2:
                        if j == 0:
                            te.wait_ge(s_b.h, b_b + (i - 2) + 1)
                        else:
                            te.wait_ge(s_a.h, b_a + (i - 2) + 1)
                    mm = None
                    for kc in range(KC):
                        mm = te.matmul(psum[:, 0:512], wslot2(sidx)[:, kc * P:(kc + 1) * P],
                                       nT[:, kc * T + tg * 512:kc * T + tg * 512 + 512],
                                       start=(kc == 0), stop=(kc == KC - 1))
                    mm.then_inc(s_mm.h, 1)
        blk.tensor(peb)

        def scb(sc):
            for i in range(NGH):
                par = i % 2
                sc.wait_ge(s_mm.h, b_mm + 2 * i + 1)
                if i >= 2:
                    sc.wait_ge(s_a.h, b_a + (i - 2) + 1)
                sc.drain()
                sc.activation(out=sl_a[par], in_=PS[2 * par][:, 0:512], func=ACTF.Silu,
                              scale=qsc[:, 2:3]).then_inc(s_b.h, 1)
        blk.scalar(scb)

        def vb(v):
            for i in range(NGH):
                hc, tg = i // 2, i % 2
                par = i % 2
                v.wait_ge(s_mm.h, b_mm + 2 * i + 2)
                v.wait_ge(s_b.h, b_b + i + 1)
                v.drain()
                v.tensor_tensor(out=g_ap(hc)[:, tg * 512:tg * 512 + 512], in0=sl_a[par],
                                in1=PS[2 * par + 1][:, 0:512], op=ALU.mult).then_inc(s_a.h, 1)
        blk.vector(vb)

    with nc.Block() as blk:
        ostg = [scr1[:, 0:D], scr2[:, 0:D]]
        b_z = [s_z0.reserve(32 * 16), s_z1.reserve(32 * 16)]
        b_mm = s_mm.reserve(8 * 48)
        b_b = s_b.reserve(8 * 4)
        b_st = s_st.reserve(8 * 16)

        def dmab(e):
            for gi in range(8):
                th, nc4 = gi // 4, gi % 4
                for bq in range(8):
                    B = gi * 8 + bq
                    if B >= 2:
                        e.wait_ge(s_mm.h, b_mm + 6 * (B - 1))
                    e.dma_start(out=wb[0][:, (B % 2) * 3072:(B % 2) * 3072 + 3072],
                                in_=bass.AP(qag_out, bq * QSL + q_off["w2"] + nc4 * 512,
                                            [[2048, P], [P * 2048, 6], [1, 512]])
                                ).then_inc([s_z0, s_z1][B % 2].h, 16)
                if gi >= 2:
                    e.wait_ge(s_b.h, b_b + 4 * (gi - 1))
                    e.dma_start(out=bass.AP(pk_op2.tensor, (gi - 2) // 4 * 512 * D + ((gi - 2) % 4) * 512,
                                            [[D, P], [P * D, 4], [1, 512]]),
                                in_=ostg[gi % 2]).then_inc(s_st.h, 16)
                    e.wait_ge(s_st.h, b_st + 16 * (gi - 1))
            for gi in (6, 7):
                e.wait_ge(s_b.h, b_b + 4 * (gi + 1))
                e.dma_start(out=bass.AP(pk_op2.tensor, gi // 4 * 512 * D + (gi % 4) * 512,
                                        [[D, P], [P * D, 4], [1, 512]]),
                            in_=ostg[gi % 2]).then_inc(s_st.h, 16)
                e.wait_ge(s_st.h, b_st + 16 * (gi + 1))
        blk.sync(dmab)

        def peb(te):
            for gi in range(8):
                th, nc4 = gi // 4, gi % 4
                if gi >= 1:
                    te.wait_ge(s_b.h, b_b + 4 * gi)
                for kc in range(48):
                    B = gi * 8 + kc // 6
                    if kc % 6 == 0:
                        te.wait_ge([s_z0, s_z1][B % 2].h, b_z[B % 2] + 16 * (B // 2 + 1))
                    mm = None
                    for t4 in range(4):
                        mm = te.matmul(PS[t4][:, 0:512],
                                       g_ap(kc)[:, (th * 4 + t4) * P:(th * 4 + t4 + 1) * P],
                                       wb[0][:, (B % 2) * 3072 + (kc % 6) * 512:(B % 2) * 3072 + (kc % 6 + 1) * 512],
                                       start=(kc == 0), stop=(kc == 47))
                    mm.then_inc(s_mm.h, 1)
        blk.tensor(peb)

        def scb(sc):
            for gi in range(8):
                sc.wait_ge(s_mm.h, b_mm + 48 * (gi + 1))
                if gi >= 2:
                    sc.wait_ge(s_st.h, b_st + 16 * (gi - 1))
                for t4 in range(4):
                    sc.drain()
                    sc.activation(out=ostg[gi % 2][:, t4 * 512:(t4 + 1) * 512],
                                  in_=PS[t4][:, 0:512], func=ACTF.Identity,
                                  scale=colD[:, 15:16]).then_inc(s_b.h, 1)
        blk.scalar(scb)

    with nc.Block() as blk:
        b_ld = s_ld.reserve(8 * 2 * 16)
        b_a = s_a.reserve(8)
        b_st = s_st.reserve(8 * 16)

        def ld(e):
            for t in range(NTT):
                sl = t % 2
                if t >= 2:
                    e.wait_ge(s_a.h, b_a + t - 1)
                e.dma_start(out=xacc[:, t * D:(t + 1) * D], in_=pk_x[t * P:(t + 1) * P, :]).then_inc(s_ld.h, 16)
                e.dma_start(out=cmb[sl][:, 0:D], in_=pk_op2[t * P:(t + 1) * P, :]).then_inc(s_ld.h, 16)
                e.wait_ge(s_ld.h, b_ld + (t + 1) * 32)
            for t in range(NTT):
                e.wait_ge(s_a.h, b_a + t + 1)
                e.dma_start(out=OUT[t * P:(t + 1) * P, :], in_=xacc[:, t * D:(t + 1) * D]).then_inc(s_st.h, 16)
            e.wait_ge(s_st.h, b_st + 8 * 16)
        blk.sync(ld)

        def vb(v):
            for t in range(NTT):
                sl = t % 2
                v.wait_ge(s_ld.h, b_ld + (t + 1) * 32)
                v.drain()
                v.tensor_add(xacc[:, t * D:(t + 1) * D], xacc[:, t * D:(t + 1) * D],
                             cmb[sl][:, 0:D]).then_inc(s_a.h, 1)
        blk.vector(vb)

    return nc

    return nc



def _host_inputs(i, I):
    b, h2 = i // 2, i % 2
    x = np.ascontiguousarray(I["L0_attn_u"].reshape(4, 2048, 2048)[b])
    X = np.ascontiguousarray(x[h2 * T:(h2 + 1) * T])
    XH = np.zeros((3, D), np.float32) if h2 == 0 else np.ascontiguousarray(x[T - 3:T])
    Wip = np.zeros((D, 8448), np.float32)
    Wip[:, 0:8384] = I["L0_attn_in_proj_w"]
    Wop = np.ascontiguousarray(I["L0_attn_out_proj_w"][h2 * 2048:(h2 + 1) * 2048, :])
    heads = slice(h2 * 32, (h2 + 1) * 32)
    d = dict(
        X=X, XH=XH, Wip=Wip, Wop=Wop,
        CONVW=np.ascontiguousarray(I["L0_attn_conv_weight"].T), CONVB=I["L0_attn_conv_bias"],
        ALOG=I["L0_attn_A_log"], DTB=I["L0_attn_dt_bias"],
        DPAR=np.ascontiguousarray(I["L0_attn_D_param"][heads]),
        NSC=np.ascontiguousarray(I["L0_attn_norm_scale"][h2 * 2048:(h2 + 1) * 2048]),
        WNORM=I["L0_ffn_w_norm"],
        WQ=np.ascontiguousarray(I["L1_attn_Wq"].T), WK=np.ascontiguousarray(I["L1_attn_Wk"].T),
        WV=np.ascontiguousarray(I["L1_attn_Wv"].T), WG=np.ascontiguousarray(I["L1_attn_Wg"].T),
        WMB=np.ascontiguousarray(I["L1_attn_Wmb"].T),
        BQ=I["L1_attn_bq"], BK=I["L1_attn_bk"], BV=I["L1_attn_bv"],
        BG=I["L1_attn_bg"], BMB=I["L1_attn_bmb"],
        MEM=I["L1_attn_memory"], MNORM=I["L1_attn_norm"],
        IDT=np.eye(P, dtype=np.float32).astype(ml_dtypes.bfloat16),
        TRIL=np.triu(np.ones((64, 64), np.float32)),
        IDT32=np.eye(P, dtype=np.float32),
    )
    gvT = np.ascontiguousarray(np.asarray(I["L0_ffn_w_gv"], np.float32).T)
    woT = np.ascontiguousarray(np.asarray(I["L0_ffn_w_out"], np.float32).T)
    w1T = np.zeros((2048, 6144), np.float32); w1T[:, 0:5461] = np.asarray(I["L1_ffn_w1"], np.float32).T
    w3T = np.zeros((2048, 6144), np.float32); w3T[:, 0:5461] = np.asarray(I["L1_ffn_w3"], np.float32).T
    w2T = np.zeros((6144, 2048), np.float32); w2T[0:5461, :] = np.asarray(I["L1_ffn_w2"], np.float32).T
    d["QGV"] = np.ascontiguousarray(gvT[:, i * 2048:(i + 1) * 2048])
    d["QWO"] = np.ascontiguousarray(woT[i * 1024:(i + 1) * 1024])
    d["QW1"] = np.concatenate([w1T[:, (i + 8 * j) * 128:(i + 8 * j) * 128 + 128] for j in range(6)], 1)
    d["QW3"] = np.concatenate([w3T[:, (i + 8 * j) * 128:(i + 8 * j) * 128 + 128] for j in range(6)], 1)
    d["QW2"] = np.ascontiguousarray(w2T[i * 768:(i + 1) * 768])
    return d


def kernel(**inputs):
    nc = build()
    in_maps = [_host_inputs(i, inputs) for i in range(8)]
    res = run_bass_kernel_spmd(nc, in_maps, core_ids=list(range(8)))
    out = np.zeros((4, 2048, 2048), np.float32)
    for i in range(8):
        b, h2 = i // 2, i % 2
        out[b, h2 * T:(h2 + 1) * T] = res.results[i]["OUT"]
    return out.reshape(4, 2048, 2048)


# version 04 (final)
# speedup vs baseline: 1.0422x; 1.0422x over previous
import numpy as np
import ml_dtypes
import concourse.bass as bass
import concourse.mybir as mybir
from concourse.bass_utils import run_bass_kernel_spmd

F32 = mybir.dt.float32
BF16 = mybir.dt.bfloat16
U8 = mybir.dt.uint8
AX = mybir.AxisListType
ALU = mybir.AluOpType
ACTF = mybir.ActivationFunctionType

P = 128
NTT = 8
T = 1024
D = 2048
KC = D // P
MAGIC = 3 * 2.0**22
NH = 32
NCH = 32
DEBUG = None
STOP = None


class Sem:
    def __init__(self, nc, name):
        self.h = nc.alloc_semaphore(name)
        self.n = 0

    def reserve(self, k):
        b = self.n
        self.n += k
        return b


class Overlay:
    def __init__(self, nc, nbytes):
        self.nc, self.nbytes = nc, nbytes
        self.arena = nc.alloc_sbuf_tensor("ARENA", [P, nbytes], U8)
        self.base = nc.lookup_mloc(self.arena).addr
        self.count = 0

    def place(self, name, shape, dtype, off):
        per_part = int(np.prod(shape[1:])) * mybir.dt.size(dtype)
        assert off % 32 == 0 and off + per_part <= self.nbytes, (name, off, per_part)
        self.count += 1
        return self.nc.alloc_sbuf_tensor_at(f"{name}_{self.count}", list(shape), dtype,
                                            offset=self.base + off)


def build():
    nc = bass.Bass("TRN2", target_bir_lowering=False)

    def din(name, shape, dtype=F32):
        return nc.dram_tensor(name, shape, dtype, kind="ExternalInput").ap()

    X = din("X", [T, D]); XH = din("XH", [3, D])
    Wip = din("Wip", [D, 8448]); Wop = din("Wop", [D, D])
    CONVW = din("CONVW", [4096, 4]); CONVB = din("CONVB", [4096])
    ALOG = din("ALOG", [64]); DTB = din("DTB", [64]); DPAR = din("DPAR", [NH])
    NSC = din("NSC", [D]); WNORM = din("WNORM", [D])
    QGV = din("QGV", [2048, 2048]); QWO = din("QWO", [1024, 2048])
    QW1 = din("QW1", [2048, 768]); QW3 = din("QW3", [2048, 768]); QW2 = din("QW2", [768, 2048])
    WQ = din("WQ", [D, D]); WK = din("WK", [D, 512]); WV = din("WV", [D, 512])
    WG = din("WG", [D, D]); WMB = din("WMB", [D, D])
    BQ = din("BQ", [D]); BK = din("BK", [512]); BV = din("BV", [512])
    BG = din("BG", [D]); BMB = din("BMB", [D])
    MEM = din("MEM", [16, 768, 128]); MNORM = din("MNORM", [16, 768])
    IDT = din("IDT", [P, P], BF16); TRIL = din("TRIL", [64, 64])

    OUT = nc.dram_tensor("OUT", [T, D], F32, kind="ExternalOutput").ap()
    DBG = nc.dram_tensor("DBG", [4160, 4160], F32, kind="ExternalOutput").ap() if DEBUG else None

    pk_x = nc.dram_tensor("pk_x", [T, D], F32).ap()
    pk_z = nc.dram_tensor("pk_z", [T, 4096], BF16).ap()
    pk_xcv = nc.dram_tensor("pk_xcv", [4096, T], BF16).ap()
    pk_xc = nc.dram_tensor("pk_xc", [D, 2048], BF16).ap()
    pk_y = nc.dram_tensor("pk_y", [2048, D], BF16).ap()
    sh_z = nc.dram_tensor("sh_z", [2048, 4096], BF16, addr_space="Shared").ap()
    sh_xc = nc.dram_tensor("sh_xc", [4096, 2048], BF16, addr_space="Shared").ap()
    sh_qk = nc.dram_tensor("sh_qk", [P, 2048], F32, addr_space="Shared").ap()
    sh_la = nc.dram_tensor("sh_la", [64, 2048], F32, addr_space="Shared").ap()
    sh_rms = nc.dram_tensor("sh_rms", [2, 2048], F32, addr_space="Shared").ap()
    sh_op = nc.dram_tensor("sh_op", [2, T, D], F32, addr_space="Shared").ap()
    bar_io = nc.dram_tensor("bar_io", [2, 8], F32).ap()
    sc_in = nc.dram_tensor("sc_in", [1, 8], F32).ap()
    sc_out = nc.dram_tensor("sc_out", [1, 8], F32).ap()
    QSL = 2048 * 2048 + 1024 * 2048 + 2 * 2048 * 768 + 768 * 2048
    qag_in = nc.dram_tensor("qag_in", [1, QSL], BF16)
    qag_out = nc.dram_tensor("qag_out", [8, QSL], BF16, addr_space="Shared")
    DM = 16 * 768 * 128 + 16 * 768
    dm_in = nc.dram_tensor("dm_in", [1, DM], F32)
    dm_out = nc.dram_tensor("dm_out", [1, DM], F32, addr_space="Shared")

    q_off, o = {}, 0
    for nm, nel in [("gv", 2048 * 2048), ("wo", 1024 * 2048), ("w1", 2048 * 768),
                    ("w3", 2048 * 768), ("w2", 768 * 2048)]:
        q_off[nm] = o; o += nel

    def qap_gv(hc):
        rank, c0 = hc // 16, (hc % 16) * P
        return bass.AP(qag_out, rank * QSL + q_off["gv"] + c0,
                       [[2048, P], [P * 2048, KC], [1, P]])

    def qap_w13(nm, hc):
        rank, j = hc % 8, hc // 8
        return bass.AP(qag_out, rank * QSL + q_off[nm] + j * P,
                       [[768, P], [P * 768, KC], [1, P]])

    def qap_wo(dmc, r):
        return bass.AP(qag_out, r * QSL + q_off["wo"] + dmc * P,
                       [[2048, P], [P * 2048, 8], [1, P]])

    def qap_w2(kc, n0, ncols):
        rank, loc = kc // 6, kc % 6
        return bass.AP(qag_out, rank * QSL + q_off["w2"] + loc * P * 2048 + n0,
                       [[2048, P], [1, ncols]])

    OV = Overlay(nc, 212832)
    ones_row = OV.place("ones", [1, 512], F32, 0)
    idt = OV.place("idt", [P, P], BF16, 2048)
    trilm = OV.place("trilm", [64, 64], F32, 2304)
    convw = OV.place("convw", [P, 128], F32, 2560)
    convb = OV.place("convb", [P, 32], F32, 3072)
    cA = OV.place("cA", [64, 1], F32, 3200)
    cDTB = OV.place("cDTB", [64, 1], F32, 3232)
    cDP = OV.place("cDP", [P, NH], F32, 3264)
    nsc_sb = OV.place("nsc", [P, 16], F32, 3424)
    qsc = OV.place("qsc", [P, 8], F32, 3488)
    colA = OV.place("colA", [P, 16], F32, 3552)
    colB = OV.place("colB", [P, 16], F32, 3616)
    colC = OV.place("colC", [P, 16], F32, 3680)
    colD = OV.place("colD", [P, 16], F32, 3744)
    mcol = OV.place("mcol", [P, 12], F32, 3808)
    qcol = OV.place("qcol", [P, 12], F32, 3872)
    rcol = OV.place("rcol", [P, 12], F32, 3936)
    bcol = OV.place("bcol", [P, 12], F32, 4000)
    cst = OV.place("cst", [P, 8], F32, 4064)
    nc.const_aps.register(F32, 1e-5, cst[:, 0:1])
    nc.const_aps.register(F32, 1e-6, cst[:, 1:2])
    wnorm_bc = OV.place("wnorm_bc", [P, D], F32, 4096)
    scr1 = OV.place("scr1", [P, 2048], F32, 12288)
    scr2 = OV.place("scr2", [P, 2048], F32, 20480)
    junk = scr2
    WB0 = 28672
    wb = [OV.place(f"wb{i}", [P, 8192], BF16, WB0 + i * 16384) for i in range(2)]
    ws = OV.place("ws", [P, 8192], F32, WB0 + 32768)
    nT = OV.place("nT", [P, KC * T], BF16, WB0 + 65536)
    BIG = WB0 + 65536 + 32768
    BIGEND = 212832
    assert BIG == 135136 + 0 or True
    BIGSZ = BIGEND - BIG

    PS = [nc.alloc_psum_tensor(f"PS{i}", [P, 512], F32) for i in range(6)]
    PT = [nc.alloc_psum_tensor(f"PT{i}", [P, 1024], BF16) for i in range(2)]

    s_ld = Sem(nc, "s_ld"); s_w = Sem(nc, "s_w"); s_cast = Sem(nc, "s_cast")
    s_z0 = Sem(nc, "s_z0"); s_z1 = Sem(nc, "s_z1")
    s_x0 = Sem(nc, "s_x0"); s_x1 = Sem(nc, "s_x1")
    s_gd = Sem(nc, "s_gd")
    s_ops = [Sem(nc, f"s_op{i}") for i in range(4)]
    s_mm = Sem(nc, "s_mm"); s_ev = Sem(nc, "s_ev"); s_tp = Sem(nc, "s_tp")
    s_st = Sem(nc, "s_st"); s_cc = Sem(nc, "s_cc"); s_a = Sem(nc, "s_a")
    s_b = Sem(nc, "s_b"); s_g = Sem(nc, "s_g"); s_pe = Sem(nc, "s_pe")

    def barrier(blk):
        b = s_cc.reserve(1)

        def g(e):
            e.collective_compute("AllReduce", ALU.add, replica_groups=[list(range(8))],
                                 ins=[bar_io[0:1, :]], outs=[bar_io[1:2, :]]).then_inc(s_cc.h, 1)
            e.wait_ge(s_cc.h, b + 1)
        blk.gpsimd(g)

    def transpose_batch(blk, items):
        n = len(items)
        b_pe = s_pe.reserve(n)
        b_tp = s_tp.reserve(n)

        def peb(te):
            for i, (src, dst) in enumerate(items):
                if i >= 2:
                    te.wait_ge(s_tp.h, b_tp + i - 1)
                pp, m = src.shape
                te.transpose(PT[i % 2][0:m, 0:pp], src, idt[0:pp, 0:pp]).then_inc(s_pe.h, 1)

        def evb(v):
            for i, (src, dst) in enumerate(items):
                v.wait_ge(s_pe.h, b_pe + i + 1)
                pp, m = src.shape
                v.tensor_copy(dst, PT[i % 2][0:m, 0:pp]).then_inc(s_tp.h, 1)
        blk.tensor(peb)
        blk.vector(evb)

    def emit_ln(blk, tiles, width, eps, halo=None):
        nt = len(tiles) + (1 if halo is not None else 0)
        b_a = s_a.reserve(3)
        b_b = s_b.reserve(2)

        def v1(v):
            last = None
            for i, src in enumerate(tiles):
                last = v.tensor_reduce(out=mcol[:, i:i + 1], in_=src, axis=AX.X, op=ALU.add)
            if halo is not None:
                last = v.tensor_reduce(out=mcol[:3, nt - 1:nt], in_=halo, axis=AX.X, op=ALU.add)
            last.then_inc(s_a.h, 1)
        blk.vector(v1)

        def sc1(sc):
            last = None
            for i, src in enumerate(tiles):
                if i > 0:
                    sc.drain()
                last = sc.activation(out=junk[:, 0:width], in_=src, func=ACTF.Square, accum_out=qcol[:, i:i + 1])
            if halo is not None:
                sc.drain()
                last = sc.activation(out=junk[:3, 0:width], in_=halo, func=ACTF.Square, accum_out=qcol[:3, nt - 1:nt])
            last.then_inc(s_b.h, 1)
        blk.scalar(sc1)

        regions = [(slice(0, P), slice(0, len(tiles)))]
        if halo is not None:
            regions.append((slice(0, 3), slice(nt - 1, nt)))

        def v2(v):
            v.wait_ge(s_a.h, b_a + 1)
            v.wait_ge(s_b.h, b_b + 1)
            last = None
            for pr, cr in regions:
                v.tensor_scalar(out=mcol[pr, cr], in0=mcol[pr, cr], scalar1=1.0 / width, scalar2=None, op0=ALU.mult)
                v.tensor_scalar(out=qcol[pr, cr], in0=qcol[pr, cr], scalar1=1.0 / width, scalar2=None, op0=ALU.mult)
                v.drain()
                v.tensor_tensor(out=rcol[pr, cr], in0=mcol[pr, cr], in1=mcol[pr, cr], op=ALU.mult)
                v.drain()
                last = v.tensor_sub(qcol[pr, cr], qcol[pr, cr], rcol[pr, cr])
            last.then_inc(s_a.h, 1)
        blk.vector(v2)

        def sc2(sc):
            sc.wait_ge(s_a.h, b_a + 2)
            last = None
            for i, (pr, cr) in enumerate(regions):
                if i:
                    sc.drain()
                last = sc.activation(out=qcol[pr, cr], in_=qcol[pr, cr], func=ACTF.Sqrt, bias=eps)
            last.then_inc(s_b.h, 1)
        blk.scalar(sc2)

        def v3(v):
            v.wait_ge(s_b.h, b_b + 2)
            last = None
            for pr, cr in regions:
                v.reciprocal(rcol[pr, cr], qcol[pr, cr])
                v.drain()
                v.tensor_tensor(out=bcol[pr, cr], in0=mcol[pr, cr], in1=rcol[pr, cr], op=ALU.mult)
                v.drain()
                last = v.tensor_scalar(out=bcol[pr, cr], in0=bcol[pr, cr], scalar1=-1.0, scalar2=None, op0=ALU.mult)
            last.then_inc(s_a.h, 1)
        blk.vector(v3)
        return nt

    def dbg_dump(tag, items):
        if DEBUG != tag:
            return False
        with nc.Block() as blk:
            def _(e):
                b = s_gd.reserve(16 * len(items))
                for i, (ap, r0, ncols) in enumerate(items):
                    e.dma_start(out=bass.AP(DBG.tensor, r0 * 4160, [[4160, ap.shape[0]], [1, ncols]]),
                                in_=ap).then_inc(s_gd.h, 16)
                e.wait_ge(s_gd.h, b + 16 * len(items))
            blk.gpsimd(_)
        return True

    with nc.Block() as blk:
        def _(e):
            b = s_ld.reserve(9 * 16)
            e.dma_start(out=idt[:, :], in_=IDT[:, :]).then_inc(s_ld.h, 16)
            e.dma_start(out=trilm[:, :], in_=TRIL[:, :]).then_inc(s_ld.h, 16)
            e.dma_start(out=bass.AP(convw, 0, [[128, P], [4, 32], [1, 4]]), in_=bass.AP(CONVW.tensor, 0, [[4, P], [512, 32], [1, 4]])).then_inc(s_ld.h, 16)
            e.dma_start(out=convb[:, :], in_=bass.AP(CONVB.tensor, 0, [[1, P], [128, 32]]), allow_slow_non_contiguous=True).then_inc(s_ld.h, 16)
            e.dma_start(out=cA[:, 0:1], in_=bass.AP(ALOG.tensor, 0, [[1, 64], [1, 1]])).then_inc(s_ld.h, 16)
            e.dma_start(out=cDTB[:, 0:1], in_=bass.AP(DTB.tensor, 0, [[1, 64], [1, 1]])).then_inc(s_ld.h, 16)
            e.dma_start(out=cDP[0:1, :], in_=bass.AP(DPAR.tensor, 0, [[NH, 1], [1, NH]])).then_inc(s_ld.h, 16)
            e.dma_start(out=nsc_sb[:, :], in_=bass.AP(NSC.tensor, 0, [[1, P], [128, 16]]), allow_slow_non_contiguous=True).then_inc(s_ld.h, 16)
            e.dma_start(out=scr1[0:1, 0:D], in_=bass.AP(WNORM.tensor, 0, [[D, 1], [1, D]])).then_inc(s_ld.h, 16)
            e.wait_ge(s_ld.h, b + 144)
        blk.sync(_)

        def _(v):
            v.memset(ones_row[:, :], 1.0)
            v.memset(cst[:, 0:1], 1e-5)
            v.memset(cst[:, 1:2], 1e-6)
        blk.vector(_)



        def _(sc):
            sc.wait_ge(s_ld.h, s_ld.n)
            sc.activation(out=cA[:, 0:1], in_=cA[:, 0:1], func=ACTF.Exp)
        blk.scalar(_)

    with nc.Block() as blk:
        def _init(e):
            b2 = s_gd.reserve(16)
            e.dma_start(out=bar_io[0:1, :], in_=ones_row[0:1, 0:8]).then_inc(s_gd.h, 16)
            e.wait_ge(s_gd.h, b2 + 16)
        blk.gpsimd(_init)
        b_pe = s_pe.reserve(5)

        def _(te):
            te.matmul(PS[0][:, 0:NH], ones_row[0:1, 0:P], cDP[0:1, :], start=True, stop=True).then_inc(s_pe.h, 1)
            for j in range(4):
                te.matmul(PS[1 + j][:, 0:512], ones_row[0:1, 0:P], scr1[0:1, j * 512:(j + 1) * 512],
                          start=True, stop=True).then_inc(s_pe.h, 1)
        blk.tensor(_)

        def _(v):
            v.wait_ge(s_pe.h, b_pe + 1)
            v.tensor_copy(cDP[:, :], PS[0][:, 0:NH])
            for j in range(4):
                v.wait_ge(s_pe.h, b_pe + 2 + j)
                v.tensor_copy(wnorm_bc[:, j * 512:(j + 1) * 512], PS[1 + j][:, 0:512])
        blk.vector(_)

    qmask = OV.place("qmask", [P, 4096], F32, BIG)
    qbf = [OV.place(f"qbf{i}", [P, 4096], BF16, BIG + 16384 + i * 8192) for i in range(2)]
    wsh = [ws[:, 0:4096], ws[:, 4096:8192]]
    qsl = {"gv": (QGV, 2048, 2048), "wo": (QWO, 1024, 2048), "w1": (QW1, 2048, 768),
           "w3": (QW3, 2048, 768), "w2": (QW2, 768, 2048)}
    qcnt = {"gv": 16384 * 2048, "wo": 8192 * 2048, "w1": 5461 * 2048,
            "w3": 5461 * 2048, "w2": 2048 * 5461}
    qeps = {"gv": 1e-5, "wo": 1e-5, "w1": 0.0, "w3": 0.0, "w2": 0.0}
    qorder = ["gv", "wo", "w1", "w3", "w2"]
    qtl = []
    for nm in qorder:
        _, rows, cols = qsl[nm]
        for r in range(0, rows, P):
            for c in range(0, cols, 4096):
                qtl.append((qorder.index(nm), r, c, min(4096, cols - c)))
    NQT = len(qtl)

    with nc.Block() as blk:
        s_zp = [s_z0, s_z1]
        b_zp = [s_z0.reserve(((NQT + 1) // 2) * 16), s_z1.reserve((NQT // 2) * 16)]
        b_a = s_a.reserve(NQT)

        def dmab(e):
            for i, (wi, r, c, cw) in enumerate(qtl):
                if i >= 2:
                    e.wait_ge(s_a.h, b_a + i - 1)
                W = qsl[qorder[wi]][0]
                e.dma_start(out=wsh[i % 2][:, 0:cw], in_=W[r:r + P, c:c + cw]).then_inc(s_zp[i % 2].h, 16)
        blk.sync(dmab)

        def vb(v):
            for i, (wi, r, c, cw) in enumerate(qtl):
                v.wait_ge(s_zp[i % 2].h, b_zp[i % 2] + (i // 2 + 1) * 16)
                v.drain()
                v.tensor_reduce(out=qcol[:, 0:1], in_=wsh[i % 2][:, 0:cw], axis=AX.X,
                                op=ALU.add, apply_absolute_value=True)
                v.drain()
                if (r, c) == (0, 0):
                    v.tensor_copy(colA[:, wi:wi + 1], qcol[:, 0:1]).then_inc(s_a.h, 1)
                else:
                    v.tensor_add(colA[:, wi:wi + 1], colA[:, wi:wi + 1], qcol[:, 0:1]).then_inc(s_a.h, 1)
        blk.vector(vb)

    with nc.Block() as blk:
        b_pe = s_pe.reserve(2)
        b_a = s_a.reserve(3)
        b_gd = s_gd.reserve(32)
        b_cc = s_cc.reserve(1)

        def v0(v):
            v.memset(cst[:, 2:3], 1.0).then_inc(s_a.h, 1)
        blk.vector(v0)

        def pe1(te):
            te.wait_ge(s_a.h, b_a + 1)
            te.matmul(PS[0][0:1, 0:5], cst[:, 2:3], colA[:, 0:5], start=True, stop=True).then_inc(s_pe.h, 1)
        blk.tensor(pe1)

        def v1(v):
            v.wait_ge(s_pe.h, b_pe + 1)
            v.tensor_copy(colB[0:1, 0:5], PS[0][0:1, 0:5]).then_inc(s_a.h, 1)
        blk.vector(v1)

        def g1(e):
            e.wait_ge(s_a.h, b_a + 2)
            e.dma_start(out=sc_in[0:1, 0:5], in_=colB[0:1, 0:5]).then_inc(s_gd.h, 16)
            e.wait_ge(s_gd.h, b_gd + 16)
            e.collective_compute("AllReduce", ALU.add, replica_groups=[list(range(8))],
                                 ins=[sc_in[0:1, 0:5]], outs=[sc_out[0:1, 0:5]]).then_inc(s_cc.h, 1)
            e.wait_ge(s_cc.h, b_cc + 1)
            e.dma_start(out=colB[0:1, 0:5], in_=sc_out[0:1, 0:5]).then_inc(s_gd.h, 32 - 16)
            e.wait_ge(s_gd.h, b_gd + 32)
        blk.gpsimd(g1)

        def v2(v):
            v.wait_ge(s_gd.h, b_gd + 32)
            for wi, nm in enumerate(qorder):
                v.drain()
                v.tensor_scalar(out=colB[0:1, wi:wi + 1], in0=colB[0:1, wi:wi + 1],
                                scalar1=1.0 / qcnt[nm], scalar2=qeps[nm], op0=ALU.mult, op1=ALU.max)
            v.drain()
            v.reciprocal(colB[0:1, 5:10], colB[0:1, 0:5]).then_inc(s_a.h, 1)
        blk.vector(v2)

        def pe2(te):
            te.wait_ge(s_a.h, b_a + 3)
            te.matmul(PS[1][:, 0:10], ones_row[0:1, 0:P], colB[0:1, 0:10],
                      start=True, stop=True).then_inc(s_pe.h, 1)
        blk.tensor(pe2)

        def v3(v):
            v.wait_ge(s_pe.h, b_pe + 2)
            v.tensor_copy(qsc[:, 0:5], PS[1][:, 0:5])
            v.drain()
            v.tensor_copy(colC[:, 0:5], PS[1][:, 5:10])
        blk.vector(v3)

    with nc.Block() as blk:
        s_zp = [s_z0, s_z1]
        b_zp = [s_z0.reserve(((NQT + 1) // 2) * 16), s_z1.reserve((NQT // 2) * 16)]
        b_a = s_a.reserve(NQT)
        b_tp = s_tp.reserve(NQT)
        b_sp = [s_ops[0].reserve(((NQT + 1) // 2) * 16), s_ops[1].reserve((NQT // 2) * 16)]

        def dmab(e):
            for i, (wi, r, c, cw) in enumerate(qtl):
                if i >= 2:
                    e.wait_ge(s_a.h, b_a + i - 1)
                W = qsl[qorder[wi]][0]
                e.dma_start(out=wsh[i % 2][:, 0:cw], in_=W[r:r + P, c:c + cw]).then_inc(s_zp[i % 2].h, 16)
        blk.sync(dmab)

        def vb(v):
            for i, (wi, r, c, cw) in enumerate(qtl):
                v.wait_ge(s_zp[i % 2].h, b_zp[i % 2] + (i // 2 + 1) * 16)
                if i >= 1:
                    v.drain()
                v.tensor_scalar(out=qmask[:, 0:cw], in0=wsh[i % 2][:, 0:cw],
                                scalar1=colC[:, wi:wi + 1], scalar2=MAGIC,
                                op0=ALU.mult, op1=ALU.add).then_inc(s_a.h, 1)
                v.drain()
                v.tensor_scalar(out=qmask[:, 0:cw], in0=qmask[:, 0:cw], scalar1=MAGIC,
                                scalar2=1.0, op0=ALU.subtract, op1=ALU.min)
                v.drain()
                if i >= 2:
                    v.wait_ge(s_ops[i % 2].h, b_sp[i % 2] + (i // 2) * 16)
                v.tensor_scalar(out=qbf[i % 2][:, 0:cw], in0=qmask[:, 0:cw], scalar1=-1.0,
                                scalar2=None, op0=ALU.max).then_inc(s_tp.h, 1)
        blk.vector(vb)

        def stb(e):
            for i, (wi, r, c, cw) in enumerate(qtl):
                e.wait_ge(s_tp.h, b_tp + i + 1)
                _, rows, cols = qsl[qorder[wi]]
                e.dma_start(out=bass.AP(qag_in, q_off[qorder[wi]] + r * cols + c, [[cols, P], [1, cw]]),
                            in_=qbf[i % 2][:, 0:cw]).then_inc(s_ops[i % 2].h, 16)
            e.wait_ge(s_ops[0].h, b_sp[0] + ((NQT + 1) // 2) * 16)
            e.wait_ge(s_ops[1].h, b_sp[1] + (NQT // 2) * 16)
        blk.gpsimd(stb)

    with nc.Block() as blk:
        b_cc = s_cc.reserve(1)

        def g1(e):
            e.collective_compute("AllGather", ALU.bypass, replica_groups=[list(range(8))],
                                 ins=[qag_in.ap()], outs=[qag_out.ap()]).then_inc(s_cc.h, 1)
            e.wait_ge(s_cc.h, b_cc + 1)
        blk.gpsimd(g1)

    if STOP == "QP":
        return nc

    x_sb = OV.place("x_sb", [P, NTT * D], F32, BIG)
    xh_sb = OV.place("xh_sb", [3, D], F32, BIG + 65536)
    nh_sb = OV.place("nh_sb", [3, D], BF16, BIG + 73728)
    nhT = OV.place("nhT", [P, KC * 3], BF16, BIG + 77824)

    with nc.Block() as blk:
        def _(e):
            b = s_ld.reserve(9 * 16)
            for t in range(NTT):
                e.dma_start(out=x_sb[:, t * D:(t + 1) * D], in_=X[t * P:(t + 1) * P, :]).then_inc(s_ld.h, 16)
            e.dma_start(out=xh_sb[:, :], in_=XH[:, :]).then_inc(s_ld.h, 16)
            e.wait_ge(s_ld.h, b + 144)
        blk.sync(_)

    with nc.Block() as blk:
        emit_ln(blk, [x_sb[:, t * D:(t + 1) * D] for t in range(NTT)], D, 1e-5, halo=xh_sb[:, :])

    with nc.Block() as blk:
        n0_st = OV.place("n0_st", [P, D], BF16, 12288)
        b_b = s_b.reserve(NTT + 1)
        b_pe = s_pe.reserve(NTT * KC + KC)
        b_tp = s_tp.reserve(NTT * KC + KC)

        def scb(sc):
            for t in range(NTT):
                if t > 0:
                    sc.wait_ge(s_tp.h, b_tp + t * KC)
                sc.activation(out=n0_st[:, :], in_=x_sb[:, t * D:(t + 1) * D], func=ACTF.Identity,
                              scale=rcol[:, t:t + 1], bias=bcol[:, t:t + 1]).then_inc(s_b.h, 1)
            sc.wait_ge(s_tp.h, b_tp + NTT * KC)
            sc.activation(out=nh_sb[:, :], in_=xh_sb[:, :], func=ACTF.Identity,
                          scale=rcol[:3, 8:9], bias=bcol[:3, 8:9]).then_inc(s_b.h, 1)
        blk.scalar(scb)

        def peb(te):
            i = 0
            for t in range(NTT):
                te.wait_ge(s_b.h, b_b + t + 1)
                for kc in range(KC):
                    if i >= 2:
                        te.wait_ge(s_tp.h, b_tp + i - 1)
                    te.transpose(PT[i % 2][:, 0:P], n0_st[:, kc * P:(kc + 1) * P], idt[:, :]).then_inc(s_pe.h, 1)
                    i += 1
            te.wait_ge(s_b.h, b_b + NTT + 1)
            for kc in range(KC):
                te.wait_ge(s_tp.h, b_tp + i - 1)
                te.transpose(PT[i % 2][0:P, 0:3], nh_sb[0:3, kc * P:(kc + 1) * P], idt[0:3, 0:3]).then_inc(s_pe.h, 1)
                i += 1
        blk.tensor(peb)

        def evb(v):
            i = 0
            for t in range(NTT):
                for kc in range(KC):
                    v.wait_ge(s_pe.h, b_pe + i + 1)
                    v.tensor_copy(nT[:, kc * T + t * P:kc * T + (t + 1) * P], PT[i % 2][:, 0:P]).then_inc(s_tp.h, 1)
                    i += 1
            for kc in range(KC):
                v.wait_ge(s_pe.h, b_pe + i + 1)
                v.tensor_copy(nhT[:, kc * 3:(kc + 1) * 3], PT[i % 2][0:P, 0:3]).then_inc(s_tp.h, 1)
                i += 1
        blk.vector(evb)

    with nc.Block() as blk:
        def _(e):
            b = s_st.reserve(NTT * 16)
            for t in range(NTT):
                e.dma_start(out=pk_x[t * P:(t + 1) * P, :], in_=x_sb[:, t * D:(t + 1) * D]).then_inc(s_st.h, 16)
            e.wait_ge(s_st.h, b + NTT * 16)
        blk.sync(_)

    if dbg_dump("ln0", [(nT[:, 0:4096], 0, 4096), (rcol[:, 0:8], 300, 8)]):
        return nc

    XW = 1027
    x_fm = OV.place("x_fm", [P, 32 * XW], BF16, BIG)
    z_st = [OV.place(f"z_st{i}", [P, 512], BF16, BIG + 65728 + i * 1024) for i in range(2)]
    bcraw = OV.place("bcraw", [P, T], F32, BIG + 67776)
    dtraw = OV.place("dtraw", [64, T], F32, BIG + 71872)
    qk32 = OV.place("qk32", [P, T], F32, BIG + 67776)
    la_my = OV.place("la_my", [64, T], F32, BIG + 71872)

    groups = [("z", g * 512, 512) for g in range(8)] + \
             [("x", 4096 + g * 512, 512) for g in range(8)] + [("bc", 8192, 256)]
    tile_seq = []
    for gi, (kind, c0, cw) in enumerate(groups):
        if kind == "z":
            tile_seq += [(gi, "z", (c0, t)) for t in range(NTT)]
        elif kind == "x":
            tile_seq += [(gi, "x", (c0, oc, tg)) for oc in range(4) for tg in range(2)]
            tile_seq += [(gi, "xh", (c0, oc)) for oc in range(4)]
        else:
            tile_seq += [(gi, "bc", (tg,)) for tg in range(2)]
            tile_seq += [(gi, "dt", (tg,)) for tg in range(2)]
    NTILE = len(tile_seq)
    first_tile = {}
    for ti, (gi, k, m) in enumerate(tile_seq):
        first_tile.setdefault(gi, ti)

    with nc.Block() as blk:
        NGR = len(groups)
        b_w = s_w.reserve(NGR * 16)
        b_cast = s_cast.reserve(NGR)
        b_mm = s_mm.reserve(NTILE)
        b_ev = s_ev.reserve(NTILE)

        def dmab(e):
            zi = 0
            for gi, (kind, c0, cw) in enumerate(groups):
                if gi >= 1:
                    e.wait_ge(s_cast.h, b_cast + gi)
                e.dma_start(out=ws[:, 0:KC * cw],
                            in_=bass.AP(Wip.tensor, c0, [[8448, P], [P * 8448, KC], [1, cw]])).then_inc(s_w.h, 16)
                if gi >= 1 and groups[gi - 1][0] == "z":
                    zc0 = groups[gi - 1][1]
                    for t in range(NTT):
                        ti = (gi - 1) * NTT + t
                        e.wait_ge(s_ev.h, b_ev + ti + 1)
                        e.dma_start(out=bass.AP(pk_z.tensor, t * P * 4096 + zc0, [[4096, P], [1, 512]]),
                                    in_=z_st[zi % 2][:, 0:512]).then_inc((s_z0 if zi % 2 == 0 else s_z1).h, 16)
                        zi += 1
            e.wait_ge(s_z0.h, b_pz0 + 16 * 32)
            e.wait_ge(s_z1.h, b_pz1 + 16 * 32)
        b_pz0 = s_z0.reserve(16 * 32)
        b_pz1 = s_z1.reserve(16 * 32)
        blk.sync(dmab)

        def castb(sc):
            for gi, (kind, c0, cw) in enumerate(groups):
                sc.wait_ge(s_w.h, b_w + (gi + 1) * 16)
                if gi >= 2:
                    lim = first_tile.get(gi - 1, NTILE)
                    sc.wait_ge(s_mm.h, b_mm + lim)
                sc.copy(wb[gi % 2][:, 0:KC * cw], ws[:, 0:KC * cw]).then_inc(s_cast.h, 1)
        blk.scalar(castb)

        def peb(te):
            for ti, (gi, knd, meta) in enumerate(tile_seq):
                kind, c0, cw = groups[gi]
                if ti == first_tile[gi]:
                    te.wait_ge(s_cast.h, b_cast + gi + 1)
                bank = ti % 2
                if ti >= 2:
                    te.wait_ge(s_ev.h, b_ev + ti - 1)
                wbg = wb[gi % 2]
                last = None
                if knd == "z":
                    t = meta[1]
                    for kc in range(KC):
                        last = te.matmul(PS[bank][:, 0:512], nT[:, kc * T + t * P:kc * T + (t + 1) * P],
                                         wbg[:, kc * 512:(kc + 1) * 512], start=(kc == 0), stop=(kc == KC - 1))
                elif knd == "x":
                    _, oc, tg = meta
                    for kc in range(KC):
                        last = te.matmul(PS[bank][:, 0:512], wbg[:, kc * 512 + oc * P:kc * 512 + (oc + 1) * P],
                                         nT[:, kc * T + tg * 512:kc * T + (tg + 1) * 512],
                                         start=(kc == 0), stop=(kc == KC - 1))
                elif knd == "xh":
                    _, oc = meta
                    for kc in range(KC):
                        last = te.matmul(PS[bank][:, 0:3], wbg[:, kc * 512 + oc * P:kc * 512 + (oc + 1) * P],
                                         nhT[:, kc * 3:(kc + 1) * 3], start=(kc == 0), stop=(kc == KC - 1))
                elif knd == "bc":
                    tg = meta[0]
                    for kc in range(KC):
                        last = te.matmul(PS[bank][:, 0:512], wbg[:, kc * 256:kc * 256 + P],
                                         nT[:, kc * T + tg * 512:kc * T + (tg + 1) * 512],
                                         start=(kc == 0), stop=(kc == KC - 1))
                else:
                    tg = meta[0]
                    for kc in range(KC):
                        last = te.matmul(PS[bank][:, 0:512], wbg[:, kc * 256 + P:kc * 256 + 2 * P],
                                         nT[:, kc * T + tg * 512:kc * T + (tg + 1) * 512],
                                         start=(kc == 0), stop=(kc == KC - 1))
                last.then_inc(s_mm.h, 1)
        blk.tensor(peb)

        def evb(v):
            for ti, (gi, knd, meta) in enumerate(tile_seq):
                bank = ti % 2
                v.wait_ge(s_mm.h, b_mm + ti + 1)
                if knd == "z":
                    t = meta[1]
                    zi2 = z_idx[ti]
                    if zi2 >= 2:
                        v.wait_ge((s_z0 if zi2 % 2 == 0 else s_z1).h,
                                  (b_pz0 if zi2 % 2 == 0 else b_pz1) + 16 * (zi2 // 2))
                    v.tensor_copy(z_st[zi2 % 2][:, 0:512], PS[bank][:, 0:512]).then_inc(s_ev.h, 1)
                elif knd == "x":
                    _, oc, tg = meta
                    ci = (meta[0] - 4096) // P + oc
                    v.tensor_copy(x_fm[:, ci * XW + 3 + tg * 512:ci * XW + 3 + (tg + 1) * 512],
                                  PS[bank][:, 0:512]).then_inc(s_ev.h, 1)
                elif knd == "xh":
                    ci = (meta[0] - 4096) // P + meta[1]
                    v.tensor_copy(x_fm[:, ci * XW:ci * XW + 3], PS[bank][:, 0:3]).then_inc(s_ev.h, 1)
                elif knd == "bc":
                    tg = meta[0]
                    v.tensor_copy(bcraw[:, tg * 512:(tg + 1) * 512], PS[bank][:, 0:512]).then_inc(s_ev.h, 1)
                else:
                    tg = meta[0]
                    v.tensor_copy(dtraw[:, tg * 512:(tg + 1) * 512], PS[bank][0:64, 0:512]).then_inc(s_ev.h, 1)
        z_idx, z_count = {}, {}
        zc = 0
        for ti, (gi, knd, meta) in enumerate(tile_seq):
            if knd == "z":
                z_idx[ti] = zc
                z_count[ti] = zc
                zc += 1
        blk.vector(evb)


    with nc.Block() as blk:
        b_a = s_a.reserve(2)
        b_b = s_b.reserve(1)

        def v1(v):
            v.tensor_single_scalar(out=scr2[:, 0:T], in_=bcraw[:, 0:T], scalar=0.0, op=ALU.min)
            v.tensor_relu(scr1[:, 0:T], bcraw[:, 0:T]).then_inc(s_a.h, 1)
        blk.vector(v1)

        def sc1(sc):
            sc.wait_ge(s_a.h, b_a + 1)
            sc.activation(out=scr2[:, 0:T], in_=scr2[:, 0:T], func=ACTF.Exp)
            sc.drain()
            sc.activation(out=scr2[0:64, T:2 * T], in_=dtraw[:, 0:T], func=ACTF.Abs, bias=cDTB[:, 0:1])
            sc.drain()
            sc.activation(out=scr2[0:64, T:2 * T], in_=scr2[0:64, T:2 * T], func=ACTF.Exp, scale=-1.0)
            sc.drain()
            sc.activation(out=scr2[0:64, T:2 * T], in_=scr2[0:64, T:2 * T], func=ACTF.Ln, bias=1.0)
            sc.activation(out=scr1[0:64, T:2 * T], in_=dtraw[:, 0:T], func=ACTF.Relu,
                          bias=cDTB[:, 0:1]).then_inc(s_b.h, 1)
        blk.scalar(sc1)

        def v2(v):
            v.wait_ge(s_b.h, b_b + 1)
            v.tensor_add(qk32[:, 0:T], scr1[:, 0:T], scr2[:, 0:T])
            v.drain()
            v.tensor_add(la_my[:, 0:T], scr1[0:64, T:2 * T], scr2[0:64, T:2 * T])
            v.drain()
            v.tensor_scalar(out=la_my[:, 0:T], in0=la_my[:, 0:T], scalar1=cA[:, 0:1], scalar2=-1.0, op0=ALU.mult, op1=ALU.mult).then_inc(s_a.h, 1)
        blk.vector(v2)

    xcv = [OV.place(f"xcv{i}", [P, T], BF16, BIG + 75968 + i * 2048) for i in range(2)]
    cacc2 = [scr1, scr2]
    with nc.Block() as blk:
        b_a = s_a.reserve(32)
        b_b = s_b.reserve(32)

        def v1(v):
            for ci in range(32):
                xin = x_fm[:, ci * XW:(ci + 1) * XW]
                acc = cacc2[ci % 2]
                if ci >= 2:
                    v.wait_ge(s_b.h, b_b + ci - 1)
                v.tensor_scalar(out=acc[:, 0:T], in0=xin[:, 0:T], scalar1=convw[:, ci * 4:ci * 4 + 1],
                                scalar2=None, op0=ALU.mult)
                last = None
                for k in range(1, 4):
                    v.drain()
                    last = v.scalar_tensor_tensor(out=acc[:, 0:T], in0=xin[:, k:k + T],
                                                  scalar=convw[:, ci * 4 + k:ci * 4 + k + 1],
                                                  in1=acc[:, 0:T], op0=ALU.mult, op1=ALU.add)
                last.then_inc(s_a.h, 1)
        blk.vector(v1)

        b_px0 = s_x0.reserve(16 * 16)
        b_px1 = s_x1.reserve(16 * 16)

        def sc1(sc):
            for ci in range(32):
                sc.wait_ge(s_a.h, b_a + ci + 1)
                if ci >= 2:
                    sc.wait_ge((s_x0 if ci % 2 == 0 else s_x1).h,
                               (b_px0 if ci % 2 == 0 else b_px1) + 16 * (ci // 2))
                sc.activation(out=xcv[ci % 2][:, :], in_=cacc2[ci % 2][:, 0:T], func=ACTF.Silu,
                              bias=convb[:, ci:ci + 1]).then_inc(s_b.h, 1)
        blk.scalar(sc1)

        def st1(e):
            for ci in range(32):
                e.wait_ge(s_b.h, b_b + ci + 1)
                e.dma_start(out=bass.AP(pk_xcv.tensor, ci * P * T, [[T, P], [1, T]]),
                            in_=xcv[ci % 2][:, :]).then_inc((s_x0 if ci % 2 == 0 else s_x1).h, 16)
            e.wait_ge(s_x0.h, b_px0 + 16 * 16)
            e.wait_ge(s_x1.h, b_px1 + 16 * 16)
        blk.sync(st1)

    if dbg_dump("inproj", [(qk32[:, 0:T], 0, T), (la_my[:, 0:T], 128, T),
                           (x_fm[:, 0:2 * XW], 192, 2 * XW),
                           (xcv[1][:, :], 320, T)]):
        return nc

    with nc.Block() as blk:
        def pub(e):
            b = s_st.reserve(4 * 16)
            for arm in e.switch_core_id(n=128):
                h2 = arm.logical % 2
                e.dma_start(out=sh_z[h2 * T:(h2 + 1) * T, :], in_=pk_z[:, :]).then_inc(s_st.h, 16)
                e.dma_start(out=bass.AP(sh_xc.tensor, h2 * T, [[2048, 4096], [1, T]]),
                            in_=pk_xcv[:, :]).then_inc(s_st.h, 16)
                e.dma_start(out=bass.AP(sh_qk.tensor, h2 * T, [[2048, P], [1, T]]),
                            in_=qk32[:, 0:T]).then_inc(s_st.h, 16)
                e.dma_start(out=bass.AP(sh_la.tensor, h2 * T, [[2048, 64], [1, T]]),
                            in_=la_my[:, 0:T]).then_inc(s_st.h, 16)
            e.wait_ge(s_st.h, b + 64)
        blk.sync(pub)
    with nc.Block() as blk:
        barrier(blk)

    def mk(name, shape, dtype, off):
        return OV.place(name, shape, dtype, off)

    cur = [BIG]

    def nxt(name, shape, dtype):
        per = int(np.prod(shape[1:])) * mybir.dt.size(dtype)
        per = (per + 31) // 32 * 32
        t_ = OV.place(name, shape, dtype, cur[0])
        cur[0] += per
        return t_

    k_f = nxt("k_f", [64, 2048], F32)
    q_f = nxt("q_f", [64, 2048], F32)
    k_b = nxt("k_b", [64, 2048], BF16)
    q_b = nxt("q_b", [64, 2048], BF16)
    la32 = nxt("la32", [32, 2048], F32)
    cs = nxt("cs", [32, 2048], F32)
    rst = nxt("rst", [32, 2048], F32)
    QKm = None
    csT2 = nxt("csT2", [64, 32 * 32], F32)
    wkT2 = nxt("wkT2", [64, 32 * 32], BF16)
    k_tm = nxt("k_tm", [64, 32 * 64], BF16)
    d_bc = nxt("d_bc", [64, 32 * 32], F32)
    dex = nxt("dex", [32, 32], F32)
    idt32 = nxt("idt32", [P, P], F32)
    csb = [nxt(f"csb{i}", [64, 512], F32) for i in range(2)]
    Dsc = [nxt(f"Dsc{i}", [64, 512], F32) for i in range(2)]
    Esc = [nxt(f"Esc{i}", [64, 512], F32) for i in range(2)]
    xpcs = [nxt(f"xpcs{i}", [64, 512], F32) for i in range(2)]
    assert cur[0] <= BIGEND, cur[0]
    QKm = OV.place("QKm", [64, 2048], F32, nc.lookup_mloc(la32).addr - OV.base)
    y_stg = [OV.place(f"y_stg{i}", [64, 2048], BF16, nc.lookup_mloc(rst).addr - OV.base + i * 4096) for i in range(2)]
    attT = OV.place("attT", [64, 16 * 512], BF16, WB0 + 32768)
    qsc8 = OV.place("qsc8", [64, 16 * 512], BF16, WB0 + 32768 + 16384)
    state = OV.place("state", [64, 16 * 64], F32, WB0)
    state_b = OV.place("state_b", [64, 16 * 64], BF16, WB0 + 4096 + 8192)
    states_st = OV.place("states_st", [64, 2 * 16 * 64], F32, WB0 + 4096)
    vslab = [OV.place(f"vslab{i}", [P, 16 * 128], BF16, WB0 + 14336 + i * 4096) for i in range(2)]
    NT0 = WB0 + 65536
    v_tm = [OV.place(f"v_tm{i}", [64, 2 * 2048], BF16, NT0 + i * 8192) for i in range(2)]
    ks = [OV.place(f"ks{i}", [64, 2 * 1024], BF16, NT0 + 16384 + i * 4096) for i in range(2)]

    s_v0 = Sem(nc, "s_v0"); s_v1 = Sem(nc, "s_v1")
    s_y0 = Sem(nc, "s_y0"); s_y1 = Sem(nc, "s_y1")

    IDT32 = din("IDT32", [P, P])

    with nc.Block() as blk:
        def rep(e):
            b = s_ld.reserve(5 * 16)
            for arm in e.switch_core_id(n=128):
                h2 = arm.logical % 2
                e.dma_start(out=pk_xc[:, :], in_=sh_xc[h2 * D:(h2 + 1) * D, 0:2048]).then_inc(s_ld.h, 16)
                e.dma_start(out=la32[:, :], in_=sh_la[h2 * 32:(h2 + 1) * 32, :]).then_inc(s_ld.h, 16)
            e.dma_start(out=k_f[:, :], in_=sh_qk[0:64, :]).then_inc(s_ld.h, 16)
            e.dma_start(out=q_f[:, :], in_=sh_qk[64:128, :]).then_inc(s_ld.h, 16)
            e.dma_start(out=idt32[:, :], in_=IDT32[:, :]).then_inc(s_ld.h, 16)
            e.wait_ge(s_ld.h, b + 80)
        blk.sync(rep)

        def cst_b(v):
            b_a2 = s_a.reserve(2)
            v.wait_ge(s_ld.h, s_ld.n)
            v.tensor_copy(k_b[:, :], k_f[:, :])
            v.tensor_copy(q_b[:, :], q_f[:, :]).then_inc(s_a.h, 1)
            v.memset(rst[:, :], 1.0)
            v.drain()
            v.memset(bass.AP(rst, 0, [[2048, 32], [64, 32]]), 0.0)
            v.drain()
            v.tensor_tensor_scan(out=cs[:, :], data0=rst[:, :], data1=la32[:, :],
                                 initial=0.0, op0=ALU.mult, op1=ALU.add).then_inc(s_a.h, 1)
        blk.vector(cst_b)

    with nc.Block() as blk:
        b_a = s_a.reserve(1)
        b_b = s_b.reserve(2)

        def v1(v):
            v.tensor_tensor(out=rst[:, :], in0=bass.AP(cs, 63, [[2048, 32], [64, 32], [0, 64]]),
                            in1=cs[:, :], op=ALU.subtract).then_inc(s_a.h, 1)
        blk.vector(v1)

        def sc1(sc):
            sc.wait_ge(s_a.h, b_a + 1)
            sc.activation(out=rst[:, :], in_=rst[:, :], func=ACTF.Exp).then_inc(s_b.h, 1)
            sc.activation(out=dex[:, :], in_=bass.AP(cs, 63, [[2048, 32], [64, 32]]),
                          func=ACTF.Exp).then_inc(s_b.h, 1)
        blk.scalar(sc1)

        ntp = 32 + 32 + 32
        b_pe = s_pe.reserve(ntp)
        b_tp = s_tp.reserve(ntp)

        def peb(te):
            i = 0
            te.wait_ge(s_b.h, b_b + 1)
            for c in range(32):
                if i >= 2:
                    te.wait_ge(s_tp.h, b_tp + i - 1)
                te.transpose(PS[4 + i % 2][0:64, 0:32], cs[:, c * 64:(c + 1) * 64], idt32[0:32, 0:32])
                te.nop().then_inc(s_pe.h, 1) if False else None
                i += 1
                te.then_inc if False else None
        def peb2(te):
            i = 0
            te.wait_ge(s_b.h, b_b + 2)
            for c in range(32):
                if i >= 2:
                    te.wait_ge(s_tp.h, b_tp + i - 1)
                te.transpose(PS[4 + i % 2][0:64, 0:32], cs[:, c * 64:(c + 1) * 64],
                             idt32[0:32, 0:32]).then_inc(s_pe.h, 1)
                i += 1
            for c in range(32):
                if i >= 2:
                    te.wait_ge(s_tp.h, b_tp + i - 1)
                te.transpose(PS[4 + i % 2][0:64, 0:32], rst[:, c * 64:(c + 1) * 64],
                             idt32[0:32, 0:32]).then_inc(s_pe.h, 1)
                i += 1
            for c in range(32):
                if i >= 2:
                    te.wait_ge(s_tp.h, b_tp + i - 1)
                te.transpose(PS[4 + i % 2][0:64, 0:64], k_f[0:64, c * 64:(c + 1) * 64],
                             idt32[0:64, 0:64]).then_inc(s_pe.h, 1)
                i += 1
        blk.tensor(peb2)

        def evb(v):
            i = 0
            for c in range(32):
                v.wait_ge(s_pe.h, b_pe + i + 1)
                v.tensor_copy(csT2[:, c * 32:(c + 1) * 32], PS[4 + i % 2][0:64, 0:32]).then_inc(s_tp.h, 1)
                i += 1
            for c in range(32):
                v.wait_ge(s_pe.h, b_pe + i + 1)
                v.tensor_copy(wkT2[:, c * 32:(c + 1) * 32], PS[4 + i % 2][0:64, 0:32]).then_inc(s_tp.h, 1)
                i += 1
            for c in range(32):
                v.wait_ge(s_pe.h, b_pe + i + 1)
                v.tensor_copy(k_tm[:, c * 64:(c + 1) * 64], PS[4 + i % 2][0:64, 0:64]).then_inc(s_tp.h, 1)
                i += 1
        blk.vector(evb)

    with nc.Block() as blk:
        b_pe = s_pe.reserve(32 + 4)
        b_ev = s_ev.reserve(32 + 4)

        def peb(te):
            i = 0
            for h in range(32):
                if i >= 2:
                    te.wait_ge(s_ev.h, b_ev + i - 1)
                te.matmul(PS[4 + i % 2][0:64, 0:32], bass.AP(idt32, h, [[128, 32], [0, 64]]),
                          dex[0:32, :], start=True, stop=True).then_inc(s_pe.h, 1)
                i += 1
            for bk in range(4):
                if i >= 2:
                    te.wait_ge(s_ev.h, b_ev + i - 1)
                last = None
                for j in range(8):
                    c = bk * 8 + j
                    last = te.matmul(PS[bk][0:64, j * 64:(j + 1) * 64],
                                     k_b[:, c * 64:(c + 1) * 64],
                                     q_b[:, c * 64:(c + 1) * 64],
                                     start=(j == 0), stop=(j == 7))
                last.then_inc(s_pe.h, 1)
                i += 1
        blk.tensor(peb)

        def evb(v):
            i = 0
            for h in range(32):
                v.wait_ge(s_pe.h, b_pe + i + 1)
                v.tensor_copy(d_bc[:, h * 32:(h + 1) * 32], PS[4 + i % 2][0:64, 0:32]).then_inc(s_ev.h, 1)
                i += 1
            for bk in range(4):
                v.wait_ge(s_pe.h, b_pe + i + 1)
                v.tensor_tensor(out=QKm[:, bk * 512:(bk + 1) * 512], in0=PS[bk][0:64, 0:512],
                                in1=bass.AP(trilm, 0, [[64, 64], [0, 8], [1, 64]]),
                                op=ALU.mult).then_inc(s_ev.h, 1)
                i += 1
        blk.vector(evb)

    if dbg_dump("ssdprep", [(cs[:, 0:2048], 0, 2048), (QKm[:, 0:2048], 40, 2048),
                            (csT2[:, 0:1024], 110, 1024), (wkT2[:, 0:1024], 180, 1024),
                            (d_bc[:, 0:1024], 250, 1024), (k_tm[:, 0:1024], 320, 1024),
                            (k_b[:, 0:2048], 400, 2048), (q_b[:, 0:2048], 470, 2048)]):
        return nc

    with nc.Block() as blk:
        NSEG = 4
        NPAIR_TOT = 32
        NICB = 2 * NSEG * 16
        b_csb_pe = s_pe.n
        s_pe.reserve(2 * NSEG * (16 + 4 * 32))
        b_sb = s_b.reserve(NICB)
        b_exp = s_ev.reserve(NICB * 2)
        b_sub = s_tp.reserve(NICB)
        b_att = s_a.reserve(NICB * 2)
        b_min = s_g.reserve(NICB + 2 * 33)
        b_vt = s_cast.reserve(NPAIR_TOT * 33)
        b_ymm = s_mm.reserve(NPAIR_TOT * 8)
        b_sev = s_st.reserve(NPAIR_TOT * 4)
        b_upd = s_w.reserve(2 + NPAIR_TOT * 2)
        for s in (s_v0, s_v1):
            s.n += 16 * 16
        for s in (s_y0, s_y1):
            s.n += 16 * 16
        b_yev = s_cc.reserve(NPAIR_TOT * 32)

        def pe_k1_val(icb):
            gsg, hh = divmod(icb, 16)
            return b_csb_pe + gsg * 144 + hh + 1

        def pe_tp_val(g):
            ipair, j = divmod(g, 32)
            gsg, lp = divmod(ipair, 4)
            return b_csb_pe + gsg * 144 + 16 + lp * 32 + j + 1

        def upd_val(gc):
            g2, c = divmod(gc, 32)
            return b_upd + (g2 + 1) + gc

        g_seq = []
        for _g2 in range(2):
            g_seq.append(("cast", ("init", _g2)))
            for _sg in range(NSEG):
                for _hh in range(16):
                    g_seq.append(("min", _g2 * 64 + _sg * 16 + _hh))
                for _lp in range(4):
                    for _par in range(2):
                        g_seq.append(("cast", _g2 * 32 + (_sg * 4 + _lp) * 2 + _par))
        g_min_val = {}
        g_cast_val = {}
        for _i, (_k, _v) in enumerate(g_seq):
            (g_min_val if _k == "min" else g_cast_val)[_v] = _i + 1

        def mirror_val(gc):
            if gc % 32 == 0:
                return g_cast_val[("init", gc // 32)]
            return g_cast_val[gc - 1]

        def y_store_one(e, ipair):
            g2, c2 = divmod(ipair, 16)
            e.wait_ge(s_cc.h, b_yev + (ipair + 1) * 32)
            e.dma_start(out=bass.AP(pk_y.tensor, (c2 * P) * D + g2 * 1024,
                                    [[D, 64], [64 * D, 2], [1, 1024]]),
                        in_=bass.AP(y_stg[ipair % 2], 0, [[2048, 64], [1024, 2], [1, 1024]])
                        ).then_inc((s_y0 if ipair % 2 == 0 else s_y1).h, 16)

        def dmab(e):
            for zi in range(NPAIR_TOT):
                g2, c2 = divmod(zi, 16)
                slot = zi % 2
                if zi >= 2:
                    e.wait_ge(s_cast.h, b_vt + 33 * (zi - 2) + 32)
                e.dma_start(out=vslab[slot][:, :],
                            in_=bass.AP(pk_xc.tensor, c2 * P, [[2048, P], [P * 2048, 16], [1, P]])
                            ).then_inc((s_v0 if slot == 0 else s_v1).h, 16)
                if zi >= 1:
                    y_store_one(e, zi - 1)
            y_store_one(e, NPAIR_TOT - 1)
            e.wait_ge(s_y0.h, 16 * 16)
            e.wait_ge(s_y1.h, 16 * 16)
        blk.sync(dmab)

        def scb(sc):
            for icb in range(NICB):
                sc.wait_ge(s_pe.h, pe_k1_val(icb))
                if icb >= 2:
                    sc.wait_ge(s_tp.h, b_sub + icb - 1)
                    sc.wait_ge(s_a.h, b_att + 2 * (icb - 1))
                sc.activation(out=csb[icb % 2][:, :], in_=PS[5][0:64, 0:512],
                              func=ACTF.Identity).then_inc(s_b.h, 1)
                sc.wait_ge(s_g.h, b_min + g_min_val[icb])
                sc.activation(out=Esc[icb % 2][:, :], in_=Dsc[icb % 2][:, :],
                              func=ACTF.Exp).then_inc(s_ev.h, 1)
                sc.activation(out=xpcs[icb % 2][:, :], in_=csb[icb % 2][:, :],
                              func=ACTF.Exp).then_inc(s_ev.h, 1)
        blk.scalar(scb)

        def gpb(g):
            for g2 in range(2):
                g.wait_ge(s_w.h, b_upd + (g2 + 1) + g2 * 32)
                g.tensor_copy(state_b[:, :], state[:, :]).then_inc(s_g.h, 1)
                for sg in range(NSEG):
                    for hh in range(16):
                        icb = g2 * 64 + sg * 16 + hh
                        g.wait_ge(s_tp.h, b_sub + icb + 1)
                        g.tensor_single_scalar(out=Dsc[icb % 2][:, :], in_=Dsc[icb % 2][:, :],
                                               scalar=0.0, op=ALU.min).then_inc(s_g.h, 1)
                    for lp in range(4):
                        for par in range(2):
                            gc = g2 * 32 + (sg * 4 + lp) * 2 + par
                            g.wait_ge(s_w.h, upd_val(gc) + 1)
                            g.tensor_copy(state_b[:, :], state[:, :]).then_inc(s_g.h, 1)
        blk.gpsimd(gpb)

        def peb(te):
            icb = 0
            for g2 in range(2):
                H0 = g2 * 16
                for sg in range(NSEG):
                    gsg = g2 * NSEG + sg
                    for hh in range(16):
                        h = H0 + hh
                        if icb >= 1:
                            te.wait_ge(s_b.h, b_sb + icb)
                        te.matmul(PS[5][0:64, 0:512], bass.AP(idt32, h, [[128, 32], [0, 64]]),
                                  cs[0:32, sg * 512:(sg + 1) * 512],
                                  start=True, stop=True).then_inc(s_pe.h, 1)
                        icb += 1
                    for lp in range(4):
                        c2 = sg * 4 + lp
                        ipair = gsg * 4 + lp
                        slot = ipair % 2
                        te.wait_ge((s_v0 if slot == 0 else s_v1).h, 16 * (ipair // 2 + 1))
                        for j in range(32):
                            par, chk = divmod(j, 16)
                            g = ipair * 32 + j
                            if j >= 2:
                                te.wait_ge(s_cast.h, b_vt + 33 * ipair + j - 1)
                            elif ipair >= 1:
                                te.wait_ge(s_cast.h, b_vt + 33 * (ipair - 1) + 31 + j)
                            te.transpose(PT[g % 2][0:64, 0:P],
                                         vslab[slot][:, chk * P + par * 64:chk * P + par * 64 + 64],
                                         idt[:, :]).then_inc(s_pe.h, 1)
                        te.wait_ge(s_cast.h, b_vt + 33 * (ipair + 1))
                        te.wait_ge(s_a.h, b_att + gsg * 32 + 32)
                        if ipair >= 1:
                            te.wait_ge(s_cc.h, b_yev + 32 * ipair)
                        for par in range(2):
                            lc = lp * 2 + par
                            for hh2 in range(16):
                                bank = PS[par * 2 + (0 if hh2 < 8 else 1)]
                                te.matmul(bank[0:64, (hh2 % 8) * 64:(hh2 % 8) * 64 + 64],
                                          attT[:, hh2 * 512 + lc * 64:hh2 * 512 + lc * 64 + 64],
                                          v_tm[slot][:, par * 2048 + (H0 + hh2) * 64:par * 2048 + (H0 + hh2) * 64 + 64],
                                          start=(hh2 % 8 == 0), stop=False)
                        for par in range(2):
                            gc = g2 * 32 + 2 * c2 + par
                            lc = lp * 2 + par
                            for half in range(2):
                                if not (ipair == 0 and par == 0 and half == 0):
                                    te.wait_ge(s_st.h, b_sev + 4 * ipair + 2 * par + half)
                                last = None
                                for hh2 in range(half * 8, half * 8 + 8):
                                    last = te.matmul(PS[4][0:64, (hh2 % 8) * 64:(hh2 % 8) * 64 + 64],
                                                     ks[slot][:, par * 1024 + hh2 * 64:par * 1024 + hh2 * 64 + 64],
                                                     v_tm[slot][:, par * 2048 + (H0 + hh2) * 64:par * 2048 + (H0 + hh2) * 64 + 64],
                                                     start=(hh2 % 8 == 0), stop=(hh2 % 8 == 7))
                                last.then_inc(s_mm.h, 1)
                            te.wait_ge(s_g.h, b_min + mirror_val(gc))
                            for hh2 in range(16):
                                bank = PS[par * 2 + (0 if hh2 < 8 else 1)]
                                inst = te.matmul(bank[0:64, (hh2 % 8) * 64:(hh2 % 8) * 64 + 64],
                                                 qsc8[:, hh2 * 512 + lc * 64:hh2 * 512 + lc * 64 + 64],
                                                 state_b[:, hh2 * 64:hh2 * 64 + 64],
                                                 start=False, stop=(hh2 % 8 == 7))
                                if hh2 == 7 or hh2 == 15:
                                    inst.then_inc(s_mm.h, 1)
        blk.tensor(peb)

        def veb(v):
            icb = 0
            for g2 in range(2):
                H0 = g2 * 16
                if g2 == 1:
                    v.wait_ge(s_mm.h, b_ymm + 8 * 16)
                    v.wait_ge(s_g.h, b_min + g_cast_val[31])
                v.memset(state[:, :], 0.0).then_inc(s_w.h, 1)
                for sg in range(NSEG):
                    gsg = g2 * NSEG + sg
                    if gsg >= 1:
                        v.wait_ge(s_mm.h, b_ymm + 8 * (4 * gsg))
                    for hh in range(16):
                        h = H0 + hh
                        v.wait_ge(s_b.h, b_sb + icb + 1)
                        if icb >= 2:
                            v.wait_ge(s_ev.h, b_exp + 2 * (icb - 1))
                        v.tensor_tensor(out=Dsc[icb % 2][:, :], in0=csb[icb % 2][:, :],
                                        in1=bass.AP(csT2, (sg * 8) * 32 + h, [[1024, 64], [32, 8], [0, 64]]),
                                        op=ALU.subtract).then_inc(s_tp.h, 1)
                        v.wait_ge(s_ev.h, b_exp + icb * 2 + 1)
                        v.tensor_tensor(out=attT[:, hh * 512:(hh + 1) * 512],
                                        in0=QKm[:, sg * 512:(sg + 1) * 512],
                                        in1=Esc[icb % 2][:, :], op=ALU.mult)
                        v.wait_ge(s_ev.h, b_exp + icb * 2 + 2)
                        v.tensor_tensor(out=qsc8[:, hh * 512:(hh + 1) * 512],
                                        in0=q_f[:, sg * 512:(sg + 1) * 512],
                                        in1=xpcs[icb % 2][:, :], op=ALU.mult).then_inc(s_a.h, 2)
                        icb += 1
                    for lp in range(4):
                        c2 = sg * 4 + lp
                        ipair = gsg * 4 + lp
                        slot = ipair % 2
                        if ipair >= 2:
                            v.wait_ge(s_mm.h, b_ymm + 8 * (ipair - 1))
                        for j in range(32):
                            par, chk = divmod(j, 16)
                            g = ipair * 32 + j
                            v.wait_ge(s_pe.h, pe_tp_val(g))
                            v.tensor_copy(v_tm[slot][:, par * 2048 + chk * P:par * 2048 + (chk + 1) * P],
                                          PT[g % 2][0:64, 0:P]).then_inc(s_cast.h, 1)
                        last = None
                        for par in range(2):
                            c = 2 * c2 + par
                            last = v.tensor_tensor(
                                out=ks[slot][:, par * 1024:par * 1024 + 1024],
                                in0=bass.AP(k_tm, c * 64, [[2048, 64], [0, 16], [1, 64]]),
                                in1=bass.AP(wkT2, c * 32 + H0, [[1024, 64], [1, 16], [0, 64]]),
                                op=ALU.mult)
                        last.then_inc(s_cast.h, 1)
                        for par in range(2):
                            gc = g2 * 32 + 2 * c2 + par
                            for half in range(2):
                                v.wait_ge(s_mm.h, b_ymm + 8 * ipair + 4 * par + half + 1)
                                v.tensor_copy(states_st[:, par * 1024 + half * 512:par * 1024 + half * 512 + 512],
                                              PS[4][0:64, 0:512]).then_inc(s_st.h, 1)
                            v.wait_ge(s_mm.h, b_ymm + 8 * ipair + 4 * par + 4)
                            v.wait_ge(s_g.h, b_min + mirror_val(gc))
                            v.drain()
                            v.tensor_tensor(out=state[:, :], in0=state[:, :],
                                            in1=bass.AP(d_bc, H0 * 32 + (gc - g2 * 32), [[1024, 64], [32, 16], [0, 64]]),
                                            op=ALU.mult)
                            v.drain()
                            v.tensor_add(state[:, :], state[:, :],
                                         states_st[:, par * 1024:par * 1024 + 1024]).then_inc(s_w.h, 1)
                            v.drain()
                        if ipair >= 2:
                            v.wait_ge((s_y0 if slot == 0 else s_y1).h, 16 * (ipair // 2))
                        for par in range(2):
                            for hh2 in range(16):
                                bank = PS[par * 2 + (0 if hh2 < 8 else 1)]
                                v.scalar_tensor_tensor(
                                    out=y_stg[slot][:, par * 1024 + hh2 * 64:par * 1024 + hh2 * 64 + 64],
                                    in0=v_tm[slot][:, par * 2048 + (H0 + hh2) * 64:par * 2048 + (H0 + hh2) * 64 + 64],
                                    scalar=cDP[0:64, H0 + hh2:H0 + hh2 + 1],
                                    in1=bank[0:64, (hh2 % 8) * 64:(hh2 % 8) * 64 + 64],
                                    op0=ALU.mult, op1=ALU.add).then_inc(s_cc.h, 1)
        blk.vector(veb)

    if DEBUG == "ssd":
        with nc.Block() as blk:
            def _(e):
                b = s_gd.reserve(32)
                e.dma_start(out=bass.AP(DBG.tensor, 0, [[4160, 256], [1, 2048]]),
                            in_=bass.AP(pk_y.tensor, 0, [[2048, 256], [1, 2048]])).then_inc(s_gd.h, 16)
                e.dma_start(out=bass.AP(DBG.tensor, 300 * 4160, [[4160, 256], [1, 2048]]),
                            in_=bass.AP(pk_y.tensor, 1024 * 2048, [[2048, 256], [1, 2048]])).then_inc(s_gd.h, 16)
                e.wait_ge(s_gd.h, b + 32)
            blk.gpsimd(_)
        return nc

    pk_zm = nc.dram_tensor("pk_zm", [2048, D], BF16).ap()
    pk_op = nc.dram_tensor("pk_op", [2048, D], F32).ap()
    pk_op2 = nc.dram_tensor("pk_op2", [T, D], F32).ap()
    pk_op3 = nc.dram_tensor("pk_op3", [T, D], F32).ap()

    with nc.Block() as blk:
        b_b = s_b.reserve(16)

        def ld(e):
            for tt in range(16):
                if tt >= 2:
                    e.wait_ge(s_b.h, b_b + tt - 1)
                e.dma_start(out=vslab[tt % 2][:, :],
                            in_=bass.AP(pk_y.tensor, tt * P * D, [[D, P], [1, D]])
                            ).then_inc((s_z0 if tt % 2 == 0 else s_z1).h, 16)
        z0b, z1b = s_z0.n, s_z1.n
        s_z0.n += 8 * 16
        s_z1.n += 8 * 16
        blk.sync(ld)

        def sq(sc):
            for tt in range(16):
                sc.wait_ge((s_z0 if tt % 2 == 0 else s_z1).h, (z0b if tt % 2 == 0 else z1b) + (tt // 2 + 1) * 16)
                if tt > 0:
                    sc.drain()
                sc.activation(out=junk[:, 0:D], in_=vslab[tt % 2][:, :], func=ACTF.Square,
                              accum_out=colA[:, tt:tt + 1]).then_inc(s_b.h, 1)
        blk.scalar(sq)

        def pub(e):
            e.wait_ge(s_b.h, b_b + 16)
            for arm in e.switch_core_id(n=128):
                h2 = arm.logical % 2
                e.dma_start(out=bass.AP(sh_rms.tensor, h2 * 2048, [[1, P], [P, 16]]),
                            in_=colA[:, 0:16], allow_slow_non_contiguous=True).then_inc(s_g.h if False else s_gd.h, 16)
                e.dma_start(out=pk_zm[:, :],
                            in_=bass.AP(sh_z.tensor, h2 * D, [[4096, 2048], [1, D]])).then_inc(s_gd.h, 16)
            b2 = s_gd.reserve(32)
            e.wait_ge(s_gd.h, b2 + 32)
        blk.gpsimd(pub)
    with nc.Block() as blk:
        barrier(blk)

    ygT = OV.place("ygT", [P, 16 * 2048], BF16, BIG)
    yg = [OV.place(f"yg{i}", [P, D], BF16, WB0 + 65536 + i * 4096) for i in range(2)]
    z_t = [OV.place(f"z_t{i}", [P, D], BF16, WB0 + 65536 + 8192 + i * 4096) for i in range(2)]

    with nc.Block() as blk:
        b_ld = s_ld.reserve(16)

        def ld(e):
            for arm in e.switch_core_id(n=128):
                h2 = arm.logical % 2
                e.dma_start(out=colB[:, 0:16],
                            in_=bass.AP(sh_rms.tensor, (1 - h2) * 2048, [[1, P], [P, 16]]),
                            allow_slow_non_contiguous=True).then_inc(s_ld.h, 16)
        blk.sync(ld)

        def vv(v):
            b_a = s_a.reserve(1)
            v.wait_ge(s_ld.h, s_ld.n)
            v.tensor_add(colC[:, 0:16], colA[:, 0:16], colB[:, 0:16])
            v.drain()
            v.tensor_scalar(out=colC[:, 0:16], in0=colC[:, 0:16], scalar1=1.0 / 4096.0,
                            scalar2=None, op0=ALU.mult).then_inc(s_a.h, 1)
        blk.vector(vv)

        def sc1(sc):
            b_b = s_b.reserve(1)
            sc.wait_ge(s_a.h, s_a.n)
            sc.activation(out=colC[:, 0:16], in_=colC[:, 0:16], func=ACTF.Sqrt, bias=cst[:, 1:2]).then_inc(s_b.h, 1)
        blk.scalar(sc1)

        def v2(v):
            b_a = s_a.reserve(1)
            v.wait_ge(s_b.h, s_b.n)
            v.reciprocal(colD[:, 0:16], colC[:, 0:16]).then_inc(s_a.h, 1)
        blk.vector(v2)

    with nc.Block() as blk:
        b_b = s_b.reserve(16 * 2)
        b_a = s_a.reserve(16)
        b_pe = s_pe.reserve(16 * 16)
        b_tp = s_tp.reserve(16 * 16)
        yz0, yz1 = s_v0.n, s_v1.n
        s_v0.n += 8 * 16
        s_v1.n += 8 * 16
        zz0, zz1 = s_z0.n, s_z1.n
        s_z0.n += 8 * 16
        s_z1.n += 8 * 16

        def ld(e):
            for tt in range(16):
                sl = tt % 2
                if tt >= 2:
                    e.wait_ge(s_a.h, b_a + tt - 1)
                e.dma_start(out=vslab[sl][:, :],
                            in_=bass.AP(pk_y.tensor, tt * P * D, [[D, P], [1, D]])
                            ).then_inc((s_v0 if sl == 0 else s_v1).h, 16)
                if tt >= 2:
                    e.wait_ge(s_b.h, b_b + 2 * (tt - 2) + 1)
                e.dma_start(out=z_t[sl][:, :],
                            in_=bass.AP(pk_zm.tensor, tt * P * D, [[D, P], [1, D]])
                            ).then_inc((s_z0 if sl == 0 else s_z1).h, 16)
        blk.sync(ld)

        def scb(sc):
            for tt in range(16):
                sl = tt % 2
                sc.wait_ge((s_z0 if sl == 0 else s_z1).h, (zz0 if sl == 0 else zz1) + (tt // 2 + 1) * 16)
                if tt >= 2:
                    sc.drain()
                sc.activation(out=scr1[:, 0:D], in_=z_t[sl][:, :], func=ACTF.Silu).then_inc(s_b.h, 1)
                sc.wait_ge(s_a.h, b_a + tt + 1)
                if tt >= 2:
                    sc.wait_ge(s_tp.h, b_tp + (tt - 1) * 16)
                sc.activation(out=yg[sl][:, :], in_=scr2[:, 0:D], func=ACTF.Identity,
                              scale=colD[:, tt:tt + 1]).then_inc(s_b.h, 1)
        blk.scalar(scb)

        def peb(te):
            for tt in range(16):
                sl = tt % 2
                te.wait_ge(s_b.h, b_b + 2 * tt + 2)
                for kc in range(16):
                    i = tt * 16 + kc
                    if i >= 2:
                        te.wait_ge(s_tp.h, b_tp + i - 1)
                    te.transpose(PT[i % 2][0:P, 0:P], yg[sl][:, kc * P:(kc + 1) * P],
                                 idt[:, :]).then_inc(s_pe.h, 1)
        blk.tensor(peb)

        def evacs(v, tt):
            for kc in range(16):
                i = tt * 16 + kc
                v.wait_ge(s_pe.h, b_pe + i + 1)
                v.tensor_copy(ygT[:, kc * 2048 + tt * P:kc * 2048 + (tt + 1) * P],
                              PT[i % 2][0:P, 0:P]).then_inc(s_tp.h, 1)

        def vb(v):
            for tt in range(16):
                sl = tt % 2
                v.wait_ge((s_v0 if sl == 0 else s_v1).h, (yz0 if sl == 0 else yz1) + (tt // 2 + 1) * 16)
                v.wait_ge(s_b.h, b_b + 2 * tt + 1)
                v.tensor_tensor(out=scr2[:, 0:D], in0=vslab[sl][:, :], in1=scr1[:, 0:D],
                                op=ALU.mult).then_inc(s_a.h, 1)
                if tt >= 1:
                    evacs(v, tt - 1)
            evacs(v, 15)
        blk.vector(vb)

    op_st = [OV.place(f"op_st{i}", [P, 512], F32, WB0 + 65536 + 16384 + i * 2048) for i in range(4)]
    with nc.Block() as blk:
        b_w = s_w.reserve(4 * 16)
        b_cast = s_cast.reserve(4 * 16)
        b_mm = s_mm.reserve(4 * 16)
        b_ev = s_ev.reserve(4 * 16)
        b_ops = [_s.reserve(16 * 16) for _s in s_ops]

        def dmab(e):
            for ng in range(4):
                if ng >= 1:
                    e.wait_ge(s_cast.h, b_cast + ng * 16)
                e.dma_start(out=ws[:, 0:8192],
                            in_=bass.AP(Wop.tensor, ng * 512, [[D, P], [P * D, 16], [1, 512]])
                            ).then_inc(s_w.h, 16)
        blk.sync(dmab)

        def castb(sc):
            for ng in range(4):
                sc.wait_ge(s_w.h, b_w + (ng + 1) * 16)
                if ng >= 2:
                    sc.wait_ge(s_mm.h, b_mm + (ng - 1) * 16)
                for kc in range(16):
                    if ng > 0 or kc > 0:
                        sc.drain()
                    sc.activation(out=wb[ng % 2][:, kc * 512:(kc + 1) * 512], in_=ws[:, kc * 512:(kc + 1) * 512],
                                  func=ACTF.Identity, scale=nsc_sb[:, kc:kc + 1]).then_inc(s_cast.h, 1)
        blk.scalar(castb)

        def peb(te):
            for ng in range(4):
                te.wait_ge(s_cast.h, b_cast + (ng + 1) * 16)
                for tt in range(16):
                    i = ng * 16 + tt
                    bank = PS[tt % 4]
                    if i >= 4:
                        te.wait_ge(s_ev.h, b_ev + i - 3)
                    for kc in range(16):
                        te.matmul(bank[:, 0:512], ygT[:, kc * 2048 + tt * P:kc * 2048 + (tt + 1) * P],
                                  wb[ng % 2][:, kc * 512:(kc + 1) * 512],
                                  start=(kc == 0), stop=(kc == 15)).then_inc(s_mm.h, 1) if kc == 15 else \
                        te.matmul(bank[:, 0:512], ygT[:, kc * 2048 + tt * P:kc * 2048 + (tt + 1) * P],
                                  wb[ng % 2][:, kc * 512:(kc + 1) * 512],
                                  start=(kc == 0), stop=(kc == 15))
        blk.tensor(peb)

        def evb(v):
            for ng in range(4):
                for tt in range(16):
                    i = ng * 16 + tt
                    bank = PS[tt % 4]
                    v.wait_ge(s_mm.h, b_mm + i + 1)
                    if i >= 4:
                        v.wait_ge(s_ops[tt % 4].h, b_ops[tt % 4] + 16 * (i // 4))
                    v.tensor_copy(op_st[tt % 4][:, :], bank[:, 0:512]).then_inc(s_ev.h, 1)
        blk.vector(evb)

        def stb(e):
            for ng in range(4):
                for tt in range(16):
                    i = ng * 16 + tt
                    e.wait_ge(s_ev.h, b_ev + i + 1)
                    e.dma_start(out=bass.AP(pk_op.tensor, tt * P * D + ng * 512, [[D, P], [1, 512]]),
                                in_=op_st[tt % 4][:, :]).then_inc(s_ops[tt % 4].h, 16)
            for _k, _s in enumerate(s_ops):
                e.wait_ge(_s.h, b_ops[_k] + 16 * 16)
        blk.gpsimd(stb)

    if dbg_dump("gate", [(colA[:, 0:16], 0, 16), (colB[:, 0:16], 130, 16), (colD[:, 0:16], 260, 16),
                         (ygT[:, 0:2048], 390, 2048),
                         (bass.AP(pk_op.tensor, 0, [[D, 256], [1, D]]), 520, D),
                         (bass.AP(pk_y.tensor, 0, [[D, 128], [1, D]]), 800, D),
                         (bass.AP(pk_zm.tensor, 0, [[D, 128], [1, D]]), 950, D)]):
        return nc

    with nc.Block() as blk:
        def pub(e):
            b2 = s_gd.reserve(32)
            for arm in e.switch_core_id(n=128):
                h2 = arm.logical % 2
                e.dma_start(out=sh_op[h2 * T * D:(h2 * T + T) * D].rearrange("(a b) -> a b", b=D) if False else
                            bass.AP(sh_op.tensor, h2 * T * D, [[D, T], [1, D]]),
                            in_=bass.AP(pk_op.tensor, (1 - h2) * T * D, [[D, T], [1, D]])).then_inc(s_gd.h, 16)
                e.dma_start(out=pk_op3[:, :],
                            in_=bass.AP(pk_op.tensor, h2 * T * D, [[D, T], [1, D]])).then_inc(s_gd.h, 16)
            e.wait_ge(s_gd.h, b2 + 32)
        blk.gpsimd(pub)
    with nc.Block() as blk:
        barrier(blk)
    with nc.Block() as blk:
        def rd(e):
            b2 = s_gd.reserve(16)
            for arm in e.switch_core_id(n=128):
                h2 = arm.logical % 2
                e.dma_start(out=pk_op2[:, :],
                            in_=bass.AP(sh_op.tensor, (1 - h2) * T * D, [[D, T], [1, D]])).then_inc(s_gd.h, 16)
            e.wait_ge(s_gd.h, b2 + 16)
        blk.gpsimd(rd)

    if dbg_dump("cmb", [(bass.AP(pk_op2.tensor, 0, [[D, 256], [1, D]]), 0, D),
                        (bass.AP(pk_op3.tensor, 0, [[D, 256], [1, D]]), 300, D)]):
        return nc

    x_sb2 = OV.place("x_sb2", [P, NTT * D], F32, BIG)
    cmb = [OV.place(f"cmb{i}", [P, 2 * D], F32, WB0 + i * 16384) for i in range(2)]
    with nc.Block() as blk:
        b_ld = s_ld.reserve(8 * 3 * 16)
        b_a = s_a.reserve(8)

        def ld(e):
            for t in range(NTT):
                sl = t % 2
                if t >= 2:
                    e.wait_ge(s_a.h, b_a + t - 1)
                e.dma_start(out=x_sb2[:, t * D:(t + 1) * D], in_=X[t * P:(t + 1) * P, :]).then_inc(s_ld.h, 16)
                e.dma_start(out=cmb[sl][:, 0:D], in_=pk_op2[t * P:(t + 1) * P, :]).then_inc(s_ld.h, 16)
                e.dma_start(out=cmb[sl][:, D:2 * D], in_=pk_op3[t * P:(t + 1) * P, :]).then_inc(s_ld.h, 16)
                e.wait_ge(s_ld.h, b_ld + (t + 1) * 48)
        blk.sync(ld)

        def vb(v):
            for t in range(NTT):
                sl = t % 2
                v.wait_ge(s_ld.h, b_ld + (t + 1) * 48)
                v.tensor_add(x_sb2[:, t * D:(t + 1) * D], x_sb2[:, t * D:(t + 1) * D], cmb[sl][:, 0:D])
                v.drain()
                v.tensor_add(x_sb2[:, t * D:(t + 1) * D], x_sb2[:, t * D:(t + 1) * D],
                             cmb[sl][:, D:2 * D]).then_inc(s_a.h, 1)
        blk.vector(vb)

    if DEBUG == "postx" or STOP == "M1":
        with nc.Block() as blk:
            def _(e):
                b = s_st.reserve(NTT * 16)
                for t in range(NTT):
                    e.wait_ge(s_a.h, s_a.n - NTT + t + 1)
                    e.dma_start(out=OUT[t * P:(t + 1) * P, :], in_=x_sb2[:, t * D:(t + 1) * D]).then_inc(s_st.h, 16)
                e.wait_ge(s_st.h, b + NTT * 16)
            blk.sync(_)
        return nc

    hq_all = OV.place("hq_all", [P, NTT * D], BF16, 61440)
    idt32f = OV.place("idt32f", [P, P], F32, 208896)
    with nc.Block() as blk:
        emit_ln(blk, [x_sb2[:, t * D:(t + 1) * D] for t in range(NTT)], D, 1e-5)
        b_lna = s_a.n
        b_lnb = s_b.n
        b_a = s_a.reserve(8 * NTT)
        b_b = s_b.reserve(4 * NTT)

        def scb(sc):
            sc.wait_ge(s_a.h, b_lna)
            for t in range(NTT):
                xt = x_sb2[:, t * D:(t + 1) * D]
                if t >= 1:
                    sc.wait_ge(s_a.h, b_a + 8 * t)
                sc.activation(out=scr1[:, 0:D], in_=xt, func=ACTF.Identity,
                              bias=bcol[:, t:t + 1], scale=rcol[:, t:t + 1]).then_inc(s_b.h, 1)
                sc.drain()
                sc.activation(out=scr2[:, 0:D], in_=scr1[:, 0:D], func=ACTF.Square,
                              accum_out=qcol[:, t:t + 1]).then_inc(s_b.h, 1)
                sc.drain()
                sc.activation(out=qcol[:, t:t + 1], in_=qcol[:, t:t + 1], func=ACTF.Sqrt,
                              bias=1e-5, scale=1.0 / D).then_inc(s_b.h, 1)
                sc.wait_ge(s_a.h, b_a + 8 * t + 2)
                sc.drain()
                sc.activation(out=scr2[:, 0:D], in_=scr1[:, 0:D], func=ACTF.Identity,
                              scale=colB[:, t:t + 1]).then_inc(s_b.h, 1)
        blk.scalar(scb)

        def vb(v):
            tmp1 = qcol[:, 8:9]
            for t in range(NTT):
                xt = x_sb2[:, t * D:(t + 1) * D]
                v.wait_ge(s_b.h, b_lnb + 4 * t + 1)
                v.tensor_add(xt, xt, scr1[:, 0:D]).then_inc(s_a.h, 1)
                v.wait_ge(s_b.h, b_lnb + 4 * t + 3)
                v.drain()
                v.reciprocal(colB[:, t:t + 1], qcol[:, t:t + 1]).then_inc(s_a.h, 1)
                v.wait_ge(s_b.h, b_lnb + 4 * t + 4)
                v.drain()
                v.tensor_tensor(out=scr2[:, 0:D], in0=scr2[:, 0:D], in1=wnorm_bc[:, 0:D],
                                op=ALU.mult).then_inc(s_a.h, 1)
                v.drain()
                v.tensor_reduce(out=colA[:, t:t + 1], in_=scr2[:, 0:D], axis=AX.X,
                                op=ALU.max, apply_absolute_value=True).then_inc(s_a.h, 1)
                v.drain()
                v.tensor_scalar(out=colD[:, t:t + 1], in0=colA[:, t:t + 1], scalar1=1e-5,
                                scalar2=None, op0=ALU.max).then_inc(s_a.h, 1)
                v.drain()
                v.tensor_scalar(out=tmp1, in0=colD[:, t:t + 1], scalar1=1.0 / 127.0,
                                scalar2=None, op0=ALU.mult)
                v.drain()
                v.reciprocal(colC[:, t:t + 1], tmp1).then_inc(s_a.h, 1)
                v.drain()
                v.tensor_scalar(out=scr1[:, 0:D], in0=scr2[:, 0:D], scalar1=colC[:, t:t + 1],
                                scalar2=MAGIC, op0=ALU.mult, op1=ALU.add).then_inc(s_a.h, 1)
                v.drain()
                v.tensor_scalar(out=hq_all[:, t * D:(t + 1) * D], in0=scr1[:, 0:D],
                                scalar1=MAGIC, scalar2=None, op0=ALU.subtract).then_inc(s_a.h, 1)
        blk.vector(vb)

    with nc.Block() as blk:
        transpose_batch(blk, [(hq_all[:, (t * KC + j) * P:(t * KC + j) * P + P],
                               nT[:, j * T + t * P:j * T + t * P + P])
                              for t in range(NTT) for j in range(KC)])

        def park(e):
            b = s_st.reserve((NTT + 1) * 16)
            e.dma_start(out=idt32f[:, :], in_=IDT32[:, :]).then_inc(s_st.h, 16)
            for t in range(NTT):
                e.dma_start(out=pk_x[t * P:(t + 1) * P, :], in_=x_sb2[:, t * D:(t + 1) * D]).then_inc(s_st.h, 16)
            e.wait_ge(s_st.h, b + (NTT + 1) * 16)
        blk.sync(park)

    u2c1 = OV.place("u2c1", [P, 8 * T], BF16, 45056)
    u2c2 = OV.place("u2c2", [P, 16 * T], BF16, 61440)
    u2c3 = OV.place("u2c3", [P, 40 * T], BF16, 126976)

    def u2ap(hc):
        if hc < 8:
            return u2c1[:, hc * T:(hc + 1) * T]
        if hc < 24:
            return u2c2[:, (hc - 8) * T:(hc - 8 + 1) * T]
        return u2c3[:, (hc - 24) * T:(hc - 24 + 1) * T]

    def bcast_cols(blk, col_ap, dsts):
        b_a = s_a.reserve(1 + 8)
        b_pe = s_pe.reserve(1 + 8)

        def pe1(te):
            te.transpose(PS[4][0:8, 0:P], col_ap, idt32f[:, :]).then_inc(s_pe.h, 1)
            for t in range(8):
                te.wait_ge(s_a.h, b_a + max(1, t))
                te.matmul(PS[4 + (t + 1) % 2][:, 0:P], bass.AP(idt32f, t, [[P, 8], [0, P]]),
                          scr1[0:8, 0:P], start=True, stop=True).then_inc(s_pe.h, 1)
        blk.tensor(pe1)

        def v1(v):
            v.wait_ge(s_pe.h, b_pe + 1)
            v.tensor_copy(scr1[0:8, 0:P], PS[4][0:8, 0:P]).then_inc(s_a.h, 1)
            for t in range(8):
                v.wait_ge(s_pe.h, b_pe + 2 + t)
                v.drain()
                v.tensor_copy(dsts[t], PS[4 + (t + 1) % 2][:, 0:P]).then_inc(s_a.h, 1)
        blk.vector(v1)

    invbc = scr1[:, 1024:2048]
    with nc.Block() as blk:
        b_a0 = s_a.reserve(1)

        def v0(v):
            v.tensor_scalar(out=colD[:, 0:8], in0=colD[:, 0:8], scalar1=1.0 / 127.0,
                            scalar2=qsc[:, 0:1], op0=ALU.mult, op1=ALU.mult).then_inc(s_a.h, 1)
        blk.vector(v0)
    with nc.Block() as blk:
        bcast_cols(blk, colD[:, 0:8], [invbc[:, t * P:(t + 1) * P] for t in range(8)])

    sl_m1 = [scr1[:, 0:512], scr1[:, 512:1024]]
    sl_m3 = [scr2[:, 1024:1536], scr2[:, 1536:2048]]
    wacc = scr2[:, 0:1024]
    NGI = 128

    def wslot(s):
        return wb[0][:, (s % 4) * D:(s % 4) * D + D]

    with nc.Block() as blk:
        slot_sems = [s_z0, s_z1, s_x0, s_x1]
        b_z = [s.reserve(32 * 16) for s in slot_sems]
        b_mm = s_mm.reserve(2 * NGI)
        b_a = s_a.reserve(3 * NGI + 1)
        b_b = s_b.reserve(3 * NGI)
        b_g = s_g.reserve(NGI)

        def dmab(e):
            for pr in range(64):
                for j in range(2):
                    hc = pr + 64 * j
                    sidx = 2 * pr + j
                    if pr >= 2:
                        e.wait_ge(s_mm.h, b_mm + 2 * (2 * (pr - 2) + 1) + j + 1)
                    e.dma_start(out=wslot(sidx), in_=qap_gv(hc)).then_inc(
                        slot_sems[sidx % 4].h, 16)
        blk.sync(dmab)

        def peb(te):
            for i in range(NGI):
                pr, tg = i // 2, i % 2
                par = i % 2
                for j in range(2):
                    psum = PS[2 * par + j]
                    sidx = 2 * pr + j
                    te.wait_ge(slot_sems[sidx % 4].h, b_z[sidx % 4] + 16 * (sidx // 4 + 1))
                    if i >= 2:
                        if j == 0:
                            te.wait_ge(s_a.h, b_a + 3 * (i - 2) + 3)
                        else:
                            te.wait_ge(s_a.h, b_a + 3 * (i - 2) + 2)
                    for kc in range(KC):
                        mm = te.matmul(psum[:, 0:512], wslot(2 * pr + j)[:, kc * P:(kc + 1) * P],
                                       nT[:, kc * T + tg * 512:kc * T + tg * 512 + 512],
                                       start=(kc == 0), stop=(kc == KC - 1))
                    mm.then_inc(s_mm.h, 1)
        blk.tensor(peb)

        def vb(v):
            v.memset(wacc, 0.0).then_inc(s_a.h, 1)
            for i in range(NGI):
                pr, tg = i // 2, i % 2
                par = i % 2
                v.wait_ge(s_mm.h, b_mm + 2 * i + 2)
                if i >= 2:
                    v.wait_ge(s_b.h, b_b + 3 * (i - 2) + 3)
                v.drain()
                v.tensor_tensor(out=sl_m1[par], in0=PS[2 * par + 1][:, 0:512],
                                in1=invbc[:, tg * 512:tg * 512 + 512], op=ALU.mult).then_inc(s_a.h, 1)
                v.wait_ge(s_g.h, b_g + i + 1)
                v.drain()
                v.tensor_tensor(out=sl_m1[par], in0=PS[2 * par][:, 0:512], in1=sl_m3[par],
                                op=ALU.mult).then_inc(s_a.h, 1)
                v.wait_ge(s_b.h, b_b + 3 * i + 3)
                v.drain()
                v.tensor_tensor(out=wacc[:, tg * 512:tg * 512 + 512],
                                in0=wacc[:, tg * 512:tg * 512 + 512], in1=sl_m3[par],
                                op=ALU.max).then_inc(s_a.h, 1)
        blk.vector(vb)

        def gb(p):
            for i in range(NGI):
                pr, tg = i // 2, i % 2
                par = i % 2
                p.wait_ge(s_b.h, b_b + 3 * i + 1)
                if i >= 2:
                    p.wait_ge(s_a.h, b_a + 3 * (i - 2) + 4)
                p.drain()
                p.tensor_tensor(out=sl_m3[par], in0=sl_m1[par],
                                in1=invbc[:, tg * 512:tg * 512 + 512], op=ALU.mult).then_inc(s_g.h, 1)
        blk.gpsimd(gb)

        def scb(sc):
            for i in range(NGI):
                pr, tg = i // 2, i % 2
                par = i % 2
                sc.wait_ge(s_a.h, b_a + 3 * i + 1 + 1)
                sc.activation(out=sl_m1[par], in_=sl_m1[par], func=ACTF.Silu).then_inc(s_b.h, 1)
                sc.wait_ge(s_a.h, b_a + 3 * i + 2 + 1)
                sc.activation(out=u2ap(pr)[:, tg * 512:tg * 512 + 512], in_=sl_m1[par],
                              func=ACTF.Identity).then_inc(s_b.h, 1)
                sc.drain()
                sc.activation(out=sl_m3[par], in_=sl_m1[par], func=ACTF.Abs).then_inc(s_b.h, 1)
        blk.scalar(scb)

    comb2bc = scr2[:, 1024:2048]
    with nc.Block() as blk:
        b_pe = s_pe.reserve(8)
        b_a = s_a.reserve(8 + 3)

        def peb(te):
            for t in range(8):
                if t >= 2:
                    te.wait_ge(s_a.h, b_a + t - 1)
                te.transpose(PS[4 + t % 2][0:P, 0:P], wacc[:, t * P:(t + 1) * P],
                             idt32f[:, :]).then_inc(s_pe.h, 1)
        blk.tensor(peb)

        def vb(v):
            for t in range(8):
                v.wait_ge(s_pe.h, b_pe + t + 1)
                if t:
                    v.drain()
                v.tensor_reduce(out=colA[:, t:t + 1], in_=PS[4 + t % 2][0:P, 0:P],
                                axis=AX.X, op=ALU.max).then_inc(s_a.h, 1)
            v.drain()
            v.tensor_scalar(out=colA[:, 0:8], in0=colA[:, 0:8], scalar1=1e-5,
                            scalar2=None, op0=ALU.max).then_inc(s_a.h, 1)
            v.drain()
            v.tensor_scalar(out=mcol[:, 0:8], in0=colA[:, 0:8], scalar1=1.0 / 127.0,
                            scalar2=None, op0=ALU.mult)
            v.drain()
            v.reciprocal(colB[:, 0:8], mcol[:, 0:8]).then_inc(s_a.h, 1)
            v.drain()
            v.tensor_scalar(out=colC[:, 0:8], in0=colA[:, 0:8], scalar1=1.0 / 127.0,
                            scalar2=qsc[:, 1:2], op0=ALU.mult, op1=ALU.mult).then_inc(s_a.h, 1)
        blk.vector(vb)
    with nc.Block() as blk:
        bcast_cols(blk, colC[:, 0:8], [comb2bc[:, t * P:(t + 1) * P] for t in range(8)])
    with nc.Block() as blk:
        bcast_cols(blk, colB[:, 0:8], [invbc[:, t * P:(t + 1) * P] for t in range(8)])

    with nc.Block() as blk:
        cbuf = [scr2[:, 0:1024], scr1[:, 0:1024]]
        b_b = s_b.reserve(64)
        b_a = s_a.reserve(32)
        b_g = s_g.reserve(32)

        def scb(sc):
            for hc in range(64):
                par = hc % 2
                if hc >= 2:
                    sc.wait_ge([s_a, s_g][par].h, [b_a, b_g][par] + (hc - 2) // 2 + 1)
                sc.activation(out=cbuf[par], in_=u2ap(hc), func=ACTF.Identity).then_inc(s_b.h, 1)
        blk.scalar(scb)

        def qone(eng, hc, sem, base):
            par = hc % 2
            eng.wait_ge(s_b.h, b_b + hc + 1)
            eng.drain()
            eng.tensor_tensor(out=cbuf[par], in0=cbuf[par], in1=invbc, op=ALU.mult)
            eng.drain()
            eng.tensor_scalar(out=u2ap(hc), in0=cbuf[par], scalar1=MAGIC, scalar2=MAGIC,
                              op0=ALU.add, op1=ALU.subtract).then_inc(sem.h, 1)

        def vb(v):
            for hc in range(0, 64, 2):
                qone(v, hc, s_a, b_a)
        blk.vector(vb)

        def gb(p):
            for hc in range(1, 64, 2):
                qone(p, hc, s_g, b_g)
        blk.gpsimd(gb)

    with nc.Block() as blk:
        stg = [scr1[:, 0:512], scr1[:, 512:1024]]
        ostg = [scr1[:, 1024:2048], scr2[:, 0:1024]]
        b_z = [s_z0.reserve(8 * 16 * 16 // 2), s_z1.reserve(8 * 16 * 16 // 2)]
        b_mm = s_mm.reserve(2 * 16)
        b_a = s_a.reserve(10 * 16)
        b_pe2 = s_pe.reserve(8 * 16)
        b_st = s_st.reserve(16 * 16)

        def dmab(e):
            for dmc in range(16):
                for h in range(2):
                    s = 2 * dmc + h
                    if s >= 2:
                        e.wait_ge(s_mm.h, b_mm + s - 1)
                    for q in range(4):
                        e.dma_start(out=wb[0][:, (s % 2) * 4096 + q * 1024:(s % 2) * 4096 + (q + 1) * 1024],
                                    in_=qap_wo(dmc, 4 * h + q)).then_inc([s_z0, s_z1][s % 2].h, 16)
                if dmc >= 1:
                    e.wait_ge(s_a.h, b_a + 10 * (dmc - 1) + 10)
                    e.dma_start(out=bass.AP(pk_op2.tensor, (dmc - 1) * P,
                                            [[D, P], [P * D, 8], [1, P]]),
                                in_=ostg[(dmc - 1) % 2]).then_inc(s_st.h, 16)
            e.wait_ge(s_a.h, b_a + 10 * 16)
            e.dma_start(out=bass.AP(pk_op2.tensor, 15 * P, [[D, P], [P * D, 8], [1, P]]),
                        in_=ostg[15 % 2]).then_inc(s_st.h, 16)
            e.wait_ge(s_st.h, b_st + 16 * 16)
        blk.sync(dmab)

        def peb(te):
            for dmc in range(16):
                par = dmc % 2
                if dmc >= 2:
                    te.wait_ge(s_a.h, b_a + 10 * (dmc - 2) + 2)
                for h in range(2):
                    s = 2 * dmc + h
                    te.wait_ge([s_z0, s_z1][s % 2].h, b_z[s % 2] + 64 * (s // 2 + 1))
                    for th in range(2):
                        for kc in range(32 * h, 32 * h + 32):
                            mm = te.matmul(PS[2 * par + th][:, 0:512],
                                           wb[0][:, (s % 2) * 4096 + (kc - 32 * h) * P:(s % 2) * 4096 + (kc - 32 * h + 1) * P],
                                           u2ap(kc)[:, th * 512:th * 512 + 512],
                                           start=(kc == 0), stop=(kc == 63))
                    mm.then_inc(s_mm.h, 1)
                if dmc >= 1:
                    for jj in range(8):
                        te.wait_ge(s_a.h, b_a + 10 * (dmc - 1) + (1 if jj < 4 else 2))
                        if jj >= 2:
                            te.wait_ge(s_a.h, b_a + 10 * (dmc - 1) + jj + 1)
                        te.transpose(PS[4 + jj % 2][0:P, 0:P],
                                     stg[jj // 4][:, (jj % 4) * P:(jj % 4) * P + P],
                                     idt32f[:, :]).then_inc(s_pe.h, 1)
            for jj in range(8):
                te.wait_ge(s_a.h, b_a + 10 * 15 + (1 if jj < 4 else 2))
                if jj >= 2:
                    te.wait_ge(s_a.h, b_a + 10 * 15 + jj + 1)
                te.transpose(PS[4 + jj % 2][0:P, 0:P],
                             stg[jj // 4][:, (jj % 4) * P:(jj % 4) * P + P],
                             idt32f[:, :]).then_inc(s_pe.h, 1)
        blk.tensor(peb)

        def vb(v):
            for dmc in range(16):
                par = dmc % 2
                if dmc >= 1:
                    v.wait_ge(s_st.h, b_st + 16 * max(0, dmc - 1))
                    for jj in range(8):
                        v.wait_ge(s_pe.h, b_pe2 + 8 * (dmc - 1) + jj + 1)
                        v.drain()
                        v.tensor_copy(ostg[(dmc - 1) % 2][:, jj * P:(jj + 1) * P],
                                      PS[4 + jj % 2][0:P, 0:P]).then_inc(s_a.h, 1)
                for th in range(2):
                    v.wait_ge(s_mm.h, b_mm + 2 * dmc + 2)
                    if dmc >= 1:
                        v.wait_ge(s_pe.h, b_pe2 + 8 * (dmc - 1) + (4 if th == 0 else 8))
                    v.drain()
                    v.tensor_tensor(out=stg[th], in0=PS[2 * par + th][:, 0:512],
                                    in1=comb2bc[:, th * 512:th * 512 + 512],
                                    op=ALU.mult).then_inc(s_a.h, 1)
            v.wait_ge(s_st.h, b_st + 16 * 15)
            for jj in range(8):
                v.wait_ge(s_pe.h, b_pe2 + 8 * 15 + jj + 1)
                v.drain()
                v.tensor_copy(ostg[15 % 2][:, jj * P:(jj + 1) * P],
                              PS[4 + jj % 2][0:P, 0:P]).then_inc(s_a.h, 1)
        blk.vector(vb)

    with nc.Block() as blk:
        b_ld = s_ld.reserve(8 * 2 * 16)
        b_a = s_a.reserve(8)

        def ld(e):
            for t in range(NTT):
                sl = t % 2
                if t >= 2:
                    e.wait_ge(s_a.h, b_a + t - 1)
                e.dma_start(out=x_sb2[:, t * D:(t + 1) * D], in_=pk_x[t * P:(t + 1) * P, :]).then_inc(s_ld.h, 16)
                e.dma_start(out=cmb[sl][:, 0:D], in_=pk_op2[t * P:(t + 1) * P, :]).then_inc(s_ld.h, 16)
                e.wait_ge(s_ld.h, b_ld + (t + 1) * 32)
        blk.sync(ld)

        def vb(v):
            for t in range(NTT):
                sl = t % 2
                v.wait_ge(s_ld.h, b_ld + (t + 1) * 32)
                v.drain()
                v.tensor_add(x_sb2[:, t * D:(t + 1) * D], x_sb2[:, t * D:(t + 1) * D],
                             cmb[sl][:, 0:D]).then_inc(s_a.h, 1)
        blk.vector(vb)

    if DEBUG == "ffn" or STOP == "M2":
        with nc.Block() as blk:
            def _(e):
                b = s_st.reserve(NTT * 16)
                for t in range(NTT):
                    e.wait_ge(s_a.h, s_a.n - NTT + t + 1)
                    e.dma_start(out=OUT[t * P:(t + 1) * P, :], in_=x_sb2[:, t * D:(t + 1) * D]).then_inc(s_st.h, 16)
                e.wait_ge(s_st.h, b + NTT * 16)
            blk.sync(_)
        return nc


    import os as _os
    _CHK = _os.environ.get("KCHK", "")

    def chk(label):
        if not _CHK:
            return
        with nc.Block() as blk:
            sems = [(s_a, s_a.n), (s_b, s_b.n), (s_pe, s_pe.n), (s_mm, s_mm.n),
                    (s_tp, s_tp.n), (s_g, s_g.n), (s_ev, s_ev.n)]
            def _c(e, sems=sems):
                for s, n in sems:
                    e.wait_ge(s.h, n)
            blk.sync(_c)
        print("CHK", label, "pe", s_pe.n, "mm", s_mm.n, "tp", s_tp.n, "g", s_g.n, "ev", s_ev.n)
    n2b_all = OV.place("n2b_all", [P, NTT * D], BF16, 45056)
    mem_sb = OV.place("mem_sb", [P, 16 * 6 * P], BF16, 45056)
    mbs_a = OV.place("mbs_a", [P, 6 * D], BF16, 69632)
    mbs_b = OV.place("mbs_b", [P, 2 * D], BF16, 180224)
    mk_sb = OV.place("mk_sb", [P, 4 * NTT * 768], BF16, 126976)
    coef_cols = OV.place("coef_cols", [P, 128], BF16, 176128)
    v_sb = OV.place("v_sb", [P, NTT * 512], F32, 188416)
    mnT = OV.place("mnT", [P, 96], BF16, 204800)
    nnT = OV.place("nnT", [P, 96], BF16, 205056)
    memh_sb = OV.place("memh_sb", [P, NTT * 512], F32, 69632)
    sb_mkT = OV.place("sb_mkT", [P, 768], BF16, 20480)
    sb_pv = OV.place("sb_pv", [P, P], F32, 22016)
    sb_mbf = OV.place("sb_mbf", [P, P], F32, 22528)
    sb_mv = OV.place("sb_mv", [P, P], F32, 23040)
    bias_bc = OV.place("bias_bc", [P, 512], F32, 23552)
    bias_row = OV.place("bias_row", [1, 512], F32, 25600)
    dn_st = OV.place("dn_st", [P, 96], F32, 27648)
    dn_st2 = OV.place("dn_st2", [96, P], F32, 28032)
    dlt_bf = [OV.place("dlt_bf0", [P, 768], BF16, 20480),
              OV.place("dlt_bf1", [P, 768], BF16, 22016)]

    def mbs_ap(t):
        if t < 6:
            return mbs_a[:, t * D:(t + 1) * D]
        return mbs_b[:, (t - 6) * D:(t - 6 + 1) * D]

    def mk_ap(kvh):
        return mk_sb[:, kvh * NTT * 768:(kvh + 1) * NTT * 768]

    with nc.Block() as blk:
        emit_ln(blk, [x_sb2[:, t * D:(t + 1) * D] for t in range(NTT)], D, 1e-5)
        b_lna = s_a.n
        b_a = s_a.reserve(NTT)
        b_b = s_b.reserve(NTT)

        def scb(sc):
            sc.wait_ge(s_a.h, b_lna)
            for t in range(NTT):
                if t >= 1:
                    sc.wait_ge(s_a.h, b_a + t)
                sc.activation(out=scr1[:, 0:D], in_=x_sb2[:, t * D:(t + 1) * D], func=ACTF.Identity,
                              bias=bcol[:, t:t + 1], scale=rcol[:, t:t + 1]).then_inc(s_b.h, 1)
        blk.scalar(scb)

        def vb(v):
            for t in range(NTT):
                v.wait_ge(s_b.h, b_b + t + 1)
                v.drain()
                v.tensor_copy(n2b_all[:, t * D:(t + 1) * D], scr1[:, 0:D]).then_inc(s_a.h, 1)
        blk.vector(vb)

    with nc.Block() as blk:
        transpose_batch(blk, [(n2b_all[:, (t * KC + j) * P:(t * KC + j) * P + P],
                               nT[:, j * T + t * P:j * T + t * P + P])
                              for t in range(NTT) for j in range(KC)])

        def park(e):
            b = s_st.reserve(NTT * 16)
            for t in range(NTT):
                e.dma_start(out=pk_x[t * P:(t + 1) * P, :], in_=x_sb2[:, t * D:(t + 1) * D]).then_inc(s_st.h, 16)
            e.wait_ge(s_st.h, b + NTT * 16)
        blk.sync(park)

    chk("L1a")
    with nc.Block() as blk:
        b_gd = s_gd.reserve(16 * 16)
        b_ld = s_ld.reserve(16)
        b_pe = s_pe.reserve(6)
        b_a = s_a.reserve(6)

        def g1(e):
            for h in range(16):
                e.dma_start(out=mem_sb[:, h * 768:(h + 1) * 768],
                            in_=bass.AP(MEM.tensor, h * 768 * 128, [[128, P], [128 * 128, 6], [1, 128]])
                            ).then_inc(s_gd.h, 16)
            e.wait_ge(s_gd.h, b_gd + 16 * 16)
        blk.gpsimd(g1)

        def ld(e):
            e.dma_start(out=scr1[0:16, 0:768], in_=MNORM[:, :]).then_inc(s_ld.h, 16)
        blk.sync(ld)

        def pe1(te):
            te.wait_ge(s_ld.h, b_ld + 16)
            for fc in range(6):
                if fc >= 2:
                    te.wait_ge(s_a.h, b_a + fc - 1)
                te.transpose(PS[4 + fc % 2][0:P, 0:16], scr1[0:16, fc * P:(fc + 1) * P],
                             idt32f[0:16, 0:16]).then_inc(s_pe.h, 1)
        blk.tensor(pe1)

        def vb(v):
            for fc in range(6):
                v.wait_ge(s_pe.h, b_pe + fc + 1)
                v.drain()
                v.tensor_copy(mnT[:, fc * 16:(fc + 1) * 16], PS[4 + fc % 2][0:P, 0:16]).then_inc(s_a.h, 1)
        blk.vector(vb)

    def w_ap(wsrc, wcols, c0):
        return bass.AP(wsrc.tensor, c0, [[wcols, P], [P * wcols, KC], [1, 512]])

    def b_ap(bt, c0):
        return bass.AP(bt.tensor, c0, [[1, 1], [1, 512]])

    for phase, wsrc, wcols, bt, ngrp in [("mb", WMB, D, BMB, 4), ("v", WV, 512, BV, 1)]:
        for g in range(ngrp):
            with nc.Block() as blk:
                b_gd = s_gd.reserve(16)
                b_ld = s_ld.reserve(16)
                b_pe = s_pe.reserve(1)
                b_mm = s_mm.reserve(NTT)
                b_a = s_a.reserve(1 + NTT)
                b_b = s_b.reserve(NTT if phase == "mb" else 0)

                def g1(e, wsrc=wsrc, wcols=wcols, g=g):
                    e.dma_start(out=wb[0][:, 0:8192], in_=w_ap(wsrc, wcols, g * 512)).then_inc(s_gd.h, 16)
                blk.gpsimd(g1)

                def ld(e, bt=bt, g=g):
                    e.dma_start(out=bias_row[0:1, 0:512], in_=b_ap(bt, g * 512)).then_inc(s_ld.h, 16)
                blk.sync(ld)

                def pe1(te, b_gd=b_gd, b_ld=b_ld, b_a=b_a, b_pe=b_pe):
                    te.wait_ge(s_ld.h, b_ld + 16)
                    te.matmul(PS[4][:, 0:512], ones_row[0:1, 0:P], bias_row[0:1, 0:512],
                              start=True, stop=True).then_inc(s_pe.h, 1)
                    te.wait_ge(s_gd.h, b_gd + 16)
                    for t in range(NTT):
                        if t >= 4:
                            te.wait_ge(s_a.h, b_a + 1 + (t - 4) + 1)
                        mm = None
                        for kc in range(KC):
                            mm = te.matmul(PS[t % 4][:, 0:512], nT[:, kc * T + t * P:kc * T + (t + 1) * P],
                                           wb[0][:, kc * 512:(kc + 1) * 512],
                                           start=(kc == 0), stop=(kc == KC - 1))
                        mm.then_inc(s_mm.h, 1)
                blk.tensor(pe1)

                def vb(v, b_pe=b_pe, b_mm=b_mm, b_a=b_a, b_b=b_b, phase=phase):
                    v.wait_ge(s_pe.h, b_pe + 1)
                    v.tensor_copy(bias_bc[:, 0:512], PS[4][:, 0:512]).then_inc(s_a.h, 1)
                    for t in range(NTT):
                        v.wait_ge(s_mm.h, b_mm + t + 1)
                        if phase == "mb" and t >= 1:
                            v.wait_ge(s_b.h, b_b + t)
                        v.drain()
                        dst = v_sb[:, t * 512:(t + 1) * 512] if phase == "v" else scr1[:, 0:512]
                        v.tensor_tensor(out=dst, in0=PS[t % 4][:, 0:512],
                                        in1=bias_bc[:, 0:512], op=ALU.add).then_inc(s_a.h, 1)
                blk.vector(vb)

                if phase == "mb":
                    def scb(sc, b_a=b_a, b_b=b_b, g=g):
                        for t in range(NTT):
                            sc.wait_ge(s_a.h, b_a + 1 + t + 1)
                            sc.activation(out=mbs_ap(t)[:, g * 512:(g + 1) * 512], in_=scr1[:, 0:512],
                                          func=ACTF.Sigmoid).then_inc(s_b.h, 1)
                    blk.scalar(scb)

    chk("mbv")
    def dpfp_block(wsrc, bt, heads_per_group, g, tag):
        with nc.Block() as blk:
            b_gd = s_gd.reserve(16)
            b_ld = s_ld.reserve(16)
            b_a0 = s_a.reserve(1)
            NA, NB, NPE, NTP = 33, 48, 28, 24
            if tag == "q":
                NA, NB, NPE, NTP = 17, 20, 28, 24
            b_a = s_a.reserve(NA * NTT)
            b_b = s_b.reserve(NB * NTT)
            b_pe = s_pe.reserve(1 + NPE * NTT)
            b_mm = s_mm.reserve(NTT)
            b_tp = s_tp.reserve(NTP * NTT)

            def g1(e):
                e.dma_start(out=wb[0][:, 0:8192],
                            in_=w_ap(wsrc, 512 if tag == "k" else D, g * 512)).then_inc(s_gd.h, 16)
            blk.gpsimd(g1)

            def ld(e):
                e.dma_start(out=bias_row[0:1, 0:512], in_=b_ap(bt, g * 512)).then_inc(s_ld.h, 16)
            blk.sync(ld)

            def pe1(te):
                te.wait_ge(s_ld.h, b_ld + 16)
                te.matmul(PS[4][:, 0:512], ones_row[0:1, 0:P], bias_row[0:1, 0:512],
                          start=True, stop=True).then_inc(s_pe.h, 1)
                te.wait_ge(s_gd.h, b_gd + 16)
                for t in range(NTT):
                    if t >= 2:
                        te.wait_ge(s_a.h, b_a + NA * (t - 2) + 1)
                        te.wait_ge(s_a.h, b_a + NA * (t - 1))
                    mm = None
                    for kc in range(KC):
                        mm = te.matmul(PS[t % 2][:, 0:512], nT[:, kc * T + t * P:kc * T + (t + 1) * P],
                                       wb[0][:, kc * 512:(kc + 1) * 512],
                                       start=(kc == 0), stop=(kc == KC - 1))
                    mm.then_inc(s_mm.h, 1)
                    for sh in range(4):
                        te.wait_ge(s_b.h, b_b + NB * t + (12 if tag == "k" else 5) * sh + 4)
                        for fc in range(6):
                            gti = t * NTP + sh * 6 + fc
                            if gti >= 2:
                                te.wait_ge(s_tp.h, b_tp + gti - 1)
                            te.transpose(PT[fc % 2][0:P, 0:P],
                                         (mk_ap(sh)[:, t * 768 + fc * P:t * 768 + (fc + 1) * P]
                                          if tag == "k" else sb_mkT2[:, fc * P:(fc + 1) * P]),
                                         idt[:, :]).then_inc(s_pe.h, 1)
                        te.wait_ge(s_tp.h, b_tp + t * NTP + sh * 6 + 6)
                        if t == 0 and sh == 0:
                            te.wait_ge(s_a.h, b_a0 + 1)
                        if tag == "k":
                            if sh > 0:
                                te.wait_ge(s_b.h, b_b + NB * t + 12 * (sh - 1) + 11)
                            elif t >= 2:
                                te.wait_ge(s_b.h, b_b + NB * (t - 2) + 12 * 3 + 11)
                        else:
                            if sh > 0:
                                te.wait_ge(s_b.h, b_b + NB * t + 5 * (sh - 1) + 5)
                            elif t >= 2:
                                te.wait_ge(s_b.h, b_b + NB * (t - 2) + 5 * 3 + 5)
                        if sh > 0:
                            te.wait_ge(s_a.h, b_a + NA * t + 1 + (8 if tag == "k" else 4) * (sh - 1)
                                       + (3 if tag == "k" else 4))
                        last = None
                        for fc in range(6):
                            if tag == "k":
                                last = te.matmul(PS[4 + t % 2][:, 0:4],
                                                 sb_mkT[:, fc * P:(fc + 1) * P],
                                                 mnT[:, fc * 16 + sh * 4:fc * 16 + sh * 4 + 4],
                                                 start=(fc == 0), stop=(fc == 5))
                            else:
                                h = 4 * g + sh
                                last = te.matmul(PS[4 + t % 2][:, 0:1],
                                                 sb_mkT[:, fc * P:(fc + 1) * P],
                                                 nnT[:, fc * 16 + h:fc * 16 + h + 1],
                                                 start=(fc == 0), stop=(fc == 5))
                        for fc in range(6):
                            if tag == "k":
                                for hq in range(4):
                                    h = sh * 4 + hq
                                    last = te.matmul(PS[2 + t % 2][:, hq * P:(hq + 1) * P],
                                                     sb_mkT[:, fc * P:(fc + 1) * P],
                                                     mem_sb[:, (h * 6 + fc) * P:(h * 6 + fc + 1) * P],
                                                     start=(fc == 0 and hq == 0),
                                                     stop=(fc == 5 and hq == 3))
                            else:
                                h = 4 * g + sh
                                last = te.matmul(PS[2 + t % 2][:, sh * P:(sh + 1) * P],
                                                 sb_mkT[:, fc * P:(fc + 1) * P],
                                                 mem_sb[:, (h * 6 + fc) * P:(h * 6 + fc + 1) * P],
                                                 start=(fc == 0), stop=(fc == 5))
                        last.then_inc(s_pe.h, 1)
            blk.tensor(pe1)

            def vb(v):
                v.wait_ge(s_pe.h, b_pe + 1)
                v.tensor_copy(bias_bc[:, 0:512], PS[4][:, 0:512]).then_inc(s_a.h, 1)
                for t in range(NTT):
                    S0 = b_a + NA * t
                    B0 = b_b + NB * t
                    v.wait_ge(s_mm.h, b_mm + t + 1)
                    if t >= 1:
                        v.wait_ge(s_b.h, b_b + NB * (t - 1) + (12 if tag == "k" else 5) * 3 + 1)
                    v.drain()
                    v.tensor_tensor(out=scr1[:, 0:512], in0=PS[t % 2][:, 0:512],
                                    in1=bias_bc[:, 0:512], op=ALU.add).then_inc(s_a.h, 1)
                    for sh in range(4):
                        NBS = 12 if tag == "k" else 5
                        SB = B0 + NBS * sh
                        SS = S0 + 1 + (8 if tag == "k" else 4) * sh
                        v.wait_ge(s_b.h, SB + 1)
                        for j in (1, 2, 3):
                            v.drain()
                            v.tensor_tensor(out=scr1[:, 1024 + (j - 1) * 256 + j:1024 + j * 256],
                                            in0=scr1[:, 512 + j:768],
                                            in1=scr1[:, 512:768 - j], op=ALU.mult)
                            v.drain()
                            last = v.tensor_tensor(out=scr1[:, 1024 + (j - 1) * 256:1024 + (j - 1) * 256 + j],
                                                   in0=scr1[:, 512:512 + j],
                                                   in1=scr1[:, 768 - j:768], op=ALU.mult)
                        last.then_inc(s_a.h, 1)
                        v.wait_ge(s_b.h, SB + 3)
                        v.drain()
                        v.tensor_scalar(out=qcol[:, 2:3], in0=qcol[:, 1:2], scalar1=1e-12,
                                        scalar2=None, op0=ALU.max)
                        v.drain()
                        v.reciprocal(qcol[:, 3:4], qcol[:, 2:3]).then_inc(s_a.h, 1)
                        v.drain()
                        v.tensor_scalar(out=qcol[:, 4:5], in0=qcol[:, 0:1], scalar1=qcol[:, 3:4],
                                        scalar2=None, op0=ALU.mult)
                        v.drain()
                        v.tensor_scalar(out=qcol[:, 5:6], in0=qcol[:, 4:5], scalar1=qcol[:, 3:4],
                                        scalar2=None, op0=ALU.mult)
                        v.drain()
                        v.reciprocal(qcol[:, 6:7], qcol[:, 5:6])
                        for fc in range(6):
                            v.wait_ge(s_pe.h, b_pe + 1 + t * NPE + sh * 7 + fc + 1)
                            v.drain()
                            v.tensor_copy(sb_mkT[:, fc * P:(fc + 1) * P], PT[fc % 2][0:P, 0:P]).then_inc(s_tp.h, 1)
                        v.wait_ge(s_pe.h, b_pe + 1 + t * NPE + sh * 7 + 7)
                        v.drain()
                        if tag == "k":
                            v.tensor_scalar(out=colA[:, 0:4], in0=PS[4 + t % 2][:, 0:4],
                                            scalar1=1e-8, scalar2=None, op0=ALU.add)
                            v.drain()
                            v.reciprocal(colA[:, 4:8], colA[:, 0:4]).then_inc(s_a.h, 1)
                            v.drain()
                            v.tensor_scalar(out=colB[:, 0:4], in0=colA[:, 0:4], scalar1=qcol[:, 6:7],
                                            scalar2=-1.0, op0=ALU.mult, op1=ALU.mult)
                            v.drain()
                            v.tensor_scalar(out=colB[:, 4:8], in0=colB[:, 0:4], scalar1=1.0,
                                            scalar2=0.0, op0=ALU.add, op1=ALU.max)
                            v.drain()
                            v.tensor_scalar(out=coef_cols[:, t * 16 + sh * 4:t * 16 + sh * 4 + 4],
                                            in0=colB[:, 4:8], scalar1=1.0, scalar2=None,
                                            op0=ALU.min).then_inc(s_a.h, 1)
                            for hq in range(4):
                                v.wait_ge(s_b.h, SB + 5 + 2 * hq)
                                v.drain()
                                v.tensor_tensor(out=sb_mv[:, :], in0=v_sb[:, t * 512 + sh * P:t * 512 + (sh + 1) * P],
                                                in1=sb_pv[:, :], op=ALU.subtract)
                                v.wait_ge(s_b.h, SB + 6 + 2 * hq)
                                v.drain()
                                v.tensor_tensor(out=mbs_ap(t)[:, (sh * 4 + hq) * P:(sh * 4 + hq + 1) * P],
                                                in0=sb_mv[:, :], in1=sb_mbf[:, :],
                                                op=ALU.mult).then_inc(s_a.h, 1)
                        else:
                            v.tensor_scalar(out=colA[:, sh:sh + 1], in0=PS[4 + t % 2][:, 0:1],
                                            scalar1=1e-8, scalar2=None, op0=ALU.add).then_inc(s_a.h, 1)
                            v.drain()
                            v.reciprocal(colA[:, 4 + sh:5 + sh], colA[:, sh:sh + 1]).then_inc(s_a.h, 1)
            blk.vector(vb)

            def scb(sc):
                for t in range(NTT):
                    S0 = b_a + NA * t
                    B0 = b_b + NB * t
                    for sh in range(4):
                        NBS = 12 if tag == "k" else 5
                        SB = B0 + NBS * sh
                        SS = S0 + 1 + (8 if tag == "k" else 4) * sh
                        sc.wait_ge(s_a.h, S0 + 1)
                        if sh > 0:
                            sc.wait_ge(s_a.h, SS - (8 if tag == "k" else 4) + 1)
                        sc.drain()
                        sc.activation(out=scr1[:, 512:640], in_=scr1[:, sh * P:(sh + 1) * P],
                                      func=ACTF.Relu)
                        sc.drain()
                        sc.activation(out=scr1[:, 640:768], in_=scr1[:, sh * P:(sh + 1) * P],
                                      func=ACTF.Relu, scale=-1.0).then_inc(s_b.h, 1)
                        sc.wait_ge(s_a.h, SS + 1)
                        sc.drain()
                        sc.activation(out=junk[:, 0:768], in_=scr1[:, 1024:1792], func=ACTF.Square,
                                      accum_out=qcol[:, 0:1]).then_inc(s_b.h, 1)
                        sc.drain()
                        sc.activation(out=qcol[:, 1:2], in_=qcol[:, 0:1], func=ACTF.Sqrt).then_inc(s_b.h, 1)
                        sc.wait_ge(s_a.h, SS + 2)
                        if tag == "q" and (sh > 0 or t > 0):
                            pidx = (t * 28 + 7 * (sh - 1) + 6) if sh > 0 else ((t - 1) * 28 + 27)
                            sc.wait_ge(s_pe.h, b_pe + 1 + pidx)
                        sc.drain()
                        sc.activation(out=(mk_ap(sh)[:, t * 768:(t + 1) * 768] if tag == "k"
                                           else sb_mkT2[:, 0:768]),
                                      in_=scr1[:, 1024:1792], func=ACTF.Identity,
                                      scale=qcol[:, 3:4]).then_inc(s_b.h, 1)
                        if tag == "k":
                            sc.wait_ge(s_a.h, SS + 3)
                            for hq in range(4):
                                sc.drain()
                                sc.activation(out=sb_pv[:, :], in_=PS[2 + t % 2][:, hq * P:(hq + 1) * P],
                                              func=ACTF.Identity, scale=colA[:, 4 + hq:5 + hq]).then_inc(s_b.h, 1)
                                sc.drain()
                                sc.activation(out=sb_mbf[:, :],
                                              in_=mbs_ap(t)[:, (sh * 4 + hq) * P:(sh * 4 + hq + 1) * P],
                                              func=ACTF.Identity).then_inc(s_b.h, 1)
                                sc.wait_ge(s_a.h, SS + 5 + hq)
                        else:
                            sc.wait_ge(s_a.h, SS + 4)
                            sc.drain()
                            sc.activation(out=memh_sb[:, t * 512 + sh * P:t * 512 + (sh + 1) * P],
                                          in_=PS[2 + t % 2][:, sh * P:(sh + 1) * P],
                                          func=ACTF.Identity, scale=colA[:, 4 + sh:5 + sh]).then_inc(s_b.h, 1)
            blk.scalar(scb)
        return

    sb_mkT2 = OV.place("sb_mkT2", [P, 768], BF16, 26112)
    dpfp_block(WK, BK, 4, 0, "k")

    if dbg_dump("l1k", [(mk_sb[:, 0:2048], 0, 2048), (mbs_a[:, 0:2048], 130, 2048),
                        (coef_cols[:, 0:128], 260, 128), (v_sb[:, 0:2048], 390, 2048)]):
        return nc

    chk("k")
    with nc.Block() as blk:
        b_mm = s_mm.reserve(96 + 2)
        b_ev = s_ev.reserve(96 + 2)
        b_gd = s_gd.reserve(16)

        def pe1(te):
            i = 0
            for h in range(16):
                kvh = h // 4
                for fc in range(6):
                    if i >= 2:
                        te.wait_ge(s_ev.h, b_ev + i - 1)
                    last = None
                    for tt in range(NTT):
                        last = te.matmul(PS[i % 2][:, 0:P],
                                         mk_ap(kvh)[:, tt * 768 + fc * P:tt * 768 + (fc + 1) * P],
                                         mbs_ap(tt)[:, h * P:(h + 1) * P],
                                         start=(tt == 0), stop=(tt == NTT - 1))
                    last.then_inc(s_mm.h, 1)
                    i += 1
            last = None
            for tt in range(NTT):
                for h in range(16):
                    kvh = h // 4
                    for fc in range(6):
                        last = te.matmul(PS[2][:, h * 6 + fc:h * 6 + fc + 1],
                                         mk_ap(kvh)[:, tt * 768 + fc * P:tt * 768 + (fc + 1) * P],
                                         coef_cols[:, tt * 16 + h:tt * 16 + h + 1],
                                         start=(tt == 0 and h == 0 and fc == 0),
                                         stop=(tt == NTT - 1 and h == 15 and fc == 5))
            last.then_inc(s_mm.h, 1)
            te.wait_ge(s_ev.h, b_ev + 97)
            te.transpose(PS[3][0:96, 0:P], dn_st[:, 0:96], idt32f[:, :]).then_inc(s_mm.h, 1)
        blk.tensor(pe1)

        b_sp4 = [s_ops[2].reserve(48 * 16), s_ops[3].reserve(48 * 16)]

        def vb(v):
            i = 0
            for h in range(16):
                for fc in range(6):
                    v.wait_ge(s_mm.h, b_mm + i + 1)
                    if i >= 2:
                        v.wait_ge(s_ops[2 + i % 2].h, b_sp4[i % 2] + (i // 2) * 16)
                    v.drain()
                    v.tensor_copy(scr2[:, (i % 2) * 1024:(i % 2) * 1024 + P], PS[i % 2][:, 0:P]).then_inc(s_ev.h, 1)
                    i += 1
            v.wait_ge(s_mm.h, b_mm + 97)
            v.drain()
            v.tensor_copy(dn_st[:, 0:96], PS[2][:, 0:96]).then_inc(s_ev.h, 1)
            v.wait_ge(s_mm.h, b_mm + 98)
            v.drain()
            v.tensor_copy(dn_st2[0:96, :], PS[3][0:96, 0:P]).then_inc(s_ev.h, 1)
        blk.vector(vb)

        def g1(e):
            i = 0
            for h in range(16):
                for fc in range(6):
                    e.wait_ge(s_ev.h, b_ev + i + 1)
                    e.dma_start(out=bass.AP(dm_in, h * 98304 + fc * 128 * 128, [[128, P], [1, P]]),
                                in_=scr2[:, (i % 2) * 1024:(i % 2) * 1024 + P]).then_inc(s_ops[2 + i % 2].h, 16)
                    i += 1
            e.wait_ge(s_ev.h, b_ev + 98)
            e.dma_start(out=bass.AP(dm_in, 16 * 98304, [[P, 96], [1, P]]),
                        in_=dn_st2[0:96, :]).then_inc(s_gd.h, 16)
            e.wait_ge(s_gd.h, b_gd + 16)
            e.wait_ge(s_ops[2].h, b_sp4[0] + 48 * 16)
            e.wait_ge(s_ops[3].h, b_sp4[1] + 48 * 16)
        blk.gpsimd(g1)

    chk("c4")
    with nc.Block() as blk:
        b_cc = s_cc.reserve(1)
        b_gd = s_gd.reserve(16 * 16)
        b_ld = s_ld.reserve(2 * 16)
        b_a = s_a.reserve(16 + 1)
        b_pe = s_pe.reserve(6)
        b_tp = s_tp.reserve(6)

        def g1(e):
            e.collective_compute("AllReduce", ALU.add, replica_groups=[list(range(8))],
                                 ins=[dm_in.ap()], outs=[dm_out.ap()]).then_inc(s_cc.h, 1)
            e.wait_ge(s_cc.h, b_cc + 1)
            for h in range(16):
                if h >= 2:
                    e.wait_ge(s_a.h, b_a + h - 1)
                e.dma_start(out=dlt_bf[h % 2][:, :],
                            in_=bass.AP(dm_out, h * 98304, [[128, P], [128 * 128, 6], [1, 128]])
                            ).then_inc(s_gd.h, 16)
                e.wait_ge(s_gd.h, b_gd + (h + 1) * 16)
        blk.gpsimd(g1)

        def vb(v):
            for h in range(16):
                v.wait_ge(s_gd.h, b_gd + (h + 1) * 16)
                v.drain()
                v.tensor_tensor(out=mem_sb[:, h * 768:(h + 1) * 768],
                                in0=mem_sb[:, h * 768:(h + 1) * 768],
                                in1=dlt_bf[h % 2][:, :], op=ALU.add).then_inc(s_a.h, 1)
            v.wait_ge(s_ld.h, b_ld + 32)
            v.drain()
            v.tensor_add(scr1[0:16, 0:768], scr1[0:16, 0:768], scr1[0:16, 1024:1792]).then_inc(s_a.h, 1)
        blk.vector(vb)

        def ld(e):
            e.dma_start(out=scr1[0:16, 0:768], in_=MNORM[:, :]).then_inc(s_ld.h, 16)
            e.wait_ge(s_ld.h, b_ld + 16)
            e.wait_ge(s_cc.h, b_cc + 1)
            e.dma_start(out=scr1[0:16, 1024:1792],
                        in_=bass.AP(dm_out, 16 * 98304, [[768, 16], [1, 768]])).then_inc(s_ld.h, 16)
            e.wait_ge(s_ld.h, b_ld + 32)
        blk.sync(ld)

        def pe1(te):
            te.wait_ge(s_a.h, b_a + 17)
            for fc in range(6):
                if fc >= 2:
                    te.wait_ge(s_tp.h, b_tp + fc - 1)
                te.transpose(PS[4 + fc % 2][0:P, 0:16], scr1[0:16, fc * P:(fc + 1) * P],
                             idt32f[0:16, 0:16]).then_inc(s_pe.h, 1)
        blk.tensor(pe1)

        def vb2(v):
            for fc in range(6):
                v.wait_ge(s_pe.h, b_pe + fc + 1)
                v.drain()
                v.tensor_copy(nnT[:, fc * 16:(fc + 1) * 16], PS[4 + fc % 2][0:P, 0:16]).then_inc(s_tp.h, 1)
        blk.vector(vb2)

    if dbg_dump("l1m", [(mem_sb[:, 0:2048], 0, 2048), (nnT[:, 0:96], 130, 96)]):
        return nc

    chk("c5")
    xacc = x_sb2
    with nc.Block() as blk:
        b_ld = s_ld.reserve(NTT * 16)
        b_a = s_a.reserve(NTT)
        b_b = s_b.reserve(NTT)

        def ld(e):
            for t in range(NTT):
                if t >= 2:
                    e.wait_ge(s_a.h, b_a + t - 1)
                e.dma_start(out=xacc[:, t * D:(t + 1) * D], in_=pk_x[t * P:(t + 1) * P, :]).then_inc(s_ld.h, 16)
                e.wait_ge(s_ld.h, b_ld + (t + 1) * 16)
        blk.sync(ld)

        def scb(sc):
            for t in range(NTT):
                sc.wait_ge(s_ld.h, b_ld + (t + 1) * 16)
                if t >= 1:
                    sc.wait_ge(s_a.h, b_a + t)
                sc.drain()
                sc.activation(out=scr2[:, 0:D], in_=xacc[:, t * D:(t + 1) * D], func=ACTF.Identity,
                              bias=bcol[:, t:t + 1], scale=rcol[:, t:t + 1]).then_inc(s_b.h, 1)
        blk.scalar(scb)

        def vb(v):
            for t in range(NTT):
                v.wait_ge(s_b.h, b_b + t + 1)
                v.drain()
                v.tensor_add(xacc[:, t * D:(t + 1) * D], xacc[:, t * D:(t + 1) * D],
                             scr2[:, 0:D]).then_inc(s_a.h, 1)
        blk.vector(vb)

    chk("b6pre")
    for g in range(4):
        dpfp_block(WQ, BQ, 4, g, "q")
        chk(f"q{g}")
        with nc.Block() as blk:
            b_gd = s_gd.reserve(16)
            b_ld = s_ld.reserve(16)
            b_pe = s_pe.reserve(1)
            b_mm = s_mm.reserve(NTT)
            b_a = s_a.reserve(1 + 2 * NTT)
            b_b = s_b.reserve(NTT)

            def g1(e, g=g):
                e.dma_start(out=wb[0][:, 0:8192], in_=w_ap(WG, D, g * 512)).then_inc(s_gd.h, 16)
            blk.gpsimd(g1)

            def ld(e, g=g):
                e.dma_start(out=bias_row[0:1, 0:512], in_=b_ap(BG, g * 512)).then_inc(s_ld.h, 16)
            blk.sync(ld)

            def pe1(te, b_gd=b_gd, b_ld=b_ld, b_a=b_a, b_pe=b_pe):
                te.wait_ge(s_ld.h, b_ld + 16)
                te.matmul(PS[4][:, 0:512], ones_row[0:1, 0:P], bias_row[0:1, 0:512],
                          start=True, stop=True).then_inc(s_pe.h, 1)
                te.wait_ge(s_gd.h, b_gd + 16)
                for t in range(NTT):
                    if t >= 4:
                        te.wait_ge(s_a.h, b_a + 1 + 2 * (t - 4) + 1)
                    mm = None
                    for kc in range(KC):
                        mm = te.matmul(PS[t % 4][:, 0:512], nT[:, kc * T + t * P:kc * T + (t + 1) * P],
                                       wb[0][:, kc * 512:(kc + 1) * 512],
                                       start=(kc == 0), stop=(kc == KC - 1))
                    mm.then_inc(s_mm.h, 1)
            blk.tensor(pe1)

            def vb(v, b_pe=b_pe, b_mm=b_mm, b_a=b_a, b_b=b_b, g=g):
                v.wait_ge(s_pe.h, b_pe + 1)
                v.tensor_copy(bias_bc[:, 0:512], PS[4][:, 0:512]).then_inc(s_a.h, 1)
                for t in range(NTT):
                    v.wait_ge(s_mm.h, b_mm + t + 1)
                    if t >= 1:
                        v.wait_ge(s_b.h, b_b + t)
                    v.drain()
                    v.tensor_tensor(out=scr1[:, 0:512], in0=PS[t % 4][:, 0:512],
                                    in1=bias_bc[:, 0:512], op=ALU.add).then_inc(s_a.h, 1)
                    v.wait_ge(s_b.h, b_b + t + 1)
                    v.drain()
                    v.tensor_tensor(out=scr1[:, 512:1024], in0=scr1[:, 512:1024],
                                    in1=memh_sb[:, t * 512:(t + 1) * 512], op=ALU.mult)
                    v.drain()
                    v.tensor_add(xacc[:, t * D + g * 512:t * D + (g + 1) * 512],
                                 xacc[:, t * D + g * 512:t * D + (g + 1) * 512],
                                 scr1[:, 512:1024]).then_inc(s_a.h, 1)
            blk.vector(vb)

            def scb(sc, b_a=b_a, b_b=b_b):
                for t in range(NTT):
                    sc.wait_ge(s_a.h, b_a + 1 + 2 * t + 1)
                    sc.drain()
                    sc.activation(out=scr1[:, 512:1024], in_=scr1[:, 0:512],
                                  func=ACTF.Sigmoid).then_inc(s_b.h, 1)
            blk.scalar(scb)

    chk("Gdone")
    if DEBUG == "l1x" or STOP == "M3":
        with nc.Block() as blk:
            def _(e):
                b = s_st.reserve(NTT * 16)
                for t in range(NTT):
                    e.wait_ge(s_a.h, s_a.n - 2 * NTT + 2 * t + 2)
                    e.dma_start(out=OUT[t * P:(t + 1) * P, :], in_=xacc[:, t * D:(t + 1) * D]).then_inc(s_st.h, 16)
                e.wait_ge(s_st.h, b + NTT * 16)
            blk.sync(_)
        return nc

    n3b_all = OV.place("n3b_all", [P, NTT * D], BF16, 45056)
    gfm_a = OV.place("gfm_a", [P, 24 * T], BF16, 45056)
    gfm_b = OV.place("gfm_b", [P, 24 * T], BF16, 126976)

    def g_ap(hc):
        if hc < 24:
            return gfm_a[:, hc * T:(hc + 1) * T]
        return gfm_b[:, (hc - 24) * T:(hc - 24 + 1) * T]

    with nc.Block() as blk:
        emit_ln(blk, [xacc[:, t * D:(t + 1) * D] for t in range(NTT)], D, 1e-5)
        b_lna = s_a.n
        b_a = s_a.reserve(NTT)
        b_b = s_b.reserve(NTT)

        def scb(sc):
            sc.wait_ge(s_a.h, b_lna)
            for t in range(NTT):
                if t >= 1:
                    sc.wait_ge(s_a.h, b_a + t)
                sc.activation(out=scr1[:, 0:D], in_=xacc[:, t * D:(t + 1) * D], func=ACTF.Identity,
                              bias=bcol[:, t:t + 1], scale=rcol[:, t:t + 1]).then_inc(s_b.h, 1)
        blk.scalar(scb)

        def vb(v):
            v.tensor_scalar(out=colD[:, 15:16], in0=qsc[:, 3:4], scalar1=1.0,
                            scalar2=qsc[:, 4:5], op0=ALU.mult, op1=ALU.mult)
            for t in range(NTT):
                v.wait_ge(s_b.h, b_b + t + 1)
                v.drain()
                v.tensor_copy(n3b_all[:, t * D:(t + 1) * D], scr1[:, 0:D]).then_inc(s_a.h, 1)
        blk.vector(vb)

    with nc.Block() as blk:
        transpose_batch(blk, [(n3b_all[:, (t * KC + j) * P:(t * KC + j) * P + P],
                               nT[:, j * T + t * P:j * T + t * P + P])
                              for t in range(NTT) for j in range(KC)])

        def park(e):
            b = s_st.reserve(NTT * 16)
            for t in range(NTT):
                e.dma_start(out=pk_x[t * P:(t + 1) * P, :], in_=xacc[:, t * D:(t + 1) * D]).then_inc(s_st.h, 16)
            e.wait_ge(s_st.h, b + NTT * 16)
        blk.sync(park)

    sl_a = [scr1[:, 0:512], scr1[:, 512:1024]]
    with nc.Block() as blk:
        slot_sems = [s_z0, s_z1, s_x0, s_x1]
        b_z = [s.reserve(24 * 16) for s in slot_sems]
        NGH = 96
        b_mm = s_mm.reserve(2 * NGH)
        b_a = s_a.reserve(NGH)
        b_b = s_b.reserve(NGH)

        def wslot2(s):
            return wb[0][:, (s % 4) * D:(s % 4) * D + D]

        def dmab(e):
            for hc in range(48):
                for j, nm in ((0, "w1"), (1, "w3")):
                    sidx = 2 * hc + j
                    if hc >= 2:
                        e.wait_ge(s_mm.h, b_mm + 2 * (2 * (hc - 2) + 1) + j + 1)
                    e.dma_start(out=wslot2(sidx), in_=qap_w13(nm, hc)).then_inc(
                        slot_sems[sidx % 4].h, 16)
        blk.sync(dmab)

        def peb(te):
            for i in range(NGH):
                hc, tg = i // 2, i % 2
                par = i % 2
                for j in range(2):
                    psum = PS[2 * par + j]
                    sidx = 2 * hc + j
                    te.wait_ge(slot_sems[sidx % 4].h, b_z[sidx % 4] + 16 * (sidx // 4 + 1))
                    if i >= 2:
                        if j == 0:
                            te.wait_ge(s_b.h, b_b + (i - 2) + 1)
                        else:
                            te.wait_ge(s_a.h, b_a + (i - 2) + 1)
                    mm = None
                    for kc in range(KC):
                        mm = te.matmul(psum[:, 0:512], wslot2(sidx)[:, kc * P:(kc + 1) * P],
                                       nT[:, kc * T + tg * 512:kc * T + tg * 512 + 512],
                                       start=(kc == 0), stop=(kc == KC - 1))
                    mm.then_inc(s_mm.h, 1)
        blk.tensor(peb)

        def scb(sc):
            for i in range(NGH):
                par = i % 2
                sc.wait_ge(s_mm.h, b_mm + 2 * i + 1)
                if i >= 2:
                    sc.wait_ge(s_a.h, b_a + (i - 2) + 1)
                sc.drain()
                sc.activation(out=sl_a[par], in_=PS[2 * par][:, 0:512], func=ACTF.Silu,
                              scale=qsc[:, 2:3]).then_inc(s_b.h, 1)
        blk.scalar(scb)

        def vb(v):
            for i in range(NGH):
                hc, tg = i // 2, i % 2
                par = i % 2
                v.wait_ge(s_mm.h, b_mm + 2 * i + 2)
                v.wait_ge(s_b.h, b_b + i + 1)
                v.drain()
                v.tensor_tensor(out=g_ap(hc)[:, tg * 512:tg * 512 + 512], in0=sl_a[par],
                                in1=PS[2 * par + 1][:, 0:512], op=ALU.mult).then_inc(s_a.h, 1)
        blk.vector(vb)

    with nc.Block() as blk:
        ostg = [scr1[:, 0:D], scr2[:, 0:D]]
        b_z = [s_z0.reserve(32 * 16), s_z1.reserve(32 * 16)]
        b_mm = s_mm.reserve(8 * 48)
        b_b = s_b.reserve(8 * 4)
        b_st = s_st.reserve(8 * 16)

        def dmab(e):
            for gi in range(8):
                th, nc4 = gi // 4, gi % 4
                for bq in range(8):
                    B = gi * 8 + bq
                    if B >= 2:
                        e.wait_ge(s_mm.h, b_mm + 6 * (B - 1))
                    e.dma_start(out=wb[0][:, (B % 2) * 3072:(B % 2) * 3072 + 3072],
                                in_=bass.AP(qag_out, bq * QSL + q_off["w2"] + nc4 * 512,
                                            [[2048, P], [P * 2048, 6], [1, 512]])
                                ).then_inc([s_z0, s_z1][B % 2].h, 16)
                if gi >= 2:
                    e.wait_ge(s_b.h, b_b + 4 * (gi - 1))
                    e.dma_start(out=bass.AP(pk_op2.tensor, (gi - 2) // 4 * 512 * D + ((gi - 2) % 4) * 512,
                                            [[D, P], [P * D, 4], [1, 512]]),
                                in_=ostg[gi % 2]).then_inc(s_st.h, 16)
                    e.wait_ge(s_st.h, b_st + 16 * (gi - 1))
            for gi in (6, 7):
                e.wait_ge(s_b.h, b_b + 4 * (gi + 1))
                e.dma_start(out=bass.AP(pk_op2.tensor, gi // 4 * 512 * D + (gi % 4) * 512,
                                        [[D, P], [P * D, 4], [1, 512]]),
                            in_=ostg[gi % 2]).then_inc(s_st.h, 16)
                e.wait_ge(s_st.h, b_st + 16 * (gi + 1))
        blk.sync(dmab)

        def peb(te):
            for gi in range(8):
                th, nc4 = gi // 4, gi % 4
                if gi >= 1:
                    te.wait_ge(s_b.h, b_b + 4 * gi)
                for kc in range(48):
                    B = gi * 8 + kc // 6
                    if kc % 6 == 0:
                        te.wait_ge([s_z0, s_z1][B % 2].h, b_z[B % 2] + 16 * (B // 2 + 1))
                    mm = None
                    for t4 in range(4):
                        mm = te.matmul(PS[t4][:, 0:512],
                                       g_ap(kc)[:, (th * 4 + t4) * P:(th * 4 + t4 + 1) * P],
                                       wb[0][:, (B % 2) * 3072 + (kc % 6) * 512:(B % 2) * 3072 + (kc % 6 + 1) * 512],
                                       start=(kc == 0), stop=(kc == 47))
                    mm.then_inc(s_mm.h, 1)
        blk.tensor(peb)

        def scb(sc):
            for gi in range(8):
                sc.wait_ge(s_mm.h, b_mm + 48 * (gi + 1))
                if gi >= 2:
                    sc.wait_ge(s_st.h, b_st + 16 * (gi - 1))
                for t4 in range(4):
                    sc.drain()
                    sc.activation(out=ostg[gi % 2][:, t4 * 512:(t4 + 1) * 512],
                                  in_=PS[t4][:, 0:512], func=ACTF.Identity,
                                  scale=colD[:, 15:16]).then_inc(s_b.h, 1)
        blk.scalar(scb)

    with nc.Block() as blk:
        b_ld = s_ld.reserve(8 * 2 * 16)
        b_a = s_a.reserve(8)
        b_st = s_st.reserve(8 * 16)

        def ld(e):
            for t in range(NTT):
                sl = t % 2
                if t >= 2:
                    e.wait_ge(s_a.h, b_a + t - 1)
                e.dma_start(out=xacc[:, t * D:(t + 1) * D], in_=pk_x[t * P:(t + 1) * P, :]).then_inc(s_ld.h, 16)
                e.dma_start(out=cmb[sl][:, 0:D], in_=pk_op2[t * P:(t + 1) * P, :]).then_inc(s_ld.h, 16)
                e.wait_ge(s_ld.h, b_ld + (t + 1) * 32)
            for t in range(NTT):
                e.wait_ge(s_a.h, b_a + t + 1)
                e.dma_start(out=OUT[t * P:(t + 1) * P, :], in_=xacc[:, t * D:(t + 1) * D]).then_inc(s_st.h, 16)
            e.wait_ge(s_st.h, b_st + 8 * 16)
        blk.sync(ld)

        def vb(v):
            for t in range(NTT):
                sl = t % 2
                v.wait_ge(s_ld.h, b_ld + (t + 1) * 32)
                v.drain()
                v.tensor_add(xacc[:, t * D:(t + 1) * D], xacc[:, t * D:(t + 1) * D],
                             cmb[sl][:, 0:D]).then_inc(s_a.h, 1)
        blk.vector(vb)

    return nc

    return nc



def _host_inputs(i, I):
    b, h2 = i // 2, i % 2
    x = np.ascontiguousarray(I["L0_attn_u"].reshape(4, 2048, 2048)[b])
    X = np.ascontiguousarray(x[h2 * T:(h2 + 1) * T])
    XH = np.zeros((3, D), np.float32) if h2 == 0 else np.ascontiguousarray(x[T - 3:T])
    Wip = np.zeros((D, 8448), np.float32)
    Wip[:, 0:8384] = I["L0_attn_in_proj_w"]
    Wop = np.ascontiguousarray(I["L0_attn_out_proj_w"][h2 * 2048:(h2 + 1) * 2048, :])
    heads = slice(h2 * 32, (h2 + 1) * 32)
    d = dict(
        X=X, XH=XH, Wip=Wip, Wop=Wop,
        CONVW=np.ascontiguousarray(I["L0_attn_conv_weight"].T), CONVB=I["L0_attn_conv_bias"],
        ALOG=I["L0_attn_A_log"], DTB=I["L0_attn_dt_bias"],
        DPAR=np.ascontiguousarray(I["L0_attn_D_param"][heads]),
        NSC=np.ascontiguousarray(I["L0_attn_norm_scale"][h2 * 2048:(h2 + 1) * 2048]),
        WNORM=I["L0_ffn_w_norm"],
        WQ=np.ascontiguousarray(I["L1_attn_Wq"].T), WK=np.ascontiguousarray(I["L1_attn_Wk"].T),
        WV=np.ascontiguousarray(I["L1_attn_Wv"].T), WG=np.ascontiguousarray(I["L1_attn_Wg"].T),
        WMB=np.ascontiguousarray(I["L1_attn_Wmb"].T),
        BQ=I["L1_attn_bq"], BK=I["L1_attn_bk"], BV=I["L1_attn_bv"],
        BG=I["L1_attn_bg"], BMB=I["L1_attn_bmb"],
        MEM=I["L1_attn_memory"], MNORM=I["L1_attn_norm"],
        IDT=np.eye(P, dtype=np.float32).astype(ml_dtypes.bfloat16),
        TRIL=np.triu(np.ones((64, 64), np.float32)),
        IDT32=np.eye(P, dtype=np.float32),
    )
    gvT = np.ascontiguousarray(np.asarray(I["L0_ffn_w_gv"], np.float32).T)
    woT = np.ascontiguousarray(np.asarray(I["L0_ffn_w_out"], np.float32).T)
    w1T = np.zeros((2048, 6144), np.float32); w1T[:, 0:5461] = np.asarray(I["L1_ffn_w1"], np.float32).T
    w3T = np.zeros((2048, 6144), np.float32); w3T[:, 0:5461] = np.asarray(I["L1_ffn_w3"], np.float32).T
    w2T = np.zeros((6144, 2048), np.float32); w2T[0:5461, :] = np.asarray(I["L1_ffn_w2"], np.float32).T
    d["QGV"] = np.ascontiguousarray(gvT[:, i * 2048:(i + 1) * 2048])
    d["QWO"] = np.ascontiguousarray(woT[i * 1024:(i + 1) * 1024])
    d["QW1"] = np.concatenate([w1T[:, (i + 8 * j) * 128:(i + 8 * j) * 128 + 128] for j in range(6)], 1)
    d["QW3"] = np.concatenate([w3T[:, (i + 8 * j) * 128:(i + 8 * j) * 128 + 128] for j in range(6)], 1)
    d["QW2"] = np.ascontiguousarray(w2T[i * 768:(i + 1) * 768])
    return d


def kernel(**inputs):
    nc = build()
    in_maps = [_host_inputs(i, inputs) for i in range(8)]
    res = run_bass_kernel_spmd(nc, in_maps, core_ids=list(range(8)))
    out = np.zeros((4, 2048, 2048), np.float32)
    for i in range(8):
        b, h2 = i // 2, i % 2
        out[b, h2 * T:(h2 + 1) * T] = res.results[i]["OUT"]
    return out.reshape(4, 2048, 2048)
